# version 12
# baseline (speedup 1.0000x reference)
import contextlib
import numpy as np
import ml_dtypes
import concourse.bass as bass
import concourse.mybir as mybir
from concourse.bass_utils import run_bass_kernel_spmd

F32 = mybir.dt.float32
BF16 = mybir.dt.bfloat16
AF = mybir.ActivationFunctionType
OP = mybir.AluOpType
AX = mybir.AxisListType
BF = ml_dtypes.bfloat16

NCORES = 8
B, T, D = 2, 8192, 1024
NT = (B * T) // NCORES
ENGINES = ("sync", "tensor", "vector", "scalar", "gpsimd")


def cdiv(a, b):
    return (a + b - 1) // b


def fm3(d, p=128):
    if len(d.ap) == 3:
        return d
    return d.rearrange("(c p) t -> p c t", p=p)


def mk_tiles(n, tt):
    out = []
    t0 = 0
    while t0 < n:
        rem = n - t0
        if rem <= tt:
            out.append((t0, rem))
            break
        if rem < tt + 16:
            h = rem // 2
            out.append((t0, h))
            out.append((t0 + h, rem - h))
            break
        out.append((t0, tt))
        t0 += tt
    return out


class Sem:
    __slots__ = ("n", "cls", "sec", "inc_secs", "h", "base", "nm")

    def __init__(self, sec, nm=""):
        self.n = 0
        self.cls = None
        self.sec = sec
        self.inc_secs = set()
        self.h = None
        self.base = 0
        self.nm = nm


class Ctx:

    def __init__(self, name="k"):
        self.nc = bass.Bass("TRN2", target_bir_lowering=False)
        self.es = contextlib.ExitStack()
        self.inputs = {}
        self._sems = {}
        self.items = {e: [] for e in ENGINES}
        self._uid = 0
        self.pre = ""
        self.io_override = {}
        self.sec_idx = 0
        self.all_sems = []
        self.scoped_mode = False

    def begin_section(self, pre):
        self.pre = pre
        self.sec_idx += 1

    def uid(self):
        self._uid += 1
        return self._uid

    def sem(self, name):
        if not name.startswith("chain_"):
            name = self.pre + name
        if name not in self._sems:
            s = Sem(self.sec_idx, name)
            self._sems[name] = s
            self.all_sems.append(s)
        return self._sems[name]

    def inp(self, name, shape, dtype, provider=None):
        if name in self.io_override:
            return self.io_override[name]
        name = self.pre + name
        t = self.nc.dram_tensor(name, list(shape), dtype, kind="ExternalInput")
        self.inputs[name] = provider
        return t.ap()

    def scratch(self, name, shape, dtype):
        if name in self.io_override:
            return self.io_override[name]
        return self.nc.dram_tensor(self.pre + name, list(shape), dtype,
                                   kind="Internal").ap()

    def out(self, name, shape, dtype):
        if name in self.io_override:
            return self.io_override[name]
        return self.nc.dram_tensor(self.pre + name, list(shape), dtype,
                                   kind="ExternalOutput").ap()

    def const(self, name, arr):
        return self.nc.inline_tensor(np.ascontiguousarray(arr),
                                     name=self.pre + name).ap()

    def sb(self, name, shape, dtype):
        if self.scoped_mode and getattr(self, "_scopes", None):
            return self.sbs(name, shape, dtype)
        return self.nc.alloc_sbuf_tensor(self.pre + name, list(shape), dtype)

    def sb_push(self):
        if not hasattr(self, "_scopes"):
            self._scopes = []
        self._scopes.append(contextlib.ExitStack())

    def sbs(self, name, shape, dtype):
        return self._scopes[-1].enter_context(
            self.nc.sbuf_tensor(self.pre + name, list(shape), dtype))

    def sb_pop(self):
        self._scopes.pop().close()

    def pss(self, name, shape, dtype=F32):
        return self._scopes[-1].enter_context(
            self.nc.psum_tensor(self.pre + name, list(shape), dtype))

    def ps(self, name, shape, dtype=F32):
        if self.scoped_mode and getattr(self, "_scopes", None):
            return self.pss(name, shape, dtype)
        return self.nc.alloc_psum_tensor(self.pre + name, list(shape), dtype)

    CHAINED = ("vector", "scalar", "gpsimd")

    def ch(self, eng):
        return self.sem("chain_" + eng)

    def rec(self, eng, fn, wait=(), inc=(), nops=None):
        if eng in self.CHAINED:
            if nops is None:
                nops = 0
            assert nops == 0 or not inc, (eng, nops, inc)
            ch = self.ch(eng)
            ch.n += nops
            if nops:
                ch.cls = "CP"
                ch.inc_secs.add(self.sec_idx)
        ws = []
        for w in wait:
            if isinstance(w, tuple):
                ws.append((w[0], w[1]))
            else:
                ws.append((w, w.n))
        incs = []
        for s, k in inc:
            s.n += k
            cls = ("SW" if (eng == "gpsimd" and k == 16) else
                   "HW" if k == 16 else "CP")
            if s.cls is None:
                s.cls = cls
            assert s.cls == cls, f"sem {s.nm} class {s.cls} vs {cls}"
            s.inc_secs.add(self.sec_idx)
            incs.append((s, k))
        exp_chain = self.ch(eng).n if eng in self.CHAINED else None
        self.items[eng].append((ws, fn, incs, exp_chain))

    def mark(self, s, k, eng):
        cls = ("SW" if (eng == "gpsimd" and k >= 16) else
               "HW" if k >= 16 else "CP")
        if s.cls is None:
            s.cls = cls
        assert s.cls == cls, (s.nm, s.cls, cls)
        s.inc_secs.add(self.sec_idx)
        s.n += k

    def dma(self, eng, out, in_, sem, k=16):
        self.rec(eng, lambda e: e.dma_start(out=out, in_=in_), inc=((sem, k),))

    def setup_cfg(self, ncfg):
        saved = self.pre
        self.pre = ""
        cfg_t = self.nc.dram_tensor("cfg", [1, ncfg], mybir.dt.int32,
                                    kind="ExternalInput")
        self.inputs["cfg"] = None
        self.cfg_sb = self.sb("cfg_sb", [1, ncfg], mybir.dt.int32)
        s = self.sem("cfgld")
        self.rec("gpsimd", lambda e: e.dma_start(out=self.cfg_sb[:, :],
                                                 in_=cfg_t.ap()),
                 inc=((s, 16),))
        self.cfg_tok = (s, 16)
        self.pre = saved

    def dyn_dma(self, dst_ap, src_t, cfg_idx, pattern, sem, wait=(), addc=0,
                eng="gpsimd", slow=False):
        if not hasattr(self, "_dyn_engines"):
            self._dyn_engines = set()
        self._dyn_engines.add(eng)
        def fn(e):
            r = self.dynregs[eng]
            e.reg_load(r, self.cfg_sb[0:1, cfg_idx:cfg_idx + 1])
            if addc:
                e.reg_add(r, r, addc)
            return e.dma_start(out=dst_ap, in_=bass.AP(src_t, r, pattern),
                               allow_slow_non_contiguous=slow)
        self.rec(eng, fn, wait=list(wait) + [self.cfg_tok],
                 inc=((sem, 16),))

    def dyn_dma_out(self, dst_t, cfg_idx, pattern, src_ap, sem, wait=(), addc=0,
                    eng="gpsimd"):
        if not hasattr(self, "_dyn_engines"):
            self._dyn_engines = set()
        self._dyn_engines.add(eng)
        def fn(e):
            r = self.dynregs[eng]
            e.reg_load(r, self.cfg_sb[0:1, cfg_idx:cfg_idx + 1])
            if addc:
                e.reg_add(r, r, addc)
            return e.dma_start(out=bass.AP(dst_t, r, pattern), in_=src_ap)
        self.rec(eng, fn, wait=list(wait) + [self.cfg_tok],
                 inc=((sem, 16),))

    def coll(self, kind, src_ap, dst_ap, groups, wait=()):
        s = self.sem(f"cc{self.uid()}")
        self.rec("gpsimd", lambda e: e.collective_compute(
            kind, mybir.AluOpType.bypass, replica_groups=groups,
            ins=[src_ap], outs=[dst_ap]), wait=wait, inc=((s, 1),))
        s.cls = "CC"
        return (s, 1)

    def barrier_val(self, sem):
        return (sem, sem.n)

    CHAIN_SET = frozenset((
        "tensor_tensor", "tensor_scalar", "tensor_copy", "tensor_add",
        "tensor_sub", "tensor_mul", "tensor_max", "scalar_tensor_tensor",
        "tensor_tensor_scan", "tensor_reduce", "reduce_max", "reduce_sum",
        "reciprocal", "select", "iota", "memset", "memzero", "activation",
        "square", "sqrt", "mul", "add", "copy", "sign", "transpose",
        "tensor_single_scalar", "tensor_scalar_add", "tensor_scalar_sub",
        "tensor_scalar_mul", "tensor_scalar_max", "tensor_scalar_min",
        "tensor_relu", "bn_stats", "bn_aggr", "stream_shuffle",
        "partition_broadcast", "affine_select", "copy_predicated",
    ))

    def _assign_sems(self):
        by_sec = {}
        for s in self.all_sems:
            if s.cls is None:
                continue
            dedicated = (s.sec == 0) or (len(s.inc_secs - {s.sec}) > 0)
            if dedicated:
                s.h = self.es.enter_context(self.nc.semaphore("d_" + s.nm))
                s.base = 0
            else:
                by_sec.setdefault(s.sec, []).append(s)
        pools = {"SW": [], "HW": [], "CP": [], "CC": []}
        nsl = [0]
        for sec in sorted(by_sec):
            used = set()
            for s in by_sec[sec]:
                pool = pools[s.cls]
                slot = next((p for p in pool if id(p) not in used), None)
                if slot is None:
                    slot = {"h": self.es.enter_context(
                        self.nc.semaphore(f"g{nsl[0]}_{s.cls}")), "acc": 0}
                    nsl[0] += 1
                    pool.append(slot)
                used.add(id(slot))
                s.h = slot["h"]
                s.base = slot["acc"]
                slot["acc"] += s.n

    def emit(self):
        self._assign_sems()
        chain_set = self.CHAIN_SET

        class ChainEng:

            def __init__(self, e, sem):
                self._e = e
                self._sem = sem
                self._n = 0

            def __getattr__(self, name):
                f = getattr(self._e, name)
                if name in chain_set and callable(f):
                    def g(*a, **k):
                        ins = f(*a, **k)
                        if self._n > 0:
                            ins._wait_ge(self._sem, self._n)
                        ins.then_inc(self._sem, 1)
                        self._n += 1
                        return ins
                    return g
                return f

        with self.nc.Block() as block:
            for eng in ENGINES:
                items = self.items[eng]
                if not items:
                    continue

                def body(e, items=items, eng=eng):
                    ce = None
                    if eng in self.CHAINED:
                        chs = self.ch(eng)
                        if chs.h is None:
                            chs.h = self.es.enter_context(
                                self.nc.semaphore("d_" + chs.nm))
                        ce = ChainEng(e, chs.h)
                    if eng in getattr(self, "_dyn_engines", ()):
                        if not hasattr(self, "dynregs"):
                            self.dynregs = {}
                        self.dynregs[eng] = self.es.enter_context(
                            e.register("dynr_" + eng))
                    for ws, fn, incs, exp_chain in items:
                        for s, v in ws:
                            if v > 0 and s.h is not None:
                                e.wait_ge(s.h, s.base + v)
                        ins = fn(ce if ce is not None else e)
                        for s, k in incs:
                            ins.then_inc(s.h, k)
                        if exp_chain is not None:
                            assert ce._n == exp_chain, \
                                f"{eng}: nops mismatch got {ce._n} want {exp_chain}"

                getattr(block, eng)(body)

    def finish_sem(self):
        return self.sem("st_done")

    def run(self, extra_by_core=None, n_cores=NCORES):
        in_maps = []
        for c in range(n_cores):
            m = {k: np.ascontiguousarray(v(c)) for k, v in self.inputs.items()
                 if v is not None}
            if extra_by_core:
                m.update(extra_by_core[c])
            in_maps.append(m)
        res = run_bass_kernel_spmd(self.nc, in_maps, core_ids=list(range(n_cores)))
        return res.results



def load_weight_bf(C, name, dram_ap, din, dout, ld):
    nk = din // 128
    w = C.sb(name, [128, nk, dout], BF16)
    src = dram_ap.rearrange("(c p) m -> p c m", p=128)
    for c in range(nk):
        C.dma("gpsimd", w[:, c, :], src[:, c, :], ld)
    return w


def make_consts(C):
    if getattr(C, "consts", None) is not None:
        return C.consts
    saved_pre = C.pre
    C.pre = ""
    cvals = np.zeros((128, 16), np.float32)
    cvals[:, 0] = 1.0
    cvals[:, 1] = 1.0 / 1024.0
    cvals[:, 2] = 1e-5
    cvals[:, 3] = 0.125
    cvals[:, 4] = 0.5
    cvals[:, 5] = 1.0 / 2048.0
    cvals[:, 6] = 8.0
    cvals[:, 7] = -1.0
    cvals[:, 8] = np.sqrt(2.0 * np.pi)
    cvals[:, 9] = -np.sqrt(0.5) * np.sqrt(2.0 * np.pi)
    cvals[:, 10] = 1.0 / 64.0
    cvals[:, 11] = 1e-12
    cst_d = C.const("cst_d", cvals)
    ones_c_d = C.const("ones_c_d", np.ones((128, 1), np.float32))
    ones_r_d = C.const("ones_r_d", np.ones((1, 128), np.float32))
    cst = C.sb("cst", [128, 16], F32)
    ones_c = C.sb("ones_c", [128, 1], F32)
    ones_r = C.sb("ones_r", [1, 128], F32)
    ld = C.sem("cld")
    C.dma("sync", cst[:, :], cst_d, ld)
    C.dma("sync", ones_c[:, :], ones_c_d, ld)
    C.dma("sync", ones_r[:, :], ones_r_d, ld)
    C.consts = dict(cst=cst, ones_c=ones_c, ones_r=ones_r, ld=ld)
    C.pre = saved_pre
    return C.consts





def emit_launch1(C, nt_out, halo=3, fused=False):
    ntok = nt_out + halo
    DIN, CONVD = 2048, 2176
    MOUT = 4256
    x_d = C.inp("x", (D, ntok), F32)
    w_d = C.inp("w_inproj", (D, MOUT), F32)
    dtb_d = C.inp("dt_bias_r", (32, 1), F32)
    convw_d = C.inp("conv_w_r", (17, 128, 4), F32)
    convb_d = C.inp("conv_b_r", (128, 17), F32)

    xn_o = C.out("xn_f", (D, ntok), F32)
    xnbf_o = C.out("xn_bf", (D, ntok), BF16)
    zs_o = C.out("z_silu", (DIN, ntok), F32)
    dt_o = C.out("dt0", (32, ntok), F32)
    xbc_o = C.out("xbc", (CONVD, nt_out), F32)
    xbcbf_o = C.out("xbc_bf", (CONVD, nt_out), BF16)
    xbcpre_d = C.scratch("xbc_pre", (CONVD, ntok), F32)

    cn = make_consts(C)
    ld = C.sem("ld")
    ldw = C.sem("ldw")
    w_sb = load_weight_bf(C, "w_sb", w_d, D, MOUT, ldw)
    dtb = C.sb("dtb", [32, 1], F32)
    convw = C.sb("convw", [128, 17, 4], F32)
    convb = C.sb("convb", [128, 17], F32)
    C.dma("sync", dtb[:, :], dtb_d, ld)
    C.dma("sync", convw[:, :, :], convw_d.rearrange("c p k -> p c k"), ld)
    C.dma("sync", convb[:, :], convb_d, ld)

    VC, SC, GC = C.ch("vector"), C.ch("scalar"), C.ch("gpsimd")
    tt = 512
    ntt = cdiv(ntok, tt)
    xsrc = fm3(x_d)
    xnf_dst = fm3(xn_o)
    xnbf_dst = fm3(xnbf_o)
    zs_dst = fm3(zs_o)
    xbcpre_dst = fm3(xbcpre_d)

    C.sb_push()
    xbuf = [C.sbs(f"xb{i}", [128, 8, tt], F32) for i in range(2)]
    xsq = C.sbs("xsq", [128, 8, tt], F32)
    xnf2 = [C.sbs(f"xnf{i}", [128, 8, tt], F32) for i in range(2)]
    xnb = [C.sbs(f"xnb{i}", [128, 8, tt], BF16) for i in range(2)]
    tmp = C.sbs("tmp", [128, tt], F32)
    rows = C.sbs("rows", [1, 4 * tt], F32)
    ebuf = [C.sbs(f"eb{i}", [128, tt], F32) for i in range(3)]
    ebbf = C.sbs("ebbf", [32, tt], F32)

    ps_s = C.ps("ps_s", [1, tt])
    ps_q = C.ps("ps_q", [1, tt])
    ps_mb = C.ps("ps_mb", [128, tt])
    ps_rb = C.ps("ps_rb", [128, tt])
    NB = 3
    psm = [C.ps(f"psm{i}", [128, tt]) for i in range(NB)]

    wxs = [C.sem("wx0"), C.sem("wx1")]
    sxns = [C.sem("sxn0"), C.sem("sxn1")]
    sxgs = [C.sem(f"sxg{i}") for i in range(3)]
    tstat = C.sem("tstat")
    tbc = C.sem("tbc")
    te = C.sem("te")

    mtiles = [(i * 128, 128, "z") for i in range(16)]
    mtiles += [(2048 + i * 128, 128, "xbc") for i in range(17)]
    mtiles += [(4224, 32, "dt")]
    NJ = len(mtiles)

    tk = {}
    for ti, (t0, tn) in enumerate(mk_tiles(ntok, tt)):
        par = ti % 2
        wx = wxs[par]
        xb = xbuf[par]
        xnbf = xnb[par]
        lw = [ld, ldw, cn["ld"]] if ti == 0 else []
        if ti >= 2:
            lw += [tk[ti - 2]["xn_done"], tk[ti - 2]["sq_done"], (tstat, ti - 1)]
        C.rec("sync", lambda e, xb=xb, t0=t0, tn=tn:
              e.dma_start(out=xb[:, :, 0:tn], in_=xsrc[:, :, t0:t0 + tn]),
              wait=lw, inc=((wx, 16),))
        C.rec("scalar", lambda e, xb=xb, tn=tn:
              e.activation(xsq[:, :, 0:tn], xb[:, :, 0:tn], AF.Square),
              wait=[(wx, wx.n)], nops=1)
        tk[ti] = dict(sq_done=(SC, SC.n))

        def mm_stats(t, xb=xb, tn=tn):
            for c in range(8):
                t.matmul(ps_s[0:1, 0:tn], cn["ones_c"][:, :], xb[:, c, 0:tn],
                         start=(c == 0), stop=(c == 7))
            ins = None
            for c in range(8):
                ins = t.matmul(ps_q[0:1, 0:tn], cn["ones_c"][:, :], xsq[:, c, 0:tn],
                               start=(c == 0), stop=(c == 7))
            return ins
        C.rec("tensor", mm_stats, wait=[(SC, SC.n), (wx, wx.n)],
              inc=((tstat, 1),))

        def row_ops(e, tn=tn):
            mean = rows[:, 0:tn]
            var = rows[:, tt:tt + tn]
            e.tensor_scalar(mean, ps_s[0:1, 0:tn], cn["cst"][0:1, 1:2], None, OP.mult)
            e.tensor_tensor(var, mean, mean, OP.mult)
            e.scalar_tensor_tensor(var, ps_q[0:1, 0:tn], cn["cst"][0:1, 1:2], var,
                                   OP.mult, OP.subtract)
            return e.tensor_scalar(var, var, cn["cst"][0:1, 2:3], None, OP.add)
        C.rec("vector", row_ops, wait=[(tstat, tstat.n)], nops=4)
        C.rec("scalar", lambda e, tn=tn:
              e.activation(rows[:, 3 * tt:3 * tt + tn], rows[:, tt:tt + tn], AF.Sqrt),
              wait=[(VC, VC.n)], nops=1)
        C.rec("vector", lambda e, tn=tn:
              e.reciprocal(rows[:, 2 * tt:2 * tt + tn], rows[:, 3 * tt:3 * tt + tn]),
              wait=[(SC, SC.n)], nops=1)

        def mm_bc(t, tn=tn):
            t.matmul(ps_mb[:, 0:tn], cn["ones_r"][:, :], rows[0:1, 0:tn],
                     start=True, stop=True)
            return t.matmul(ps_rb[:, 0:tn], cn["ones_r"][:, :],
                            rows[0:1, 2 * tt:2 * tt + tn], start=True, stop=True)
        C.rec("tensor", mm_bc, wait=[(VC, VC.n)], inc=((tbc, 1),))

        xnf = xnf2[par]
        def xn_ops(e, xb=xb, xnbf=xnbf, xnf=xnf, tn=tn):
            ins = None
            for c in range(8):
                e.tensor_tensor(tmp[:, 0:tn], xb[:, c, 0:tn], ps_mb[:, 0:tn],
                                OP.subtract)
                e.tensor_tensor(xnf[:, c, 0:tn], tmp[:, 0:tn], ps_rb[:, 0:tn],
                                OP.mult)
                ins = e.tensor_copy(xnbf[:, c, 0:tn], xnf[:, c, 0:tn])
            return ins
        C.rec("vector", xn_ops,
              wait=[(tbc, tbc.n), (te, NJ * max(0, ti - 1)),
                    (sxns[par], 32 * (ti // 2))], nops=24)
        tk[ti]["xn_done"] = (VC, VC.n)

        C.rec("sync", lambda e, xnf=xnf, t0=t0, tn=tn:
              e.dma_start(out=xnf_dst[:, :, t0:t0 + tn], in_=xnf[:, :, 0:tn]),
              wait=[(VC, VC.n)], inc=((sxns[par], 16),))
        C.rec("sync", lambda e, xnbf=xnbf, t0=t0, tn=tn:
              e.dma_start(out=xnbf_dst[:, :, t0:t0 + tn], in_=xnbf[:, :, 0:tn]),
              wait=[(VC, VC.n)], inc=((sxns[par], 16),))

        for mi, (moff, msz, kind) in enumerate(mtiles):
            j = ti * NJ + mi
            bank = j % NB

            def mm_proj(t, moff=moff, msz=msz, bank=bank, tn=tn, xnbf=xnbf):
                ins = None
                for k in range(8):
                    ins = t.matmul(psm[bank][0:msz, 0:tn], w_sb[:, k, moff:moff + msz],
                                   xnbf[:, k, 0:tn], start=(k == 0), stop=(k == 7))
                return ins
            C.rec("tensor", mm_proj,
                  wait=[tk[ti]["xn_done"], (sxgs[bank], 16 * (j // NB))],
                  inc=((te, 1),))

            ebw = [(te, te.n), (sxgs[bank], 16 * (j // NB))]
            if kind == "z":
                C.rec("scalar", lambda e, bank=bank, msz=msz, tn=tn:
                      e.activation(ebuf[bank][0:msz, 0:tn], psm[bank][0:msz, 0:tn],
                                   AF.Silu),
                      wait=ebw, nops=1)
                C.rec("gpsimd", lambda e, bank=bank, moff=moff, t0=t0, tn=tn:
                      e.dma_start(out=zs_dst[:, moff // 128, t0:t0 + tn],
                                  in_=ebuf[bank][:, 0:tn]),
                      wait=[(SC, SC.n)], inc=((sxgs[bank], 16),))
            elif kind == "xbc":
                C.rec("scalar", lambda e, bank=bank, msz=msz, tn=tn:
                      e.activation(ebuf[bank][0:msz, 0:tn], psm[bank][0:msz, 0:tn],
                                   AF.Copy),
                      wait=ebw, nops=1)
                C.rec("gpsimd", lambda e, bank=bank, moff=moff, t0=t0, tn=tn:
                      e.dma_start(out=xbcpre_dst[:, (moff - 2048) // 128, t0:t0 + tn],
                                  in_=ebuf[bank][:, 0:tn]),
                      wait=[(SC, SC.n)], inc=((sxgs[bank], 16),))
            else:
                def epi_dt(e, bank=bank, tn=tn):
                    e.activation(ebbf[0:32, 0:tn], psm[bank][0:32, 0:tn],
                                 AF.Exp, bias=dtb[0:32, 0:1])
                    return e.activation(ebuf[bank][0:32, 0:tn], ebbf[0:32, 0:tn],
                                        AF.Ln, bias=cn["ones_c"][0:32, 0:1])
                C.rec("scalar", epi_dt, wait=ebw, nops=2)
                C.rec("gpsimd", lambda e, bank=bank, t0=t0, tn=tn:
                      e.dma_start(out=dt_o[0:32, t0:t0 + tn],
                                  in_=ebuf[bank][0:32, 0:tn]),
                      wait=[(SC, SC.n)], inc=((sxgs[bank], 16),))

    store_barrier = [(s, s.n) for s in sxgs + sxns]
    C.sb_pop()
    cbuf = [C.sb(f"cb{i}", [128, 17, tt + 3], F32) for i in range(2)]
    cacc = [C.sb(f"cacc{i}", [128, tt], F32) for i in range(4)]
    cout = [C.sb(f"cout{i}", [128, tt], F32) for i in range(2)]
    coutb = [C.sb(f"coutb{i}", [128, tt], BF16) for i in range(2)]
    xbc_dst = fm3(xbc_o)
    xbcbf_dst = fm3(xbcbf_o)
    wcvs = [C.sem("wcv0"), C.sem("wcv1")]
    dacks = [C.sem("dack0"), C.sem("dack1")]

    ntc = cdiv(nt_out, tt)
    cc = 0
    ctk = {}
    for ti in range(ntc):
        t0 = ti * tt
        tn = min(tt, nt_out - t0)
        cpar = ti % 2
        wcv = wcvs[cpar]
        cb = cbuf[cpar]
        lw = list(store_barrier)
        if ti >= 2:
            lw.append(ctk[ti - 2])
        C.rec("sync", lambda e, cb=cb, t0=t0, tn=tn:
              e.dma_start(out=cb[:, :, 0:tn + 3], in_=xbcpre_dst[:, :, t0:t0 + tn + 3]),
              wait=lw, inc=((wcv, 16),))
        for c in range(17):
            p = cc % 2
            a0, a1 = cacc[2 * p], cacc[2 * p + 1]
            co, cob = cout[p], coutb[p]

            def conv_ops(e, cb=cb, c=c, tn=tn, a0=a0, a1=a1):
                e.tensor_scalar(a0[:, 0:tn], cb[:, c, 0:tn], convw[:, c, 0:1],
                                None, OP.mult)
                e.scalar_tensor_tensor(a1[:, 0:tn], cb[:, c, 1:1 + tn],
                                       convw[:, c, 1:2], a0[:, 0:tn], OP.mult, OP.add)
                e.scalar_tensor_tensor(a0[:, 0:tn], cb[:, c, 2:2 + tn],
                                       convw[:, c, 2:3], a1[:, 0:tn], OP.mult, OP.add)
                return e.scalar_tensor_tensor(a1[:, 0:tn], cb[:, c, 3:3 + tn],
                                              convw[:, c, 3:4], a0[:, 0:tn],
                                              OP.mult, OP.add)
            cw = [(wcv, wcv.n), ld]
            if cc >= 2:
                cw.append(ctk[("s", cc - 2)])
            C.rec("vector", conv_ops, wait=cw, nops=4)

            def conv_acts(e, c=c, tn=tn, a1=a1, co=co, cob=cob):
                e.activation(co[:, 0:tn], a1[:, 0:tn], AF.Silu,
                             bias=convb[:, c:c + 1])
                return e.activation(cob[:, 0:tn], co[:, 0:tn], AF.Copy)
            sw = [(VC, VC.n)]
            if cc >= 2:
                sw.append((dacks[p], 32 * (cc // 2)))
            C.rec("scalar", conv_acts, wait=sw, nops=2)
            ctk[("s", cc)] = (SC, SC.n)
            C.rec("gpsimd", lambda e, c=c, t0=t0, tn=tn, co=co:
                  e.dma_start(out=xbc_dst[:, c, t0:t0 + tn], in_=co[:, 0:tn]),
                  wait=[(SC, SC.n)], inc=((dacks[p], 16),))
            C.rec("gpsimd", lambda e, c=c, t0=t0, tn=tn, cob=cob:
                  e.dma_start(out=xbcbf_dst[:, c, t0:t0 + tn], in_=cob[:, 0:tn]),
                  wait=[(SC, SC.n)], inc=((dacks[p], 16),))
            cc += 1
        ctk[ti] = (VC, VC.n)

    bar1 = [(s, s.n) for s in sxns + sxgs + dacks]
    if not fused:
        C.rec("gpsimd", lambda e: e.nop(), wait=bar1)
        return None
    xbct_o = C.scratch("xbc_tok", (nt_out, 2176), BF16)
    sxt = xbct_o.rearrange("(tt p) j -> p tt j", p=128)
    VC = C.ch("vector")
    C.sb_push()
    tb = [C.sbs(f"tt_a{i}", [128, 128], BF16) for i in range(2)]
    te_ = [C.sbs(f"tt_e{i}", [128, 128], BF16) for i in range(2)]
    ibf1 = C.sbs("tt_ibf", [128, 128], BF16)
    pst = [C.pss(f"tt_ps{i}", [128, 128], BF16) for i in range(2)]
    ib_d = C.const("tt_ibf_c", np.eye(128).astype(ml_dtypes.bfloat16))
    ldt = C.sem("tt_ld")
    C.dma("sync", ibf1[:, :], ib_d, ldt)
    wv = [C.sem(f"tt_w{i}") for i in range(2)]
    sv = [C.sem(f"tt_s{i}") for i in range(2)]
    ttm = C.sem("tt_m")
    it = 0
    toks = {}
    for ch in range(17):
        for tt_i in range(nt_out // 128):
            sl = it % 2
            lw = bar1 + [(ldt, ldt.n)] if it == 0 else []
            if it >= 2:
                lw.append(toks[it - 2])
            C.rec("sync", lambda e, sl=sl, ch=ch, tt_i=tt_i:
                  e.dma_start(out=tb[sl][:, :],
                              in_=xbcbf_dst[:, ch, tt_i * 128:(tt_i + 1) * 128]),
                  wait=lw + [(sv[sl], 16 * (it // 2))], inc=((wv[sl], 16),))
            C.rec("tensor", lambda t, sl=sl:
                  t.transpose(pst[sl][:, :], tb[sl][:, :], ibf1[:, :]),
                  wait=[(wv[sl], wv[sl].n), (VC, VC.n)], inc=((ttm, 1),))
            C.rec("vector", lambda e, sl=sl:
                  e.tensor_copy(te_[sl][:, :], pst[sl][:, :]),
                  wait=[(ttm, ttm.n)], nops=1)
            toks[it] = (VC, VC.n)
            C.rec("sync", lambda e, sl=sl, ch=ch, tt_i=tt_i:
                  e.dma_start(out=sxt[:, tt_i, ch * 128:(ch + 1) * 128],
                              in_=te_[sl][:, :]),
                  wait=[(VC, VC.n)], inc=((sv[sl], 16),))
            it += 1
    C.sb_pop()
    return dict(xbc_tok=xbct_o, bar=[(s, s.n) for s in sv] + bar1)



def stage_rms(C, uid, src_d, ntok, dim, dst_bf_d=None, dst_f_d=None,
              scale_sb=None, bias_sb=None, eps_col=2, wait=(), with_mean=False,
              tt=512):
    cn = C.consts
    VC, SC = C.ch("vector"), C.ch("scalar")
    nk = dim // 128
    ntt = cdiv(ntok, tt)
    ssrc = fm3(src_d)
    sdst_bf = fm3(dst_bf_d) if dst_bf_d is not None else None
    sdst_f = fm3(dst_f_d) if dst_f_d is not None else None
    xb = [C.sbs(f"r{uid}_x{i}", [128, nk, tt], F32) for i in range(2)]
    ob = [C.sbs(f"r{uid}_o{i}", [128, nk, tt], BF16) for i in range(2)]
    of = [C.sbs(f"r{uid}_f{i}", [128, nk, tt], F32) for i in range(2)] if dst_f_d is not None else None
    xsq = C.sbs(f"r{uid}_sq", [128, nk, tt], F32)
    rows = C.sbs(f"r{uid}_rows", [1, 3 * tt], F32)
    ps_s = C.pss(f"r{uid}_pss", [1, tt]) if with_mean else None
    ps_mb = C.pss(f"r{uid}_psmb", [128, tt]) if with_mean else None
    ps_q = C.pss(f"r{uid}_psq", [1, tt])
    ps_rb = C.pss(f"r{uid}_psrb", [128, tt])
    wx = [C.sem(f"r{uid}_wx{i}") for i in range(2)]
    st = [C.sem(f"r{uid}_st{i}") for i in range(2)]
    tstat = C.sem(f"r{uid}_ts")
    tbc = C.sem(f"r{uid}_tb")
    tk = {}
    nst_per_tile = (1 if dst_bf_d is not None else 0) + (1 if dst_f_d is not None else 0)
    for ti, (t0, tn) in enumerate(mk_tiles(ntok, tt)):
        par = ti % 2
        lw = list(wait) if ti == 0 else []
        lw.append((C.consts["ld"], C.consts["ld"].n))
        if ti >= 2:
            lw += [tk[ti - 2]["done"], tk[ti - 2]["sq"], (tstat, ti - 1)]
        C.rec("sync", lambda e, xb=xb[par], t0=t0, tn=tn:
              e.dma_start(out=xb[:, :, 0:tn], in_=ssrc[:, :, t0:t0 + tn]),
              wait=lw, inc=((wx[par], 16),))
        C.rec("scalar", lambda e, xb=xb[par], tn=tn:
              e.activation(xsq[:, :, 0:tn], xb[:, :, 0:tn], AF.Square),
              wait=[(wx[par], wx[par].n)], nops=1)
        tk[ti] = dict(sq=(SC, SC.n))

        def mm_stats(t, tn=tn, xb=xb[par]):
            ins = None
            if with_mean:
                for c in range(nk):
                    t.matmul(ps_s[0:1, 0:tn], cn["ones_c"][:, :], xb[:, c, 0:tn],
                             start=(c == 0), stop=(c == nk - 1))
            for c in range(nk):
                ins = t.matmul(ps_q[0:1, 0:tn], cn["ones_c"][:, :], xsq[:, c, 0:tn],
                               start=(c == 0), stop=(c == nk - 1))
            return ins
        C.rec("tensor", mm_stats, wait=[(SC, SC.n), (wx[par], wx[par].n)],
              inc=((tstat, 1),))

        inv_dim_col = 1 if dim == 1024 else 5
        if with_mean:
            def row_ops(e, tn=tn, icol=inv_dim_col):
                mrow = rows[:, 2 * tt:2 * tt + tn]
                e.tensor_scalar(mrow, ps_s[0:1, 0:tn],
                                cn["cst"][0:1, icol:icol + 1], None, OP.mult)
                e.tensor_tensor(rows[:, 0:tn], mrow, mrow, OP.mult)
                e.scalar_tensor_tensor(rows[:, 0:tn], ps_q[0:1, 0:tn],
                                       cn["cst"][0:1, icol:icol + 1],
                                       rows[:, 0:tn], OP.mult, OP.subtract)
                e.tensor_scalar(rows[:, 0:tn], rows[:, 0:tn],
                                cn["cst"][0:1, eps_col:eps_col + 1], None, OP.add)
                return e.tensor_copy(rows[:, tt:tt + tn], rows[:, 0:tn])
            C.rec("vector", row_ops, wait=[(tstat, tstat.n)], nops=5)
        else:
            def row_ops(e, tn=tn, icol=inv_dim_col):
                e.tensor_scalar(rows[:, 0:tn], ps_q[0:1, 0:tn],
                                cn["cst"][0:1, icol:icol + 1],
                                cn["cst"][0:1, eps_col:eps_col + 1], OP.mult, OP.add)
                return e.tensor_copy(rows[:, tt:tt + tn], rows[:, 0:tn])
            C.rec("vector", row_ops, wait=[(tstat, tstat.n)], nops=2)
        C.rec("scalar", lambda e, tn=tn:
              e.activation(rows[:, 0:tn], rows[:, tt:tt + tn], AF.Sqrt),
              wait=[(VC, VC.n)], nops=1)
        C.rec("vector", lambda e, tn=tn:
              e.reciprocal(rows[:, tt:tt + tn], rows[:, 0:tn]),
              wait=[(SC, SC.n)], nops=1)
        def mm_bc(t, tn=tn):
            if with_mean:
                t.matmul(ps_mb[:, 0:tn], cn["ones_r"][:, :],
                         rows[0:1, 2 * tt:2 * tt + tn], start=True, stop=True)
            return t.matmul(ps_rb[:, 0:tn], cn["ones_r"][:, :],
                            rows[0:1, tt:tt + tn], start=True, stop=True)
        C.rec("tensor", mm_bc, wait=[(VC, VC.n)], inc=((tbc, 1),))

        def app(e, xb=xb[par], ob=ob[par], ofb=(of[par] if of else None), tn=tn):
            ins = None
            for c in range(nk):
                if with_mean:
                    e.tensor_tensor(xsq[:, c, 0:tn], xb[:, c, 0:tn],
                                    ps_mb[:, 0:tn], OP.subtract)
                    xb_eff = xsq
                else:
                    xb_eff = xb
                if scale_sb is not None:
                    e.tensor_tensor(xsq[:, c, 0:tn], xb_eff[:, c, 0:tn],
                                    ps_rb[:, 0:tn], OP.mult)
                    src = xsq[:, c, 0:tn]
                    bias1 = bias_sb[:, c:c + 1] if bias_sb is not None else None
                    op1 = OP.add if bias_sb is not None else OP.bypass
                    if ofb is not None:
                        e.tensor_scalar(ofb[:, c, 0:tn], src, scale_sb[:, c:c + 1],
                                        bias1, OP.mult, op1)
                        src = ofb[:, c, 0:tn]
                        ins = e.tensor_copy(ob[:, c, 0:tn], src)
                    else:
                        ins = e.tensor_scalar(ob[:, c, 0:tn], src,
                                              scale_sb[:, c:c + 1], bias1,
                                              OP.mult, op1)
                else:
                    if ofb is not None:
                        e.tensor_tensor(ofb[:, c, 0:tn], xb_eff[:, c, 0:tn],
                                        ps_rb[:, 0:tn], OP.mult)
                        ins = e.tensor_copy(ob[:, c, 0:tn], ofb[:, c, 0:tn])
                    else:
                        ins = e.tensor_tensor(ob[:, c, 0:tn], xb_eff[:, c, 0:tn],
                                              ps_rb[:, 0:tn], OP.mult)
            return ins
        per_c = (1 + (1 if scale_sb is not None else 0) + (1 if of else 0)
                 + (1 if with_mean else 0))
        C.rec("vector", app,
              wait=[(tbc, tbc.n), (st[par], 16 * nst_per_tile * (ti // 2))],
              nops=nk * per_c)
        tk[ti]["done"] = (VC, VC.n)
        if sdst_bf is not None:
            C.rec("sync", lambda e, ob=ob[par], t0=t0, tn=tn:
                  e.dma_start(out=sdst_bf[:, :, t0:t0 + tn], in_=ob[:, :, 0:tn]),
                  wait=[(VC, VC.n)], inc=((st[par], 16),))
        if sdst_f is not None:
            C.rec("sync", lambda e, ofb=of[par], t0=t0, tn=tn:
                  e.dma_start(out=sdst_f[:, :, t0:t0 + tn], in_=ofb[:, :, 0:tn]),
                  wait=[(VC, VC.n)], inc=((st[par], 16),))
    return [(s, s.n) for s in st + wx]


def stage_proj(C, uid, src_d, w_d, d_in, d_out, ntok, epi, *,
               tg=None, wslab=None, aux_d=None, aux_dim=None, wait=(),
               mtiles=None, tt=512, st_per_job=1, aux2_d=None, aux2_dim=None,
               aux2_off=0):
    cn = C.consts
    kp = 128 if d_in % 128 == 0 else d_in
    nk = d_in // kp
    tg = tg or ntok
    wslab = wslab or d_out
    ngr = cdiv(ntok, tg)
    nsl = cdiv(d_out, wslab)
    ssrc = fm3(src_d, kp)
    wsrc = w_d.rearrange("(c p) m -> p c m", p=kp)
    saux = fm3(aux_d) if aux_d is not None else None
    saux2 = fm3(aux2_d) if aux2_d is not None else None
    xg = C.sbs(f"p{uid}_x", [kp, nk, tg], BF16)
    nka = (aux_dim // 128) if aux_dim else 0
    axg = C.sbs(f"p{uid}_a", [128, nka, tg], F32) if aux_d is not None else None
    nka2 = (aux2_dim // 128) if aux2_dim else 0
    axg2 = (C.sbs(f"p{uid}_a2", [128, nka2, tg + aux2_off], F32)
            if aux2_d is not None else None)
    wbuf = [C.sbs(f"p{uid}_w{i}", [kp, nk, wslab], BF16) for i in range(2 if nsl > 1 or ngr > 1 else 1)]
    NB = 3
    psm = [C.pss(f"p{uid}_ps{i}", [128, tt]) for i in range(NB)]
    wx = C.sem(f"p{uid}_wx")
    wa = C.sem(f"p{uid}_wa")
    wws = [C.sem(f"p{uid}_ww{i}") for i in range(2)]
    te = C.sem(f"p{uid}_te")
    sxgs = [C.sem(f"p{uid}_sx{i}") for i in range(NB)]
    if mtiles is None:
        mtiles = [(m0, min(128, d_out - m0)) for m0 in range(0, d_out, 128)]
    j = 0
    wldw_tok = {}
    for g in range(ngr):
        g0 = g * tg
        gn = min(tg, ntok - g0)
        C.rec("sync", lambda e, g0=g0, gn=gn:
              e.dma_start(out=xg[:, :, 0:gn], in_=ssrc[:, :, g0:g0 + gn]),
              wait=(list(wait) + [cn["ld"]]) if g == 0 else [(te, te.n)],
              inc=((wx, 16),))
        if aux_d is not None:
            C.rec("sync", lambda e, g0=g0, gn=gn:
                  e.dma_start(out=axg[:, :, 0:gn], in_=saux[:, :, g0:g0 + gn]),
                  wait=[] if g == 0 else [(s, s.n) for s in sxgs],
                  inc=((wa, 16),))
        if aux2_d is not None:
            C.rec("sync", lambda e, g0=g0, gn=gn:
                  e.dma_start(out=axg2[:, :, 0:gn + aux2_off],
                              in_=saux2[:, :, g0:g0 + gn + aux2_off]),
                  wait=[] if g == 0 else [(s, s.n) for s in sxgs],
                  inc=((wa, 16),))
        for sl in range(nsl):
            s0 = sl * wslab
            sn = min(wslab, d_out - s0)
            wslot = (g * nsl + sl) % len(wbuf)
            if g == 0 or len(wbuf) > 1:
                gi = g * nsl + sl
                lw2 = list(wait) if gi < 2 else []
                if gi >= 2 and len(wbuf) == 2:
                    lw2 = [wldw_tok[gi - 2]]
                wsem = wws[wslot]
                def wload(e, wslot=wslot, s0=s0, sn=sn, wsem=wsem):
                    ins = None
                    for c in range(nk):
                        ins = e.dma_start(out=wbuf[wslot][:, c, 0:sn],
                                          in_=wsrc[:, c, s0:s0 + sn])
                        ins.then_inc(wsem.h, 16)
                    return ins
                C.rec("gpsimd", wload, wait=lw2)
                C.mark(wsem, 16 * nk, "gpsimd")
            for (moff, msz) in [mt for mt in mtiles if s0 <= mt[0] < s0 + sn]:
                for (t0, tn) in mk_tiles(gn, tt):
                    bank = j % NB

                    def mm(t, moff=moff, msz=msz, bank=bank, t0=t0, tn=tn,
                           wslot=wslot, s0=s0):
                        ins = None
                        for k in range(nk):
                            ins = t.matmul(psm[bank][0:msz, 0:tn],
                                           wbuf[wslot][:, k, moff - s0:moff - s0 + msz],
                                           xg[:, k, t0:t0 + tn],
                                           start=(k == 0), stop=(k == nk - 1))
                        return ins
                    mmw = [(wx, wx.n), (wws[wslot], wws[wslot].n),
                           (sxgs[bank], 16 * st_per_job * (j // NB))]
                    if aux_d is not None or aux2_d is not None:
                        mmw.append((wa, wa.n))
                    C.rec("tensor", mm, wait=mmw, inc=((te, 1),))
                    info = dict(ps=psm[bank], bank=bank, moff=moff, msz=msz,
                                t0=t0, tn=tn, g0=g0, aux=axg, aux2=axg2,
                                a2off=aux2_off, sxg=sxgs[bank],
                                tew=(te, te.n),
                                ebw=(sxgs[bank], 16 * st_per_job * (j // NB)))
                    epi(C, info)
                    j += 1
            wldw_tok[g * nsl + sl] = (te, te.n)
    return [(s, s.n) for s in sxgs] + [(wx, wx.n)]


def mk_epi(C, uid, spec):
    NB = 3
    dst_f = spec.get("dst_f")
    dst_bf = spec.get("dst_bf")

    def _fm(d):
        if len(d.ap) == 3:
            return d
        p = 128 if d.shape[0] % 128 == 0 else d.shape[0]
        return d.rearrange("(c p) t -> p c t", p=p)
    df = _fm(dst_f) if dst_f is not None else None
    db = _fm(dst_bf) if dst_bf is not None else None
    eb = [C.sbs(f"e{uid}_f{i}", [128, 512], F32) for i in range(NB)]
    ebb = [C.sbs(f"e{uid}_b{i}", [128, 512], BF16) for i in range(NB)] if dst_bf is not None else None
    SC, VC = C.ch("scalar"), C.ch("vector")

    def epi(C, info):
        bank, msz, tn = info["bank"], info["msz"], info["tn"]
        ch = info["moff"] // 128
        act = spec.get("act") or AF.Copy
        kw = {}
        if "scale_sb" in spec:
            kw["scale"] = spec["scale_sb"][0:msz, ch:ch + 1]
        if "bias_sb" in spec:
            kw["bias"] = spec["bias_sb"][0:msz, ch:ch + 1]
        if "scale_cols" in spec:
            c0 = spec["scale_cols"][0]
            kw["scale"] = C.consts["cst"][0:msz, c0:c0 + 1]
        if "bias_cols" in spec:
            c0 = spec["bias_cols"][0]
            kw["bias"] = C.consts["cst"][0:msz, c0:c0 + 1]
        if spec.get("softplus"):
            def sp_ops(e, bank=bank, msz=msz, tn=tn, kw=kw):
                e.activation(eb[bank][0:msz, 0:tn], info["ps"][0:msz, 0:tn],
                             AF.Exp, **kw)
                return e.activation(eb[bank][0:msz, 0:tn], eb[bank][0:msz, 0:tn],
                                    AF.Ln, bias=C.consts["ones_c"][0:msz, 0:1])
            C.rec("scalar", sp_ops, wait=[info["tew"], info["ebw"]], nops=2)
            last = (SC, SC.n)
        elif spec.get("act") is None and "bias" in kw:
            C.rec("vector", lambda e, bank=bank, msz=msz, tn=tn, kw=kw:
                  e.tensor_scalar(eb[bank][0:msz, 0:tn], info["ps"][0:msz, 0:tn],
                                  kw["bias"], None, OP.add),
                  wait=[info["tew"], info["ebw"]], nops=1)
            last = (VC, VC.n)
        else:
            C.rec("scalar", lambda e, bank=bank, msz=msz, tn=tn, act=act, kw=kw:
                  e.activation(eb[bank][0:msz, 0:tn], info["ps"][0:msz, 0:tn],
                               act, **kw),
                  wait=[info["tew"], info["ebw"]], nops=1)
            last = (SC, SC.n)
        nv = 0
        if spec.get("aux") in ("add", "mul"):
            aop = OP.add if spec["aux"] == "add" else OP.mult
            C.rec("vector", lambda e, bank=bank, msz=msz, tn=tn, ch=ch, aop=aop:
                  e.tensor_tensor(eb[bank][0:msz, 0:tn], eb[bank][0:msz, 0:tn],
                                  info["aux"][0:msz, ch, info["t0"]:info["t0"] + tn],
                                  aop),
                  wait=[last], nops=1)
            nv += 1
            last = (VC, VC.n)
        if spec.get("aux2") == "add":
            C.rec("vector", lambda e, bank=bank, msz=msz, tn=tn, ch=ch:
                  e.tensor_tensor(eb[bank][0:msz, 0:tn], eb[bank][0:msz, 0:tn],
                                  info["aux2"][0:msz, ch,
                                               info["a2off"] + info["t0"]:
                                               info["a2off"] + info["t0"] + tn],
                                  OP.add),
                  wait=[last], nops=1)
            nv += 1
            last = (VC, VC.n)
        if spec.get("post"):
            nv2 = spec["post_n"]
            C.rec("vector", lambda e, bank=bank, tn=tn:
                  spec["post"](e, eb[bank], info),
                  wait=[last], nops=nv2)
            last = (VC, VC.n)
        if dst_bf is not None:
            C.rec("vector", lambda e, bank=bank, msz=msz, tn=tn:
                  e.tensor_copy(ebb[bank][0:msz, 0:tn], eb[bank][0:msz, 0:tn]),
                  wait=[last], nops=1)
            last = (VC, VC.n)
        tabs = info["g0"] + info["t0"]
        if dst_f is not None:
            C.rec("gpsimd", lambda e, bank=bank, msz=msz, tn=tn, ch=ch, tabs=tabs:
                  e.dma_start(out=df[0:msz, ch, tabs:tabs + tn],
                              in_=eb[bank][0:msz, 0:tn]),
                  wait=[last], inc=((info["sxg"], 16),))
        if dst_bf is not None:
            C.rec("gpsimd", lambda e, bank=bank, msz=msz, tn=tn, ch=ch, tabs=tabs:
                  e.dma_start(out=db[0:msz, ch, tabs:tabs + tn],
                              in_=ebb[bank][0:msz, 0:tn]),
                  wait=[last], inc=((info["sxg"], 16),))
    return epi


def stage_elem2(C, uid, a_d, b_d, dst_d, dim, ntok, op=OP.mult, wait=()):
    nk = dim // 128
    tt = 512
    ntt = cdiv(ntok, tt)
    sa = fm3(a_d)
    sb_ = fm3(b_d)
    sd = fm3(dst_d)
    ab = [C.sbs(f"el{uid}_a{i}", [128, nk, tt], F32) for i in range(2)]
    bb = [C.sbs(f"el{uid}_b{i}", [128, nk, tt], F32) for i in range(2)]
    ob = [C.sbs(f"el{uid}_o{i}", [128, nk, tt], F32) for i in range(2)]
    wl = [C.sem(f"el{uid}_w{i}") for i in range(2)]
    st = [C.sem(f"el{uid}_s{i}") for i in range(2)]
    VC = C.ch("vector")
    tk = {}
    for ti, (t0, tn) in enumerate(mk_tiles(ntok, tt)):
        par = ti % 2
        lw = list(wait) if ti == 0 else []
        if ti >= 2:
            lw.append(tk[ti - 2])
        C.rec("sync", lambda e, a=ab[par], t0=t0, tn=tn:
              e.dma_start(out=a[:, :, 0:tn], in_=sa[:, :, t0:t0 + tn]),
              wait=lw, inc=((wl[par], 16),))
        C.rec("sync", lambda e, b=bb[par], t0=t0, tn=tn:
              e.dma_start(out=b[:, :, 0:tn], in_=sb_[:, :, t0:t0 + tn]),
              wait=[], inc=((wl[par], 16),))
        C.rec("vector", lambda e, a=ab[par], b=bb[par], o=ob[par], tn=tn:
              e.tensor_tensor(o[:, :, 0:tn], a[:, :, 0:tn], b[:, :, 0:tn], op),
              wait=[(wl[par], wl[par].n), (st[par], 16 * (ti // 2))], nops=1)
        tk[ti] = (VC, VC.n)
        C.rec("sync", lambda e, o=ob[par], t0=t0, tn=tn:
              e.dma_start(out=sd[:, :, t0:t0 + tn], in_=o[:, :, 0:tn]),
              wait=[(VC, VC.n)], inc=((st[par], 16),))
    return [(s, s.n) for s in st]



def emit_g2f(C, nt, fz, zs_d, g2_d):
    VC = C.ch("vector")
    cn = C.consts
    y3g_t = fz["y3g"]
    xbc_d = fz["xbc"]
    drep_d = C.inp("d_rep", (128, 16), F32)
    ld = C.sem("g2f_ld")
    drep = C.sb("g2f_dr", [128, 16], F32)
    C.dma("sync", drep[:, :], drep_d, ld)
    sxs = fm3(xbc_d)
    szs = fm3(zs_d)
    sg2 = fm3(g2_d)
    g2tt = 256
    yb = [C.sb(f"g2f_y{i}", [128, 16, g2tt], F32) for i in range(2)]
    xb = [C.sb(f"g2f_x{i}", [128, 16, g2tt], F32) for i in range(2)]
    zb = [C.sb(f"g2f_z{i}", [128, 16, g2tt], F32) for i in range(2)]
    ob = [C.sb(f"g2f_o{i}", [128, 16, g2tt], F32) for i in range(2)]
    tmp = C.sb("g2f_t", [128, g2tt], F32)
    wl = [C.sem(f"g2f_w{i}") for i in range(2)]
    st = [C.sem(f"g2f_s{i}") for i in range(2)]
    toks = {}
    for ti, (t0, tn) in enumerate(mk_tiles(nt, g2tt)):
        par = ti % 2
        lw = list(fz["y3_bar"]) + [(ld, ld.n), cn["ld"]] if ti == 0 else []
        if ti >= 2:
            lw.append(toks[ti - 2])
        C.dyn_dma(yb[par][:, :, 0:tn], y3g_t, fz["CFG_S0"],
                  [[fz["TB"], 128], [128 * fz["TB"], 16], [1, tn]],
                  wl[par], wait=lw, addc=t0, eng="scalar")
        C.rec("sync", lambda e, b=xb[par], t0=t0, tn=tn:
              e.dma_start(out=b[:, :, 0:tn], in_=sxs[:, 0:16, t0:t0 + tn]),
              wait=[], inc=((wl[par], 16),))
        C.rec("sync", lambda e, b=zb[par], t0=t0, tn=tn:
              e.dma_start(out=b[:, :, 0:tn], in_=szs[:, 0:16, t0:t0 + tn]),
              wait=[], inc=((wl[par], 16),))

        def ops(e, y=yb[par], x=xb[par], z=zb[par], o=ob[par], tn=tn):
            ins = None
            for ch in range(16):
                e.scalar_tensor_tensor(tmp[:, 0:tn], x[:, ch, 0:tn],
                                       drep[:, ch:ch + 1], y[:, ch, 0:tn],
                                       OP.mult, OP.add)
                ins = e.tensor_tensor(o[:, ch, 0:tn], tmp[:, 0:tn],
                                      z[:, ch, 0:tn], OP.mult)
            return ins
        C.rec("vector", ops, wait=[(wl[par], wl[par].n),
                                   (st[par], 16 * (ti // 2))], nops=32)
        toks[ti] = (VC, VC.n)
        C.rec("sync", lambda e, o=ob[par], t0=t0, tn=tn:
              e.dma_start(out=sg2[:, :, t0:t0 + tn], in_=o[:, :, 0:tn]),
              wait=[(VC, VC.n)], inc=((st[par], 16),))
    return [(s, s.n) for s in st + wl]


def emit_launch2(C, nt, fused=False, fz=None):
    DIN = 2048
    y_d = C.inp("y", (DIN, nt), F32) if not fused else None
    zs_d = C.inp("z_silu", (DIN, nt), F32)
    xn_d = C.inp("xn_f", (D, nt), F32)
    mnorm_d = C.inp("mnorm_r", (128, 16), F32)
    wout_d = C.inp("w_outproj", (DIN, D), F32)
    wattn_d = C.inp("w_attn", (D, 1152), F32)

    x2_o = C.out("x2", (D, nt), F32)
    q_o = C.out("q_bf", (D, nt), BF16)
    kv_o = C.out("kv_bf", (128, nt), BF16)
    g2_d = C.scratch("g2", (DIN, nt), F32)
    rms_d = C.scratch("rms_bf", (DIN, nt), BF16)
    r2_d = C.scratch("r2_bf", (D, nt), BF16)

    cn = make_consts(C)
    ld = C.sem("ld")
    mnorm = C.sb("mnorm", [128, 16], F32)
    C.dma("sync", mnorm[:, :], mnorm_d, ld)

    C.sb_push()
    if not fused:
        b1 = stage_elem2(C, "g2", y_d, zs_d, g2_d, DIN, nt, wait=[])
    else:
        b1 = emit_g2f(C, nt, fz, zs_d, g2_d)
    C.sb_pop()
    C.sb_push()
    b2 = stage_rms(C, "m", g2_d, nt, DIN, dst_bf_d=rms_d, scale_sb=mnorm,
                   wait=b1 + [ld])
    C.sb_pop()
    C.sb_push()
    epi1 = mk_epi(C, "o", dict(dst_f=x2_o, aux="add"))
    b3 = stage_proj(C, "o", rms_d, wout_d, DIN, D, nt, epi1,
                    aux_d=xn_d, aux_dim=D, wait=b2)
    C.sb_pop()
    C.sb_push()
    b4 = stage_rms(C, "r2", x2_o, nt, D, dst_bf_d=r2_d, wait=b3)
    C.sb_pop()
    C.sb_push()
    epi2 = mk_epi(C, "q", dict(dst_bf=q_o))
    mt_q = [(m0, 128) for m0 in range(0, 1024, 128)]
    b5 = stage_proj(C, "q", r2_d, wattn_d, D, 1024, nt, epi2, wait=b4,
                    mtiles=mt_q)
    epi3 = mk_epi(C, "kv", dict(dst_bf=kv_o))
    b6 = stage_proj(C, "kv", r2_d,
                    C.inp("w_attn_kv", (D, 128), F32), D, 128, nt, epi3,
                    wait=b4)
    C.sb_pop()
    C.rec("gpsimd", lambda e: e.nop(), wait=b5 + b6)
    return b5 + b6



def emit_launch3(C, Tfull, NU=8, fused=False, fz=None):
    L = 128
    NCH = Tfull // L
    if not fused:
        xs_bf_d = C.inp("xs_bf", (128, NU, NCH, 64), BF16)
        xs_f_d = C.inp("xs_f", (128, NU, NCH, 64), F32)
        btok_d = C.inp("b_tok", (128, NCH, 64), BF16)
        bfm_d = C.inp("b_fm", (64, Tfull), BF16)
        cfm_d = C.inp("c_fm", (64, Tfull), BF16)
        dttok_d = C.inp("dt_tok", (128, NU, NCH), F32)
        y_o = C.out("y_tok", (128, NU, NCH, 64), F32)
    else:
        y3_ap = fz["y3"]
    alog_d = C.inp("alog_r", (128, NU), F32)
    dvec_d = C.inp("d_r", (128, NU), F32)

    cn = make_consts(C)
    tri_d = C.const("tri_c", np.triu(np.ones((128, 128), np.float32)))
    i128_d = C.const("i128_c", np.eye(128, dtype=np.float32))
    mneg = np.where(np.arange(128)[:, None] <= np.arange(128)[None, :],
                    0.0, -1e30).astype(np.float32)
    mask_d = C.const("mask_c", mneg)

    ld = C.sem("ld")
    ldg = C.sem("ldg")
    if fused:
        xs_sb = C.sb("xs_bf_s", [128, NCH, NU, 64], BF16)
        xsl = lambda u, c: xs_sb[:, c, u, :]
    else:
        xs_sb = C.sb("xs_bf_s", [128, NU, NCH, 64], BF16)
        xsl = lambda u, c: xs_sb[:, u, c, :]
    xs_bf = xs_sb
    btok = C.sb("btok_s", [128, NCH, 64], BF16)
    bfm = C.sb("bfm_s", [64, Tfull], BF16)
    cfm = C.sb("cfm_s", [64, Tfull], BF16)
    alog = C.sb("alog_s", [128, NU], F32)
    avals = C.sb("avals_s", [128, NU], F32)
    dvec = C.sb("dvec_s", [128, NU], F32)
    tri = C.sb("tri_s", [128, 128], F32)
    i128 = C.sb("i128_s", [128, 128], F32)
    mask = C.sb("mask_s", [128, 128], F32)
    if not fused:
        dttok = C.sb("dttok_s", [128, NU, NCH], F32)
        for u in range(NU):
            C.dma("sync", xs_bf[:, u, :, :], xs_bf_d[:, u, :, :], ld)
        C.dma("sync", btok[:, :, :], btok_d, ld)
        C.dma("sync", bfm[:, :], bfm_d, ld)
        C.dma("sync", cfm[:, :], cfm_d, ld)
        C.dma("sync", dttok[:, :, :], dttok_d, ld)
    else:
        xbcg_t = fz["xbcg"]
        bcfmg_t = fz["bcfmg"]
        dtfmg_t = fz["dtfmg"]
        entry = list(fz["l3_entry"])
        BLK = fz.get("BLK", 2048)
        CPB = BLK // 128
        NBL = Tfull // BLK
        dtrows = C.sb("dtrows_s", [1, NU * BLK], F32)
        ldq = C.sem("ldq")
        C.dyn_dma(xs_sb[:, :, :, :], xbcg_t, fz["CFG_U0"],
                  [[2176, 128], [128 * 2176, NCH], [1, NU * 64]],
                  ldq, wait=entry)
        for bl in range(NBL):
            C.rec("sync", lambda e, bl=bl:
                  e.dma_start(out=btok[:, bl * CPB:(bl + 1) * CPB, :],
                              in_=bass.AP(xbcg_t, bl * BLK * 2176 + 2048,
                                          [[2176, 128], [128 * 2176, CPB], [1, 64]])),
                  wait=entry, inc=((ld, 16),))
            C.rec("sync", lambda e, bl=bl:
                  e.dma_start(out=bfm[:, bl * BLK:(bl + 1) * BLK],
                              in_=bass.AP(bcfmg_t, bl * 128 * BLK,
                                          [[BLK, 64], [1, BLK]])),
                  wait=entry, inc=((ld, 16),))
            C.rec("sync", lambda e, bl=bl:
                  e.dma_start(out=cfm[:, bl * BLK:(bl + 1) * BLK],
                              in_=bass.AP(bcfmg_t, (bl * 128 + 64) * BLK,
                                          [[BLK, 64], [1, BLK]])),
                  wait=entry, inc=((ld, 16),))
    C.dma("sync", alog[:, :], alog_d, ld)
    C.dma("sync", dvec[:, :], dvec_d, ld)
    C.dma("sync", tri[:, :], tri_d, ld)
    C.dma("sync", i128[:, :], i128_d, ld)
    C.dma("sync", mask[:, :], mask_d, ld)

    VC, SC = C.ch("vector"), C.ch("scalar")
    C.rec("scalar", lambda e: e.activation(avals[:, :], alog[:, :], AF.Exp),
          wait=[(ld, ld.n), cn["ld"]], nops=1)
    C.rec("vector", lambda e: e.tensor_scalar(avals[:, :], avals[:, :],
                                              cn["cst"][:, 7:8], None, OP.mult),
          wait=[(SC, SC.n)], nops=1)
    a_ready = (VC, VC.n)

    hT = [C.sb(f"hT{u}", [64, 64], F32) for u in range(NU)]
    hTb = [C.sb(f"hTb{u}", [64, 64], BF16) for u in range(NU)]
    for u in range(NU):
        C.rec("vector", lambda e, u=u: (e.memset(hT[u][:, :], 0.0),
                                        e.memset(hTb[u][:, :], 0.0))[-1], nops=2)
    h0_ready = (VC, VC.n)

    NS = 4
    bA = [C.ps(f"pbA{r}", [128, 512]) for r in range(NS)]
    bB = [C.ps(f"pbB{r}", [128, 512]) for r in range(NS)]
    ps_d = [bA[r][:, 0:128] for r in range(NS)]
    ps_gt = [bA[r][:, 128:256] for r in range(NS)]
    ps_y = [bA[r][:, 256:320] for r in range(NS)]
    ps_s = [bA[r][0:64, 320:384] for r in range(NS)]
    ps_cd = [bA[r][:, 384:385] for r in range(NS)]
    ps_pl = [bA[r][0:64, 385:386] for r in range(NS)]
    ps_pb = [bB[r][0:64, 0:128] for r in range(NS)]
    ps_row = [bB[r][0:1, 128:256] for r in range(NS)]
    t_lp = [C.sb(f"tlp{r}", [128, 1], F32) for r in range(NS)]
    t_lr = [C.sb(f"tlr{r}", [1, 128], F32) for r in range(NS)]
    t_ln = [C.sb(f"tln{r}", [1, 128], F32) for r in range(NS)]
    t_pr = [C.sb(f"tpr{r}", [1, 128], F32) for r in range(NS)]
    t_dm = [C.sb(f"tdm{r}", [128, 128], F32) for r in range(NS)]
    t_e = [C.sb(f"te{r}", [128, 128], F32) for r in range(NS)]
    t_wt = [C.sb(f"twt{r}", [128, 128], F32) for r in range(NS)]
    t_wtb = [C.sb(f"twtb{r}", [128, 128], BF16) for r in range(NS)]
    t_wc = [C.sb(f"twc{r}", [128, 1], F32) for r in range(NS)]
    t_xw = [C.sb(f"txw{r}", [128, 64], BF16) for r in range(NS)]
    t_cb = [C.sb(f"tcb{r}", [64, 128], BF16) for r in range(NS)]
    t_pl = [C.sb(f"tpl{r}", [64, 1], F32) for r in range(NS)]
    t_y = [C.sb(f"ty{r}", [128, 64], F32) for r in range(NS)]
    t_xf = [C.sb(f"txf{r}", [128, 64], F32) for r in range(NS)] if not fused else None
    t_dtc = [C.sb(f"tdtc{r}", [128, 1], F32) for r in range(NS)] if fused else None
    ps_dtc = [bB[r][:, 384:385] for r in range(NS)]
    ps_y2 = [bB[r][0:64, 256:384] for r in range(NS)]
    t_y2 = [C.sb(f"ty2{r}", [64, 128], F32) for r in range(NS)] if fused else None

    tmm = C.sem("tmm")
    sst = [C.sem(f"sst{r}") for r in range(NS)]
    wxf = [C.sem(f"wxf{r}") for r in range(NS)]

    done_tok = {}
    MM_PER_IT = 6
    it = 0
    dtw = C.sem("dtw") if fused else None
    if fused:
        BLK = fz.get("BLK", 2048)
        CPB = BLK // 128
    for c in range(NCH):
        if fused and c % CPB == 0:
            bl = c // CPB
            C.dyn_dma(dtrows[0:1, :], dtfmg_t, fz["CFG_DT0"],
                      [[1, NU * BLK]],
                      dtw, wait=[(tmm, tmm.n)] + (list(fz["l3_entry"]) if c == 0 else []),
                      addc=bl * 32 * BLK)
        for u in range(NU):
            r = it % NS
            guard = [done_tok[it - NS]] if it >= NS else [a_ready, h0_ready]
            if not fused:
                C.rec("sync", lambda e, r=r, u=u, c=c:
                      e.dma_start(out=t_xf[r][:, :], in_=xs_f_d[:, u, c, :]),
                      wait=guard + [(sst[r], 16 * (it // NS))], inc=((wxf[r], 16),))
            if fused:
                C.rec("tensor", lambda t, r=r, u=u, c=c:
                      t.matmul(ps_dtc[r][:, :],
                               dtrows[0:1, u * BLK + (c % CPB) * 128:
                                      u * BLK + (c % CPB) * 128 + 128],
                               cn["ones_c"][0:1, 0:1], start=True, stop=True),
                      wait=guard + [(ld, ld.n), (dtw, dtw.n), (ldq, 16)],
                      inc=((tmm, 1),))
                C.rec("vector", lambda e, r=r:
                      e.tensor_copy(t_dtc[r][:, :], ps_dtc[r][:, :]),
                      wait=[(tmm, tmm.n)], nops=1)
                C.rec("tensor", lambda t, r=r:
                      t.matmul(ps_cd[r][:, :], tri[:, :], t_dtc[r][:, :],
                               start=True, stop=True),
                      wait=[(VC, VC.n)], inc=((tmm, 1),))
            else:
                C.rec("tensor", lambda t, r=r, u=u, c=c:
                      t.matmul(ps_cd[r][:, :], tri[:, :], dttok[:, u, c:c + 1],
                               start=True, stop=True),
                      wait=guard + [(ld, ld.n)], inc=((tmm, 1),))
            C.rec("vector", lambda e, r=r, u=u:
                  e.tensor_scalar(t_lp[r][:, :], ps_cd[r][:, :],
                                  avals[:, u:u + 1], None, OP.mult),
                  wait=[(tmm, tmm.n), a_ready] + guard, nops=1)
            C.rec("tensor", lambda t, r=r:
                  t.matmul(ps_row[r][:, :], t_lp[r][:, :], i128[:, :],
                           start=True, stop=True),
                  wait=[(VC, VC.n)], inc=((tmm, 1),))
            def rowops(e, r=r):
                e.tensor_copy(t_lr[r][:, :], ps_row[r][:, :])
                e.tensor_scalar(t_ln[r][:, :], ps_row[r][:, :],
                                cn["cst"][0:1, 7:8], None, OP.mult)
                return e
            C.rec("vector", lambda e, r=r: (
                e.tensor_copy(t_lr[r][:, :], ps_row[r][:, :]),
                e.tensor_scalar(t_ln[r][:, :], ps_row[r][:, :],
                                cn["cst"][0:1, 7:8], None, OP.mult))[-1],
                wait=[(tmm, tmm.n)], nops=2)
            def mm_delta(t, r=r, c=c):
                t.matmul(ps_d[r][:, :], t_ln[r][:, :], cn["ones_r"][:, :],
                         start=True, stop=False)
                t.matmul(ps_d[r][:, :], cn["ones_r"][:, :], t_lr[r][:, :],
                         start=False, stop=True)
                return t.matmul(ps_gt[r][:, :], bfm[:, c * L:(c + 1) * L],
                                cfm[:, c * L:(c + 1) * L], start=True, stop=True)
            C.rec("tensor", mm_delta, wait=[(VC, VC.n)], inc=((tmm, 1),))
            C.rec("vector", lambda e, r=r:
                  e.tensor_tensor(t_dm[r][:, :], ps_d[r][:, :], mask[:, :], OP.add),
                  wait=[(tmm, tmm.n)], nops=1)
            def acts1(e, r=r):
                e.activation(t_e[r][:, :], t_dm[r][:, :], AF.Exp)
                return e.activation(t_pr[r][:, :], t_lr[r][:, :], AF.Exp)
            C.rec("scalar", acts1, wait=[(VC, VC.n)], nops=2)
            dtsc = (t_dtc[r][:, :] if fused else dttok[:, u, c:c + 1])
            def wv1(e, r=r, dtsc=dtsc):
                e.tensor_tensor(t_wt[r][:, :], t_e[r][:, :], ps_gt[r][:, :],
                                OP.mult)
                return e.tensor_scalar(t_wtb[r][:, :], t_wt[r][:, :],
                                       dtsc, None, OP.mult)
            C.rec("vector", wv1, wait=[(SC, SC.n)], nops=2)
            C.rec("scalar", lambda e, r=r:
                  e.activation(t_wc[r][:, :], t_dm[r][:, 127:128], AF.Exp),
                  wait=[(VC, VC.n)], nops=1)
            def wv2(e, r=r, u=u, c=c):
                e.tensor_scalar(t_wc[r][:, :], t_wc[r][:, :],
                                dttok[:, u, c:c + 1], None, OP.mult)
                e.tensor_scalar(t_xw[r][:, :], xsl(u, c),
                                t_wc[r][:, :], None, OP.mult)
                return e
            C.rec("vector", lambda e, r=r, u=u, c=c, dtsc=dtsc: (
                e.tensor_scalar(t_wc[r][:, :], t_wc[r][:, :],
                                dtsc, None, OP.mult),
                e.tensor_scalar(t_xw[r][:, :], xsl(u, c),
                                t_wc[r][:, :], None, OP.mult))[-1],
                wait=[(SC, SC.n)], nops=2)
            def mm_p(t, r=r):
                t.matmul(ps_pb[r][:, :], cn["ones_r"][0:1, 0:64], t_pr[r][:, :],
                         start=True, stop=True)
                return t.matmul(ps_pl[r][:, :], cn["ones_r"][0:1, 0:64],
                                t_pr[r][:, 127:128], start=True, stop=True)
            C.rec("tensor", mm_p, wait=[(VC, VC.n)], inc=((tmm, 1),))
            def wv3(e, r=r, c=c):
                e.tensor_tensor(t_cb[r][:, :], cfm[:, c * L:(c + 1) * L],
                                ps_pb[r][:, :], OP.mult)
                return e.tensor_copy(t_pl[r][:, :], ps_pl[r][:, :])
            C.rec("vector", wv3, wait=[(tmm, tmm.n)], nops=2)
            if fused:
                def mm_y(t, r=r, u=u, c=c):
                    t.matmul(ps_y2[r][:, :], xsl(u, c), t_wtb[r][:, :],
                             start=True, stop=False)
                    t.matmul(ps_y2[r][:, :], hTb[u][:, :], t_cb[r][:, :],
                             start=False, stop=True)
                    return t.matmul(ps_s[r][:, :], btok[:, c, :], t_xw[r][:, :],
                                    start=True, stop=True)
            else:
                def mm_y(t, r=r, u=u, c=c):
                    t.matmul(ps_y[r][:, :], t_wtb[r][:, :], xsl(u, c),
                             start=True, stop=False)
                    t.matmul(ps_y[r][:, :], t_cb[r][:, :], hTb[u][:, :],
                             start=False, stop=True)
                    return t.matmul(ps_s[r][:, :], btok[:, c, :], t_xw[r][:, :],
                                    start=True, stop=True)
            C.rec("tensor", mm_y, wait=[(VC, VC.n)], inc=((tmm, 1),))
            if fused:
                def wv4(e, r=r, u=u):
                    e.tensor_copy(t_y2[r][:, :], ps_y2[r][:, :])
                    e.scalar_tensor_tensor(hT[u][:, :], hT[u][:, :], t_pl[r][:, :],
                                           ps_s[r][:, :], OP.mult, OP.add)
                    return e.tensor_copy(hTb[u][:, :], hT[u][:, :])
                C.rec("vector", wv4, wait=[(tmm, tmm.n),
                                           (sst[r], 16 * (it // NS))], nops=3)
                done_tok[it] = (VC, VC.n)
                C.rec("gpsimd", lambda e, r=r, u=u, c=c:
                      e.dma_start(out=y3_ap[64 * u:64 * u + 64,
                                           c * 128:(c + 1) * 128],
                                  in_=t_y2[r][:, :]),
                      wait=[(VC, VC.n)], inc=((sst[r], 16),))
            else:
                def wv4(e, r=r, u=u):
                    e.scalar_tensor_tensor(t_y[r][:, :], t_xf[r][:, :],
                                           dvec[:, u:u + 1], ps_y[r][:, :],
                                           OP.mult, OP.add)
                    e.scalar_tensor_tensor(hT[u][:, :], hT[u][:, :], t_pl[r][:, :],
                                           ps_s[r][:, :], OP.mult, OP.add)
                    return e.tensor_copy(hTb[u][:, :], hT[u][:, :])
                C.rec("vector", wv4, wait=[(tmm, tmm.n), (wxf[r], wxf[r].n)], nops=3)
                done_tok[it] = (VC, VC.n)
                C.rec("gpsimd", lambda e, r=r, u=u, c=c:
                      e.dma_start(out=y_o[:, u, c, :], in_=t_y[r][:, :]),
                      wait=[(VC, VC.n)], inc=((sst[r], 16),))
            it += 1

    if fused:
        return [(s, s.n) for s in sst]
    C.rec("gpsimd", lambda e: e.nop(), wait=[(s, s.n) for s in sst])



def emit_launch4(C, Tfull, NU2=4, fused=False, fz=None):
    NST = Tfull // 128
    NQB = Tfull // 512
    if not fused:
        q_d = C.inp("q_in", (64, NU2, Tfull), BF16)
        k_d = C.inp("k_in", (64, Tfull), BF16)
        vo_d = C.inp("vo_tok", (128, NST, 65), BF16)
        y_o = C.out("y_mqa", (64, NU2, Tfull), F32)
    else:
        qkvg_t = fz["qkvg"]
        y4_ap = fz["y4"]
        entry = list(fz["l4_entry"])
        BLK = fz.get("BLK", 2048)
        NBL = Tfull // BLK

    cn = make_consts(C)
    masks = []
    for j in range(4):
        m = (np.arange(128)[:, None] + 128 * j <= np.arange(512)[None, :])
        masks.append(C.const(f"mq{j}", m.astype(ml_dtypes.bfloat16)))

    ld = C.sem("ld")
    q_sb = C.sb("q_sb", [64, NU2, Tfull], BF16)
    k_sb = C.sb("k_sb", [64, Tfull], BF16)
    vo_sb = C.sb("vo_sb", [128, NST, 65], BF16)
    msk = [C.sb(f"msk{j}", [128, 512], BF16) for j in range(4)]
    if not fused:
        for u in range(NU2):
            C.dma("sync", q_sb[:, u, :], q_d[:, u, :], ld)
        C.dma("sync", k_sb[:, :], k_d, ld)
        C.dma("sync", vo_sb[:, :, :], vo_d, ld)
    else:
        ldq = C.sem("l4ldq")
        for u in range(NU2):
            C.dyn_dma(q_sb[:, u, :], qkvg_t, fz["CFG_Q"],
                      [[BLK, 64], [1152 * BLK, NBL], [1, BLK]], ldq,
                      wait=(entry if u == 0 else []), addc=u * 64 * BLK)
        vfm = C.sb("l4_vfm", [64, Tfull], BF16)
        for bl in range(NBL):
            C.rec("sync", lambda e, bl=bl: e.dma_start(
                out=k_sb[:, bl * BLK:(bl + 1) * BLK],
                in_=bass.AP(qkvg_t, (bl * 1152 + 1024) * BLK,
                            [[BLK, 64], [1, BLK]])),
                wait=entry, inc=((ld, 16),))
            C.rec("sync", lambda e, bl=bl: e.dma_start(
                out=vfm[:, bl * BLK:(bl + 1) * BLK],
                in_=bass.AP(qkvg_t, (bl * 1152 + 1088) * BLK,
                            [[BLK, 64], [1, BLK]])),
                wait=entry, inc=((ld, 16),))
    for j in range(4):
        C.dma("sync", msk[j][:, :], masks[j], ld)

    VC, SC = C.ch("vector"), C.ch("scalar")
    tmm = C.sem("tmm")
    sst = [C.sem(f"sst{i}") for i in range(2)]
    if fused:
        ib4_d = C.const("l4_ib_c", np.eye(128).astype(ml_dtypes.bfloat16))
        ib4 = C.sb("l4_ib", [128, 128], BF16)
        C.dma("sync", ib4[:, :], ib4_d, ld)
        pvt = C.ps("l4_pvt", [128, 256], BF16)
        C.rec("vector", lambda e: e.memset(vo_sb[:, :, 64:65], 1.0), nops=1)
        ttm4 = C.sem("ttm4")
        for st in range(NST):
            sl = st % 2
            C.rec("tensor", lambda t, st=st, sl=sl:
                  t.transpose(pvt[:, sl * 128:sl * 128 + 64],
                              vfm[:, st * 128:(st + 1) * 128], ib4[0:64, 0:64]),
                  wait=[(ld, ld.n), (VC, VC.n)], inc=((ttm4, 1),))
            C.rec("vector", lambda e, st=st, sl=sl:
                  e.tensor_copy(vo_sb[:, st, 0:64],
                                pvt[:, sl * 128:sl * 128 + 64]),
                  wait=[(ttm4, ttm4.n)], nops=1)

    sq = C.sb("sq", [64, 512], F32)
    srow = C.sb("srow", [1, Tfull], F32)
    stat = C.sb("stat", [1, 16], F32)
    cb = C.sb("cbias", [128, NU2], F32)
    ps_st = C.ps("ps_st", [1, 512])
    ps_cb = C.ps("ps_cb", [128, NU2])

    exq = [(ldq, ldq.n)] if fused else []

    def norm_row(idx, src_ap):
        for st in range(Tfull // 512):
            C.rec("scalar", lambda e, st=st:
                  e.activation(sq[:, :], src_ap[:, st * 512:(st + 1) * 512],
                               AF.Square),
                  wait=[(ld, ld.n), cn["ld"], (VC, VC.n)] + exq, nops=1)
            C.rec("tensor", lambda t, st=st:
                  t.matmul(ps_st[:, :], cn["ones_c"][0:64, 0:1], sq[:, :],
                           start=True, stop=True),
                  wait=[(SC, SC.n)], inc=((tmm, 1),))
            C.rec("vector", lambda e, st=st:
                  e.tensor_copy(srow[:, st * 512:(st + 1) * 512], ps_st[:, :]),
                  wait=[(tmm, tmm.n)], nops=1)
        C.rec("vector", lambda e, idx=idx:
              e.reduce_max(stat[0:1, idx:idx + 1], srow[:, :], AX.X, None),
              wait=[(VC, VC.n)], nops=1)

    for u in range(NU2):
        norm_row(u, q_sb[:, u, :])
    norm_row(8, k_sb[:, :])
    def sqrts(e):
        e.activation(stat[0:1, 0:NU2], stat[0:1, 0:NU2], AF.Sqrt)
        return e.activation(stat[0:1, 8:9], stat[0:1, 8:9], AF.Sqrt)
    C.rec("scalar", sqrts, wait=[(VC, VC.n)], nops=2)

    def cb_ops(e):
        e.tensor_scalar(stat[0:1, 0:NU2], stat[0:1, 0:NU2],
                        stat[0:1, 8:9], None, OP.mult)
        return e.tensor_scalar(stat[0:1, 0:NU2], stat[0:1, 0:NU2],
                               cn["cst"][0:1, 3:4], None, OP.mult)
    C.rec("vector", cb_ops, wait=[(SC, SC.n)], nops=2)
    C.rec("tensor", lambda t: t.matmul(ps_cb[:, :], cn["ones_r"][:, :],
                                       stat[0:1, 0:NU2], start=True, stop=True),
          wait=[(VC, VC.n)], inc=((tmm, 1),))
    C.rec("vector", lambda e: e.tensor_scalar(cb[:, :], ps_cb[:, :],
                                              cn["cst"][:, 7:8], None, OP.mult),
          wait=[(tmm, tmm.n)], nops=1)
    cb_ready = (VC, VC.n)

    ps_sc = [C.ps(f"ps_sc{i}", [128, 512]) for i in range(2)]
    ps_y = [C.ps(f"ps_y{i}", [65, 512]) for i in range(2)]
    ps_rb = C.ps("ps_rb", [64, 512])
    att = [C.sb(f"att{i}", [128, 512], BF16) for i in range(2)]
    rz = C.sb("rz", [1, 512], F32)
    yu = [C.sb(f"yu{i}", [64, 512], F32) for i in range(2)]
    ye = [C.sb(f"ye{i}", [64, 512], F32) for i in range(2)]

    att_free = {}
    it = 0
    for u in range(NU2):
        for qb in range(NQB):
            qs = qb * 512
            nst = 4 * qb + 4
            ypar = it % 2
            for st in range(nst):
                sl = st % 2

                def mm_sc(t, sl=sl, st=st, u=u, qs=qs):
                    return t.matmul(ps_sc[sl][:, :], k_sb[:, st * 128:(st + 1) * 128],
                                    q_sb[:, u, qs:qs + 512], start=True, stop=True)
                C.rec("tensor", mm_sc, wait=[cb_ready, (SC, SC.n)],
                      inc=((tmm, 1),))
                j = st - 4 * qb
                C.rec("scalar", lambda e, sl=sl, u=u:
                      e.activation(att[sl][:, :], ps_sc[sl][:, :], AF.Exp,
                                   bias=cb[:, u:u + 1], scale=cn["cst"][:, 3:4]),
                      wait=[(tmm, tmm.n)], nops=1)
                nv = 0
                if 0 <= j < 4:
                    C.rec("vector", lambda e, sl=sl, j=j:
                          e.tensor_tensor(att[sl][:, :], att[sl][:, :],
                                          msk[j][:, :], OP.mult),
                          wait=[(SC, SC.n)], nops=1)
                    nv = 1
                rdy = (VC, VC.n) if nv else (SC, SC.n)

                def mm_av(t, sl=sl, st=st, ypar=ypar, first=(st == 0),
                          last=(st == nst - 1)):
                    return t.matmul(ps_y[ypar][:, :], vo_sb[:, st, :],
                                    att[sl][:, :], start=first, stop=last,
                                    skip_group_check=True)
                C.rec("tensor", mm_av, wait=[rdy], inc=((tmm, 1),))
                att_free[(u, qb, st)] = ((tmm, tmm.n), None)
            C.rec("vector", lambda e, ypar=ypar:
                  e.reciprocal(rz[:, :], ps_y[ypar][64:65, :]),
                  wait=[(tmm, tmm.n), (sst[ypar], 16 * (it // 2))], nops=1)
            C.rec("vector", lambda e, ypar=ypar:
                  e.tensor_copy(ye[ypar][:, :], ps_y[ypar][0:64, :]),
                  wait=[], nops=1)
            C.rec("tensor", lambda t: t.matmul(ps_rb[:, :], cn["ones_r"][0:1, 0:64],
                                               rz[:, :], start=True, stop=True),
                  wait=[(VC, VC.n)], inc=((tmm, 1),))
            C.rec("vector", lambda e, ypar=ypar:
                  e.tensor_tensor(yu[ypar][:, :], ye[ypar][:, :], ps_rb[:, :],
                                  OP.mult),
                  wait=[(tmm, tmm.n)], nops=1)
            if fused:
                C.rec("gpsimd", lambda e, ypar=ypar, u=u, qs=qs:
                      e.dma_start(out=y4_ap[u * 64:(u + 1) * 64, qs:qs + 512],
                                  in_=yu[ypar][:, :]),
                      wait=[(VC, VC.n)], inc=((sst[ypar], 16),))
            else:
                C.rec("gpsimd", lambda e, ypar=ypar, u=u, qs=qs:
                      e.dma_start(out=y_o[:, u, qs:qs + 512], in_=yu[ypar][:, :]),
                      wait=[(VC, VC.n)], inc=((sst[ypar], 16),))
            it += 1

    if fused:
        return [(s, s.n) for s in sst]
    C.rec("gpsimd", lambda e: e.nop(), wait=[(s, s.n) for s in sst])


def stage_cast(C, uid, src_d, dst_d, dim, ntok, wait=()):
    nk = dim // 128
    tt = 512
    ntt = cdiv(ntok, tt)
    sa = fm3(src_d)
    sd = fm3(dst_d)
    ab = [C.sbs(f"cs{uid}_a{i}", [128, nk, tt], F32) for i in range(2)]
    ob = [C.sbs(f"cs{uid}_o{i}", [128, nk, tt], BF16) for i in range(2)]
    wl = [C.sem(f"cs{uid}_w{i}") for i in range(2)]
    st = [C.sem(f"cs{uid}_s{i}") for i in range(2)]
    VC = C.ch("vector")
    tk = {}
    for ti, (t0, tn) in enumerate(mk_tiles(ntok, tt)):
        par = ti % 2
        lw = list(wait) if ti == 0 else []
        if ti >= 2:
            lw.append(tk[ti - 2])
        C.rec("sync", lambda e, a=ab[par], t0=t0, tn=tn:
              e.dma_start(out=a[:, :, 0:tn], in_=sa[:, :, t0:t0 + tn]),
              wait=lw, inc=((wl[par], 16),))
        C.rec("vector", lambda e, a=ab[par], o=ob[par], tn=tn:
              e.tensor_copy(o[:, :, 0:tn], a[:, :, 0:tn]),
              wait=[(wl[par], wl[par].n), (st[par], 16 * (ti // 2))], nops=1)
        tk[ti] = (VC, VC.n)
        C.rec("sync", lambda e, o=ob[par], t0=t0, tn=tn:
              e.dma_start(out=sd[:, :, t0:t0 + tn], in_=o[:, :, 0:tn]),
              wait=[(VC, VC.n)], inc=((st[par], 16),))
    return [(s, s.n) for s in st]


def stage_muladd(C, uid, s_d, g_d, x_d, dst_d, dim, ntok, wait=()):
    nk = dim // 128
    tt = 512
    ntt = cdiv(ntok, tt)
    ss = fm3(s_d)
    sg = fm3(g_d)
    sx = fm3(x_d)
    sd = fm3(dst_d)
    ab = [C.sbs(f"ma{uid}_a{i}", [128, nk, tt], F32) for i in range(2)]
    bb = [C.sbs(f"ma{uid}_b{i}", [128, nk, tt], F32) for i in range(2)]
    cb = [C.sbs(f"ma{uid}_c{i}", [128, nk, tt], F32) for i in range(2)]
    ob = [C.sbs(f"ma{uid}_o{i}", [128, nk, tt], F32) for i in range(2)]
    wl = [C.sem(f"ma{uid}_w{i}") for i in range(2)]
    st = [C.sem(f"ma{uid}_s{i}") for i in range(2)]
    VC = C.ch("vector")
    tk = {}
    for ti, (t0, tn) in enumerate(mk_tiles(ntok, tt)):
        par = ti % 2
        lw = list(wait) if ti == 0 else []
        if ti >= 2:
            lw.append(tk[ti - 2])
        for src, buf in ((ss, ab[par]), (sg, bb[par]), (sx, cb[par])):
            C.rec("sync", lambda e, src=src, buf=buf, t0=t0, tn=tn:
                  e.dma_start(out=buf[:, :, 0:tn], in_=src[:, :, t0:t0 + tn]),
                  wait=lw, inc=((wl[par], 16),))
            lw = []
        def ops(e, a=ab[par], b=bb[par], c=cb[par], o=ob[par], tn=tn):
            e.tensor_tensor(o[:, :, 0:tn], a[:, :, 0:tn], b[:, :, 0:tn], OP.mult)
            return e.tensor_tensor(o[:, :, 0:tn], o[:, :, 0:tn], c[:, :, 0:tn],
                                   OP.add)
        C.rec("vector", ops,
              wait=[(wl[par], wl[par].n), (st[par], 16 * (ti // 2))], nops=2)
        tk[ti] = (VC, VC.n)
        C.rec("sync", lambda e, o=ob[par], t0=t0, tn=tn:
              e.dma_start(out=sd[:, :, t0:t0 + tn], in_=o[:, :, 0:tn]),
              wait=[(VC, VC.n)], inc=((st[par], 16),))
    return [(s, s.n) for s in st]



def emit_launch5(C, nt):
    y_d = C.inp("y_mqa_f", (D, nt), F32)
    x2_d = C.inp("x2", (D, nt), F32)
    wproj_d = C.inp("w_proj", (D, D), F32)
    x3_o = C.out("x3", (D, nt), F32)
    yb_d = C.scratch("y_bf", (D, nt), BF16)
    cn = make_consts(C)
    C.sb_push()
    b1 = stage_cast(C, "y", y_d, yb_d, D, nt)
    C.sb_pop()
    C.sb_push()
    epi = mk_epi(C, "p", dict(dst_f=x3_o, aux="add"))
    b2 = stage_proj(C, "p", yb_d, wproj_d, D, D, nt, epi, aux_d=x2_d, aux_dim=D,
                    wait=b1)
    C.sb_pop()
    C.rec("gpsimd", lambda e: e.nop(), wait=b2)
    return b2



def emit_launch6(C, nt):
    ntok = nt + 1
    x3_d = C.inp("x3h", (D, ntok), F32)
    x_d = C.inp("x_in", (D, nt), F32)
    xnbf_d = C.inp("xn_bf_in", (D, nt), BF16)
    maak_d = C.inp("maa_k_r", (128, 8), F32)
    maar_d = C.inp("maa_r_r", (128, 8), F32)
    wkey_d = C.inp("w_key", (D, 4096), F32)
    wrec_d = C.inp("w_recept", (D, D), F32)
    wval_d = C.inp("w_value", (4096, D), F32)
    w1_d = C.inp("w1T", (D, 4096), F32)
    w2_d = C.inp("w2T", (D, 4096), F32)
    h1_d = C.inp("h1T", (4096, 4096), F32)
    h2_d = C.inp("h2T", (4096, 4096), F32)
    w3_d = C.inp("w3T", (4096, D), F32)
    s1_d = C.inp("w1s_r", (128, 32), F32)
    s2_d = C.inp("w2s_r", (128, 32), F32)
    sh1_d = C.inp("h1s_r", (128, 32), F32)
    sh2_d = C.inp("h2s_r", (128, 32), F32)
    s3_d = C.inp("w3s_r", (128, 8), F32)

    xl1_o = C.out("x_l1", (D, nt), F32)
    r3_d = C.scratch("r3_f", (D, ntok), F32)
    xk_d = C.scratch("xk_bf", (D, nt), BF16)
    xr_d = C.scratch("xr_bf", (D, nt), BF16)
    sig_d = C.scratch("sig_f", (D, nt), F32)
    kk_d = C.scratch("kk_bf", (4096, nt), BF16)
    s_d = C.scratch("s_f", (D, nt), F32)
    t1_d = C.scratch("t1_f", (4096, nt), F32)
    hh_d = C.scratch("h_bf", (4096, nt), BF16)
    g1_d = C.scratch("g1_bf", (4096, nt), BF16)
    g2_d = C.scratch("g2_bf", (4096, nt), BF16)
    g_d = C.scratch("g_f", (D, nt), F32)

    cn = make_consts(C)
    ld = C.sem("ld")
    maak = C.sb("maak", [128, 8], F32)
    maar = C.sb("maar", [128, 8], F32)
    s1s = C.sb("s1s", [128, 32], F32)
    s2s = C.sb("s2s", [128, 32], F32)
    sh1s = C.sb("sh1s", [128, 32], F32)
    sh2s = C.sb("sh2s", [128, 32], F32)
    s3s = C.sb("s3s", [128, 8], F32)
    for sb_, d_ in ((maak, maak_d), (maar, maar_d), (s1s, s1_d), (s2s, s2_d),
                    (sh1s, sh1_d), (sh2s, sh2_d), (s3s, s3_d)):
        C.dma("sync", sb_[:, :], d_, ld)

    C.sb_push()
    b1 = stage_rms(C, "r3", x3_d, ntok, D, dst_f_d=r3_d, wait=[ld])
    C.sb_pop()

    C.sb_push()
    sr3 = fm3(r3_d)
    sxk = fm3(xk_d)
    sxr = fm3(xr_d)
    tt = 512
    ntt = cdiv(nt, tt)
    rb = [C.sbs(f"km_a{i}", [128, 8, tt + 1], F32) for i in range(2)]
    xxb = C.sbs("km_xx", [128, 8, tt], F32)
    okb = [C.sbs(f"km_k{i}", [128, 8, tt], BF16) for i in range(2)]
    orb = [C.sbs(f"km_r{i}", [128, 8, tt], BF16) for i in range(2)]
    wl = [C.sem(f"km_w{i}") for i in range(2)]
    stk = [C.sem(f"km_s{i}") for i in range(2)]
    VC, SC = C.ch("vector"), C.ch("scalar")
    tkx = {}
    for ti, (t0, tn) in enumerate(mk_tiles(nt, tt)):
        par = ti % 2
        lw = list(b1) + [ld] if ti == 0 else []
        if ti >= 2:
            lw.append(tkx[ti - 2])
        C.rec("sync", lambda e, a=rb[par], t0=t0, tn=tn:
              e.dma_start(out=a[:, :, 0:tn + 1], in_=sr3[:, :, t0:t0 + tn + 1]),
              wait=lw, inc=((wl[par], 16),))
        def ops(e, a=rb[par], ok=okb[par], orr=orb[par], tn=tn):
            ins = None
            for c in range(8):
                cur = a[:, c, 1:1 + tn]
                prv = a[:, c, 0:tn]
                e.tensor_tensor(xxb[:, c, 0:tn], prv, cur, OP.subtract)
                e.scalar_tensor_tensor(ok[:, c, 0:tn], xxb[:, c, 0:tn],
                                       maak[:, c:c + 1], cur, OP.mult, OP.add)
                ins = e.scalar_tensor_tensor(orr[:, c, 0:tn], xxb[:, c, 0:tn],
                                             maar[:, c:c + 1], cur, OP.mult, OP.add)
            return ins
        C.rec("vector", ops,
              wait=[(wl[par], wl[par].n), (stk[par], 32 * (ti // 2))], nops=24)
        tkx[ti] = (VC, VC.n)
        C.rec("sync", lambda e, ok=okb[par], t0=t0, tn=tn:
              e.dma_start(out=sxk[:, :, t0:t0 + tn], in_=ok[:, :, 0:tn]),
              wait=[(VC, VC.n)], inc=((stk[par], 16),))
        C.rec("sync", lambda e, orr=orb[par], t0=t0, tn=tn:
              e.dma_start(out=sxr[:, :, t0:t0 + tn], in_=orr[:, :, 0:tn]),
              wait=[(VC, VC.n)], inc=((stk[par], 16),))
    b2 = [(s, s.n) for s in stk]
    C.sb_pop()

    C.sb_push()
    epi_r = mk_epi(C, "rc", dict(dst_f=sig_d, act=AF.Sigmoid))
    b3 = stage_proj(C, "rc", xr_d, wrec_d, D, D, nt, epi_r, wait=b2)
    def post_k(e, eb, info):
        return e.tensor_scalar(eb[0:info["msz"], 0:info["tn"]],
                               eb[0:info["msz"], 0:info["tn"]],
                               cn["cst"][0:info["msz"], 4:5],
                               cn["cst"][0:info["msz"], 4:5], OP.mult, OP.add)
    epi_k = mk_epi(C, "kk", dict(dst_bf=kk_d, act=AF.Erf,
                                 scale_cols=(8, 8), bias_cols=(9, 9),
                                 post=post_k, post_n=1))
    b4 = stage_proj(C, "kk", xk_d, wkey_d, D, 4096, nt, epi_k, wait=b2)
    C.sb_pop()
    C.sb_push()
    epi_v = mk_epi(C, "vv", dict(dst_f=s_d, aux="mul", aux2="add"))
    b5 = stage_proj(C, "vv", kk_d, wval_d, 4096, D, nt, epi_v,
                    aux_d=sig_d, aux_dim=D, aux2_d=x3_d, aux2_dim=D,
                    aux2_off=1, wait=b3 + b4, tg=min(nt, 1024), wslab=256)
    C.sb_pop()

    C.sb_push()
    epi_t1 = mk_epi(C, "t1", dict(dst_f=t1_d, act=AF.Silu, scale_sb=s1s))
    b6 = stage_proj(C, "t1", xnbf_d, w1_d, D, 4096, nt, epi_t1, wait=b5 + [ld])
    C.sb_pop()
    C.sb_push()
    epi_t2 = mk_epi(C, "t2", dict(dst_bf=hh_d, scale_sb=s2s, aux="mul"))
    b7 = stage_proj(C, "t2", xnbf_d, w2_d, D, 4096, nt, epi_t2,
                    aux_d=t1_d, aux_dim=4096, wait=b6, tg=min(nt, 512), wslab=1024)
    C.sb_pop()
    C.sb_push()
    epi_h1 = mk_epi(C, "h1", dict(dst_bf=g1_d, act=AF.Gelu, scale_sb=sh1s))
    b8 = stage_proj(C, "h1", hh_d, h1_d, 4096, 4096, nt, epi_h1,
                    tg=min(nt, 1024), wslab=512, wait=b7)
    C.sb_pop()
    C.sb_push()
    epi_h2 = mk_epi(C, "h2", dict(dst_bf=g2_d, act=AF.Gelu, scale_sb=sh2s))
    b9 = stage_proj(C, "h2", g1_d, h2_d, 4096, 4096, nt, epi_h2,
                    tg=min(nt, 1024), wslab=512, wait=b8)
    C.sb_pop()
    C.sb_push()
    epi_w3 = mk_epi(C, "w3", dict(dst_f=g_d, act=AF.Silu, scale_sb=s3s))
    b10 = stage_proj(C, "w3", g2_d, w3_d, 4096, D, nt, epi_w3, wait=b9,
                     tg=min(nt, 1024), wslab=512)
    C.sb_pop()
    C.sb_push()
    b11 = stage_muladd(C, "fin", s_d, g_d, x_d, xl1_o, D, nt, wait=b5 + b10)
    C.sb_pop()
    C.rec("gpsimd", lambda e: e.nop(), wait=b11)
    return b11



def emit_launch7(C, nt):
    xl1_d = C.inp("x_l1_in", (D, nt), F32)
    wxdt_d = C.inp("w_xdtT", (D, 64), F32)
    bxdt_d = C.inp("b_xdt_r", (64, 1), F32)
    wdt_d = C.inp("w_dtT", (64, D), F32)
    bdt_d = C.inp("b_dt_r", (128, 8), F32)
    wb_d = C.inp("w_bT", (D, 16), F32)
    bb_d = C.inp("b_b_r", (16, 1), F32)
    wc_d = C.inp("w_cT", (D, 16), F32)
    bc_d = C.inp("b_c_r", (16, 1), F32)

    xn1f_o = C.out("xn1_f", (D, nt), F32)
    xn1b_o = C.out("xn1_bf", (D, nt), BF16)
    dt_o = C.out("dt1", (D, nt), F32)
    bt_o = C.out("bt1", (16, nt), F32)
    ct_o = C.out("ct1", (16, nt), F32)
    xdt_d = C.scratch("xdt_bf", (64, nt), BF16)

    cn = make_consts(C)
    ld = C.sem("ld")
    bxdt = C.sb("bxdt", [64, 1], F32)
    bdt = C.sb("bdt", [128, 8], F32)
    bbs = C.sb("bbs", [16, 1], F32)
    bcs = C.sb("bcs", [16, 1], F32)
    for sb_, d_ in ((bxdt, bxdt_d), (bdt, bdt_d), (bbs, bb_d), (bcs, bc_d)):
        C.dma("sync", sb_[:, :], d_, ld)

    C.sb_push()
    b1 = stage_rms(C, "ln", xl1_d, nt, D, dst_bf_d=xn1b_o, dst_f_d=xn1f_o,
                   with_mean=True, wait=[ld])
    C.sb_pop()
    C.sb_push()
    epi_x = mk_epi(C, "xd", dict(dst_bf=xdt_d, bias_sb=bxdt))
    b2 = stage_proj(C, "xd", xn1b_o, wxdt_d, D, 64, nt, epi_x, wait=b1)
    C.sb_pop()
    C.sb_push()
    epi_dt = mk_epi(C, "dt", dict(softplus=True, bias_sb=bdt, dst_f=dt_o))
    b3 = stage_proj(C, "dt", xdt_d, wdt_d, 64, D, nt, epi_dt, wait=b2)
    C.sb_pop()
    C.sb_push()
    epi_b = mk_epi(C, "bt", dict(bias_sb=bbs, dst_f=bt_o))
    b4 = stage_proj(C, "bt", xn1b_o, wb_d, D, 16, nt, epi_b, wait=b3)
    epi_c = mk_epi(C, "ct", dict(bias_sb=bcs, dst_f=ct_o))
    b5 = stage_proj(C, "ct", xn1b_o, wc_d, D, 16, nt, epi_c, wait=b3)
    C.sb_pop()
    C.rec("gpsimd", lambda e: e.nop(), wait=b3 + b4 + b5)
    return b3 + b4 + b5



def emit_launch8(C, Tseg, nseg=2):
    TT_ = Tseg * nseg
    u_d = C.inp("u_sl", (128, TT_), F32)
    dt_d = C.inp("dt_sl", (128, TT_), F32)
    a_d = C.inp("a_sl", (128, 16), F32)
    dv_d = C.inp("d_sl", (128, 1), F32)
    bt_d = C.inp("bt_in", (16, TT_), F32)
    ct_d = C.inp("ct_in", (16, TT_), F32)
    y_o = C.out("y1_sl", (128, TT_), F32)

    cn = make_consts(C)
    sel_np = np.zeros((16, 16 * 128), np.float32)
    for n in range(16):
        sel_np[n, n * 128:(n + 1) * 128] = 1.0
    sel_d = C.const("sel_c", sel_np)

    ld = C.sem("ld")
    avals = C.sb("avals", [128, 16], F32)
    dvs = C.sb("dvs", [128, 1], F32)
    sel = C.sb("sel", [16, 16 * 128], F32)
    C.dma("sync", avals[:, :], a_d, ld)
    C.dma("sync", dvs[:, :], dv_d, ld)
    C.dma("sync", sel[:, :], sel_d, ld)
    VC, SC = C.ch("vector"), C.ch("scalar")
    C.rec("scalar", lambda e: e.activation(avals[:, :], avals[:, :], AF.Exp),
          wait=[(ld, ld.n), cn["ld"]], nops=1)
    C.rec("vector", lambda e: e.tensor_scalar(avals[:, :], avals[:, :],
                                              cn["cst"][:, 7:8], None, OP.mult),
          wait=[(SC, SC.n)], nops=1)
    a_ready = (VC, VC.n)

    tt = 512
    NSL = 2
    ub = [C.sb(f"l8u{i}", [128, tt], F32) for i in range(NSL)]
    dtb = [C.sb(f"l8dt{i}", [128, tt], F32) for i in range(NSL)]
    btb = [C.sb(f"l8bt{i}", [16, tt], F32) for i in range(NSL)]
    ctb = [C.sb(f"l8ct{i}", [16, tt], F32) for i in range(NSL)]
    yb = [C.sb(f"l8y{i}", [128, tt], F32) for i in range(NSL)]
    du = C.sb("l8du", [128, tt], F32)
    wt = C.sb("l8wt", [128, tt], F32)
    da = C.sb("l8da", [128, tt], F32)
    hb = C.sb("l8h", [128, tt], F32)
    yt = C.sb("l8yt", [128, tt], F32)
    hlast = C.sb("l8hl", [128, 16], F32)
    ps_bb = C.ps("l8pbb", [128, 512])
    ps_cb = C.ps("l8pcb", [128, 512])
    wl = [C.sem(f"l8w{i}") for i in range(NSL)]
    sst = [C.sem(f"l8s{i}") for i in range(NSL)]
    tmm = C.sem("l8tmm")

    gi = 0
    tile_toks = []
    for b in range(nseg):
        for (t0, tn) in mk_tiles(Tseg, tt):
            g0 = b * Tseg + t0
            sl = gi % NSL
            lw = [a_ready] if gi == 0 else []
            if gi >= NSL:
                lw.append(tile_toks[gi - NSL])
            for src, dst in ((u_d, ub[sl]), (dt_d, dtb[sl]), (bt_d, btb[sl]),
                            (ct_d, ctb[sl])):
                def ld_fn(e, src=src, dst=dst, g0=g0, tn=tn):
                    if len(src.ap) == 3:
                        bw = src.ap[2][1]
                        return e.dma_start(out=dst[:, 0:tn],
                                           in_=src[:, g0 // bw,
                                                   g0 % bw:g0 % bw + tn])
                    return e.dma_start(
                        out=dst[:, 0:tn] if dst.shape[0] == 128 else dst[0:16, 0:tn],
                        in_=src[:, g0:g0 + tn] if src.shape[0] == 128 else src[0:16, g0:g0 + tn])
                C.rec("sync", ld_fn, wait=lw, inc=((wl[sl], 16),))
                lw = []
            C.rec("vector", lambda e, sl=sl, tn=tn:
                  e.tensor_tensor(du[:, 0:tn], dtb[sl][:, 0:tn], ub[sl][:, 0:tn],
                                  OP.mult),
                  wait=[(wl[sl], wl[sl].n), (sst[sl], 16 * (gi // NSL))], nops=1)
            for n in range(16):
                def mm_bc(t, sl=sl, n=n, tn=tn):
                    t.matmul(ps_bb[:, 0:tn], sel[:, n * 128:(n + 1) * 128],
                             btb[sl][0:16, 0:tn], start=True, stop=True)
                    return t.matmul(ps_cb[:, 0:tn], sel[:, n * 128:(n + 1) * 128],
                                    ctb[sl][0:16, 0:tn], start=True, stop=True)
                C.rec("tensor", mm_bc, wait=[(VC, VC.n), (wl[sl], wl[sl].n)],
                      inc=((tmm, 1),))
                C.rec("vector", lambda e, tn=tn:
                      e.tensor_tensor(wt[:, 0:tn], du[:, 0:tn], ps_bb[:, 0:tn],
                                      OP.mult),
                      wait=[(tmm, tmm.n)], nops=1)
                C.rec("scalar", lambda e, sl=sl, n=n, tn=tn:
                      e.activation(da[:, 0:tn], dtb[sl][:, 0:tn], AF.Exp,
                                   scale=avals[:, n:n + 1]),
                      wait=[(wl[sl], wl[sl].n), a_ready, (VC, VC.n)], nops=1)

                def scan_ops(e, sl=sl, n=n, tn=tn, t0=t0):
                    init = 0.0 if t0 == 0 else hlast[:, n:n + 1]
                    e.tensor_tensor_scan(hb[:, 0:tn], da[:, 0:tn], wt[:, 0:tn],
                                         init, OP.mult, OP.add)
                    e.tensor_copy(hlast[:, n:n + 1], hb[:, tn - 1:tn])
                    if n == 0:
                        return e.tensor_tensor(yb[sl][:, 0:tn], hb[:, 0:tn],
                                               ps_cb[:, 0:tn], OP.mult)
                    e.tensor_tensor(yt[:, 0:tn], hb[:, 0:tn], ps_cb[:, 0:tn],
                                    OP.mult)
                    return e.tensor_tensor(yb[sl][:, 0:tn], yb[sl][:, 0:tn],
                                           yt[:, 0:tn], OP.add)
                C.rec("vector", scan_ops, wait=[(SC, SC.n)],
                      nops=3 if n == 0 else 4)
            C.rec("vector", lambda e, sl=sl, tn=tn:
                  e.scalar_tensor_tensor(yb[sl][:, 0:tn], ub[sl][:, 0:tn],
                                         dvs[:, 0:1], yb[sl][:, 0:tn],
                                         OP.mult, OP.add),
                  wait=[], nops=1)
            tile_toks.append((VC, VC.n))

            def st_fn(e, sl=sl, g0=g0, tn=tn):
                if len(y_o.ap) == 3:
                    bw = y_o.ap[2][1]
                    return e.dma_start(out=y_o[:, g0 // bw, g0 % bw:g0 % bw + tn],
                                       in_=yb[sl][:, 0:tn])
                return e.dma_start(out=y_o[:, g0:g0 + tn], in_=yb[sl][:, 0:tn])
            C.rec("gpsimd", st_fn, wait=[(VC, VC.n)], inc=((sst[sl], 16),))
            gi += 1

    C.rec("gpsimd", lambda e: e.nop(), wait=[(s, s.n) for s in sst])
    return [(s, s.n) for s in sst]



def emit_launch9(C, nt):
    H = 514
    ntok = nt + H
    NQK = nt + 512
    NA = nt + 256
    NW = nt // 256 + 1
    x_d = C.inp("x9", (D, ntok), F32)
    wqkv_d = C.inp("wqkv", (D, 3072), F32)
    wo_d = C.inp("wo", (D, D), F32)
    qs_d = C.inp("qs_r", (128, 8), F32)
    ks_d = C.inp("ks_r", (128, 8), F32)
    csq_d = C.inp("csq", (128, NQK), F32)
    ssq_d = C.inp("ssq", (128, NQK), F32)
    csk_d = C.inp("csk", (128, NQK), F32)
    ssk_d = C.inp("ssk", (128, NQK), F32)
    h0f_d = C.inp("h0flag", (128, 8), F32)
    bm0_d = C.inp("bmask0", (128, 256), BF16)
    bm1_d = C.inp("bmask1", (128, 256), BF16)
    lng_d = C.inp("ln_g_r", (128, 8), F32)
    lnb_d = C.inp("ln_b_r", (128, 8), F32)
    w1_d = C.inp("w1l", (D, 5460), F32)
    b1a_d = C.inp("b1a_r", (128, 22), F32)
    b1g_d = C.inp("b1g_r", (128, 22), F32)
    w2_d = C.inp("w2lT", (2816, D), F32)
    b2_d = C.inp("b2_r", (128, 8), F32)
    xout_o = C.out("x9o", (D, nt), F32)

    cn = make_consts(C)
    rot_np = np.zeros((128, 128), np.float32)
    for h2 in range(2):
        b0 = 64 * h2
        for i in range(32):
            rot_np[b0 + i, b0 + 32 + i] = -1.0
            rot_np[b0 + 32 + i, b0 + i] = 1.0
    rotT_d = C.const("rotT", np.ascontiguousarray(rot_np.T.astype(ml_dtypes.bfloat16)))
    hsel_np = np.zeros((128, 8, 16), np.float32)
    for c in range(8):
        hsel_np[0:64, c, 2 * c] = 1.0
        hsel_np[64:128, c, 2 * c + 1] = 1.0
    hsel_d = C.const("hsel", hsel_np)
    hexp_np = np.zeros((16, 8, 128), np.float32)
    for c in range(8):
        hexp_np[2 * c, c, 0:64] = 1.0
        hexp_np[2 * c + 1, c, 64:128] = 1.0
    hexp_d = C.const("hexp", hexp_np)
    i128bf_d = C.const("i128bf", np.eye(128, dtype=np.float32))
    ibf_d = C.const("ibf", np.eye(128).astype(ml_dtypes.bfloat16))
    onesbf_d = C.const("onesbf", np.ones((128, 1), ml_dtypes.bfloat16))
    amask_np = []
    for j in range(4):
        p = np.arange(128)[:, None]
        q = np.arange(256)[None, :]
        if j == 0:
            m = (q <= p)
        elif j == 1:
            m = (q >= p - 128) & (q <= p + 128)
        elif j == 2:
            m = (q >= p)
        else:
            m = (q >= p + 128)
        amask_np.append(m)
    amask_d = [C.const(f"am{j}", amask_np[j].astype(ml_dtypes.bfloat16))
               for j in range(4)]

    ld = C.sem("ld")
    rotT = C.sb("rotT_s", [128, 128], BF16)
    hsel = C.sb("hsel_s", [128, 8, 16], F32)
    hexp = C.sb("hexp_s", [16, 8, 128], F32)
    i128 = C.sb("i128_s", [128, 128], F32)
    ibf = C.sb("ibf_s", [128, 128], BF16)
    onesbf = C.sb("onesbf_s", [128, 1], BF16)
    amask = C.sb("am_s", [128, 1024], BF16)
    bmask = C.sb("bm_s", [128, 512], BF16)
    qs = C.sb("qs_s", [128, 8], F32)
    ks = C.sb("ks_s", [128, 8], F32)
    h0f = C.sb("h0f_s", [128, 8], F32)
    lng = C.sb("lng_s", [128, 8], F32)
    lnb = C.sb("lnb_s", [128, 8], F32)
    for sb_, d_ in ((rotT, rotT_d), (hsel, hsel_d), (hexp, hexp_d),
                    (i128, i128bf_d), (ibf, ibf_d), (onesbf, onesbf_d),
                    (qs, qs_d), (ks, ks_d), (h0f, h0f_d), (lng, lng_d),
                    (lnb, lnb_d)):
        C.dma("sync", sb_[...], d_, ld)
    for j in range(4):
        C.dma("sync", amask[:, j * 256:(j + 1) * 256], amask_d[j], ld)
    C.dma("sync", bmask[:, 0:256], bm0_d, ld)
    C.dma("sync", bmask[:, 256:512], bm1_d, ld)

    xs_d = C.scratch("xs9_bf", (D, NQK), BF16)
    qkv_d = C.scratch("qkv9_bf", (3072, NQK), BF16)
    qp_d = C.scratch("qp9_bf", (D, NQK), BF16)
    kp_d = C.scratch("kp9_bf", (D, NQK), BF16)
    aw_d = C.scratch("aw9_bf", (D, NA), BF16)
    xatt_d = C.scratch("xatt9", (D, NA), F32)
    xs2_d = C.scratch("xs29", (D, nt), F32)
    x2n_d = C.scratch("x2n9_bf", (D, nt), BF16)
    VC, SC = C.ch("vector"), C.ch("scalar")

    C.sb_push()
    sx9 = fm3(x_d)
    sxs = fm3(xs_d)
    ab = [C.sbs(f"sa_a{i}", [128, 8, 512 + 1], F32) for i in range(2)]
    ob = [C.sbs(f"sa_o{i}", [128, 8, 512], BF16) for i in range(2)]
    wl = [C.sem(f"sa_w{i}") for i in range(2)]
    stq = [C.sem(f"sa_s{i}") for i in range(2)]
    tks = {}
    for ti, (t0, tn) in enumerate(mk_tiles(NQK, 512)):
        par = ti % 2
        lw = [ld, cn["ld"]] if ti == 0 else []
        if ti >= 2:
            lw.append(tks[ti - 2])
        C.rec("sync", lambda e, a=ab[par], t0=t0, tn=tn:
              e.dma_start(out=a[:, :, 0:tn + 1], in_=sx9[:, :, t0 + 1:t0 + 2 + tn]),
              wait=lw, inc=((wl[par], 16),))

        def sh_ops(e, a=ab[par], o=ob[par], tn=tn):
            ins = None
            for c in range(8):
                src = a[:, c, 1:1 + tn] if c < 4 else a[:, c, 0:tn]
                ins = e.tensor_copy(o[:, c, 0:tn], src)
            return ins
        C.rec("vector", sh_ops,
              wait=[(wl[par], wl[par].n), (stq[par], 16 * (ti // 2))], nops=8)
        tks[ti] = (VC, VC.n)
        C.rec("sync", lambda e, o=ob[par], t0=t0, tn=tn:
              e.dma_start(out=sxs[:, :, t0:t0 + tn], in_=o[:, :, 0:tn]),
              wait=[(VC, VC.n)], inc=((stq[par], 16),))
    ba = [(s, s.n) for s in stq]
    C.sb_pop()

    C.sb_push()
    epi_qkv = mk_epi(C, "qk", dict(dst_bf=qkv_d))
    bb = stage_proj(C, "qk", xs_d, wqkv_d, D, 3072, NQK, epi_qkv, wait=ba)
    C.sb_pop()

    C.sb_push()
    sqkv = fm3(qkv_d)
    sqp = fm3(qp_d)
    skp = fm3(kp_d)
    scsq = csq_d
    sssq = ssq_d
    scsk = csk_d
    sssk = ssk_d
    qb = [C.sbs(f"nc_q{i}", [128, 8, 512], BF16) for i in range(2)]
    tbl = [C.sbs(f"nc_t{i}", [128, 4, 512], F32) for i in range(2)]
    sqb = C.sbs("nc_sq", [128, 8, 512], F32)
    rqrow = C.sbs("nc_rq", [16, 512], F32)
    qn = C.sbs("nc_qn", [128, 512], F32)
    qnb = C.sbs("nc_qnb", [128, 512], BF16)
    t1b = C.sbs("nc_tt1", [128, 512], F32)
    oqb = [C.sbs(f"nc_o{i}", [128, 8, 512], BF16) for i in range(2)]
    ps16 = C.pss("nc_ps16", [16, 512])
    psrb = C.pss("nc_psrb", [128, 512])
    psrot = C.pss("nc_psrot", [128, 512])
    wl2 = [C.sem(f"nc_w{i}") for i in range(2)]
    st2 = [C.sem(f"nc_s{i}") for i in range(2)]
    tq = C.sem("nc_tq")
    tks2 = {}
    it2 = 0
    for qk in range(2):
        rows_off = 0 if qk == 0 else 8
        src_cs = scsq if qk == 0 else scsk
        src_ss = sssq if qk == 0 else sssk
        scl = qs if qk == 0 else ks
        dst = sqp if qk == 0 else skp
        for ti, (t0, tn) in enumerate(mk_tiles(NQK, 512)):
            par = it2 % 2
            lw = [ld] + list(bb) if it2 == 0 else []
            if it2 >= 2:
                lw.append(tks2[it2 - 2])
            C.rec("sync", lambda e, b_=qb[par], t0=t0, tn=tn, ro=rows_off:
                  e.dma_start(out=b_[:, :, 0:tn], in_=sqkv[:, ro:ro + 8, t0:t0 + tn]),
                  wait=lw, inc=((wl2[par], 16),))
            wsem2 = wl2[par]
            def tload(e, b_=tbl[par], t0=t0, tn=tn, cs=src_cs, ss=src_ss,
                      wsem2=wsem2):
                e.dma_start(out=b_[:, 0, 0:tn],
                            in_=cs[:, t0:t0 + tn]).then_inc(wsem2.h, 16)
                return e.dma_start(out=b_[:, 1, 0:tn],
                                   in_=ss[:, t0:t0 + tn]).then_inc(wsem2.h, 16)
            C.rec("sync", tload, wait=[])
            C.mark(wsem2, 32, "sync")
            C.rec("scalar", lambda e, b_=qb[par], tn=tn:
                  e.activation(sqb[:, :, 0:tn], b_[:, :, 0:tn], AF.Square),
                  wait=[(wl2[par], wl2[par].n)], nops=1)

            def mm_ss(t, tn=tn):
                ins = None
                for c in range(8):
                    ins = t.matmul(ps16[:, 0:tn], hsel[:, c, :], sqb[:, c, 0:tn],
                                   start=(c == 0), stop=(c == 7))
                return ins
            C.rec("tensor", mm_ss, wait=[(SC, SC.n)], inc=((tq, 1),))
            C.rec("scalar", lambda e, tn=tn:
                  e.activation(rqrow[:, 0:tn], ps16[:, 0:tn], AF.Sqrt),
                  wait=[(tq, tq.n)], nops=1)

            def rowo(e, tn=tn):
                e.tensor_scalar(rqrow[:, 0:tn], rqrow[:, 0:tn],
                                cn["cst"][0:16, 11:12], None, OP.max)
                return e.reciprocal(rqrow[:, 0:tn], rqrow[:, 0:tn])
            C.rec("vector", rowo, wait=[(SC, SC.n)], nops=2)
            rq_tok = (VC, VC.n)
            for c in range(8):
                C.rec("tensor", lambda t, c=c, tn=tn:
                      t.matmul(psrb[:, 0:tn], hexp[:, c, :], rqrow[0:16, 0:tn],
                               start=True, stop=True),
                      wait=[rq_tok, (VC, VC.n)], inc=((tq, 1),))

                def app1(e, b_=qb[par], c=c, tn=tn, scl=scl):
                    e.scalar_tensor_tensor(qn[:, 0:tn], b_[:, c, 0:tn],
                                           scl[:, c:c + 1], psrb[:, 0:tn],
                                           OP.mult, OP.mult)
                    return e.tensor_copy(qnb[:, 0:tn], qn[:, 0:tn])
                C.rec("vector", app1, wait=[(tq, tq.n)], nops=2)
                C.rec("tensor", lambda t, tn=tn:
                      t.matmul(psrot[:, 0:tn], rotT[:, :], qnb[:, 0:tn],
                               start=True, stop=True),
                      wait=[(VC, VC.n)], inc=((tq, 1),))

                def app2(e, b_=tbl[par], o=oqb[par], c=c, tn=tn):
                    e.tensor_tensor(t1b[:, 0:tn], psrot[:, 0:tn], b_[:, 1, 0:tn],
                                    OP.mult)
                    e.tensor_tensor(qn[:, 0:tn], qn[:, 0:tn], b_[:, 0, 0:tn],
                                    OP.mult)
                    e.tensor_tensor(qn[:, 0:tn], qn[:, 0:tn], t1b[:, 0:tn], OP.add)
                    return e.tensor_copy(o[:, c, 0:tn], qn[:, 0:tn])
                a2w = [(tq, tq.n)]
                if c == 0:
                    a2w.append((st2[par], 16 * (it2 // 2)))
                C.rec("vector", app2, wait=a2w, nops=4)
            tks2[it2] = (VC, VC.n)
            C.rec("sync", lambda e, o=oqb[par], dst=dst, t0=t0, tn=tn:
                  e.dma_start(out=dst[:, :, t0:t0 + tn], in_=o[:, :, 0:tn]),
                  wait=[(VC, VC.n)], inc=((st2[par], 16),))
            it2 += 1
    bc = [(s, s.n) for s in st2]
    C.sb_pop()

    C.sb_push()
    vres = C.sbs("vres9", [128, 20, 1040], BF16)
    C.rec("vector", lambda e: [e.memset(vres[:, a, :], 1.0)
                               for a in range(20)][-1], nops=20)
    C.sb_push()
    vb = [C.sbs(f"vt_v{i}", [128, 8, 512], BF16) for i in range(2)]
    psvt = [C.pss(f"vt_ps{i}", [128, 128], BF16) for i in range(2)]
    wl3 = [C.sem(f"vt_w{i}") for i in range(2)]
    tv = C.sem("vt_tv")
    tks3 = {}
    it3 = 0
    for ti, (t0, tn) in enumerate(mk_tiles(NQK, 512)):
        par = ti % 2
        lw = [ld] + bb if ti == 0 else []
        if ti >= 2:
            lw.append(tks3[ti - 2])
        C.rec("sync", lambda e, b_=vb[par], t0=t0, tn=tn:
              e.dma_start(out=b_[:, :, 0:tn], in_=sqkv[:, 16:24, t0:t0 + tn]),
              wait=lw, inc=((wl3[par], 16),))
        for c in range(8):
            for sub in range(cdiv(tn, 128)):
                s0 = sub * 128
                sn = min(128, tn - s0)
                sp = it3 % 2

                def mm_tr(t, b_=vb[par], c=c, s0=s0, sn=sn, sp=sp):
                    return t.transpose(psvt[sp][0:sn, :], b_[:, c, s0:s0 + sn],
                                       ibf[:, :])
                C.rec("tensor", mm_tr,
                      wait=[(wl3[par], wl3[par].n), (VC, VC.n)], inc=((tv, 1),))

                def vcp(e, sp=sp, sn=sn, c=c, a=(t0 + s0) // 128):
                    e.tensor_copy(vres[0:sn, a, 130 * c:130 * c + 64],
                                  psvt[sp][0:sn, 0:64])
                    return e.tensor_copy(vres[0:sn, a, 130 * c + 65:130 * c + 129],
                                         psvt[sp][0:sn, 64:128])
                C.rec("vector", vcp, wait=[(tv, tv.n)], nops=2)
                it3 += 1
        tks3[ti] = (VC, VC.n)
    bd = [(VC, VC.n)]
    C.sb_pop()

    onesr_bf_d = C.const("onesr_bf", np.ones((1, 128), ml_dtypes.bfloat16))
    onesr_bf = C.sb("onesr_bf_s", [1, 128], BF16)
    C.dma("sync", onesr_bf[:, :], onesr_bf_d, ld)
    C.sb_push()
    saw = fm3(aw_d)
    NSA = 4
    qw = [C.sbs(f"at_q{i}", [64, 256], BF16) for i in range(NSA)]
    kw = [C.sbs(f"at_k{i}", [64, 512], BF16) for i in range(NSA)]
    attb = [C.sbs(f"at_a{i}", [128, 512], BF16) for i in range(NSA)]
    negm = [C.sbs(f"at_m{i}", [1, 256], BF16) for i in range(NSA)]
    qn9a = [C.sbs(f"at_qn{i}", [128, 2], F32) for i in range(NSA)]
    rzb = [C.sbs(f"at_rz{i}", [1, 256], F32) for i in range(NSA)]
    yeb = [C.sbs(f"at_ye{i}", [64, 256], F32) for i in range(NSA)]
    yhb = [C.sbs(f"at_yh{i}", [64, 256], BF16) for i in range(NSA)]
    bS = [C.pss(f"at_bS{i}", [128, 512]) for i in range(NSA)]
    bV = [C.pss(f"at_bV{i}", [128, 512]) for i in range(NSA)]
    wl4 = [C.sem(f"at_w{i}") for i in range(NSA)]
    st4 = [C.sem(f"at_s{i}") for i in range(NSA)]
    ta = C.sem("at_t")
    iters = [(h, wi) for h in range(16) for wi in range(1, NW + 1)]
    NIT = len(iters)
    ldtok, t1, t2, t3, t4, t5, t6, t7, t8 = ({} for _ in range(9))
    v3, v4, v5, v6, v7, sx1, sx2, dn = ({} for _ in range(8))

    def st_L(i):
        h, wi = iters[i]
        sl = i % NSA
        chq, ro = h // 2, 64 * (h % 2)
        j0 = wi * 256
        k0 = (wi - 1) * 256
        lw = bc + bd + [ld] if i == 0 else []
        if i >= NSA:
            lw.append(dn[i - NSA])
        C.rec("sync", lambda e, sl=sl, chq=chq, ro=ro, j0=j0:
              e.dma_start(out=qw[sl][:, :], in_=sqp[ro:ro + 64, chq, j0:j0 + 256]),
              wait=lw, inc=((wl4[sl], 16),))
        C.rec("sync", lambda e, sl=sl, chq=chq, ro=ro, k0=k0:
              e.dma_start(out=kw[sl][:, :], in_=skp[ro:ro + 64, chq, k0:k0 + 512]),
              wait=[], inc=((wl4[sl], 16),))
        ldtok[i] = (wl4[sl], wl4[sl].n)

    def st_T1(i):
        sl = i % NSA
        w = [ldtok[i]] + ([dn[i - NSA]] if i >= NSA else [])
        C.rec("tensor", lambda t, sl=sl:
              t.matmul(bS[sl][:, 0:512], qw[sl][:, 0:128], kw[sl][:, :],
                       start=True, stop=True), wait=w, inc=((ta, 1),))
        t1[i] = (ta, ta.n)

    def st_V1(i):
        sl = i % NSA
        C.rec("vector", lambda e, sl=sl:
              e.reduce_max(qn9a[sl][:, 0:1], bS[sl][:, 0:512], AX.X, None),
              wait=[t1[i]], nops=1)

    def st_T2(i):
        sl = i % NSA
        w = [ldtok[i]] + ([dn[i - NSA]] if i >= NSA else [])
        C.rec("tensor", lambda t, sl=sl:
              t.matmul(bV[sl][:, 0:512], qw[sl][:, 128:256], kw[sl][:, :],
                       start=True, stop=True), wait=w, inc=((ta, 1),))
        t2[i] = (ta, ta.n)

    def st_V2(i):
        sl = i % NSA
        C.rec("vector", lambda e, sl=sl:
              e.reduce_max(qn9a[sl][:, 1:2], bV[sl][:, 0:512], AX.X, None),
              wait=[t2[i]], nops=1)
        v3[i] = None
        t3.setdefault("v2_%d" % i, (VC, VC.n))

    def st_T3(i):
        sl = i % NSA

        def mm_mrow(t, sl=sl):
            t.matmul(bV[sl][0:1, 256:384], qn9a[sl][:, 0:1], i128[:, :],
                     start=True, stop=True)
            return t.matmul(bV[sl][0:1, 384:512], qn9a[sl][:, 1:2],
                            i128[:, :], start=True, stop=True)
        C.rec("tensor", mm_mrow, wait=[t3["v2_%d" % i]], inc=((ta, 1),))
        t3[i] = (ta, ta.n)

    def st_V3(i):
        sl = i % NSA
        C.rec("vector", lambda e, sl=sl:
              e.tensor_scalar(negm[sl][:, :], bV[sl][0:1, 256:512],
                              cn["cst"][0:1, 7:8], None, OP.mult),
              wait=[t3[i]], nops=1)
        v3[i] = (VC, VC.n)

    def mk_p2(i, half):
        sl = i % NSA

        def mm_p2(t, sl=sl, half=half):
            ins = None
            for jj in range(2):
                j = 2 * half + jj
                t.matmul(bS[sl][:, jj * 256:(jj + 1) * 256],
                         kw[sl][:, j * 128:(j + 1) * 128],
                         qw[sl][:, :], start=True, stop=False,
                         skip_group_check=True)
                ins = t.matmul(bS[sl][:, jj * 256:(jj + 1) * 256],
                               onesr_bf[:, :], negm[sl][:, :],
                               start=False, stop=True,
                               skip_group_check=True)
            return ins
        return mm_p2

    def st_T4(i):
        C.rec("tensor", mk_p2(i, 0), wait=[v3[i]], inc=((ta, 1),))
        t4[i] = (ta, ta.n)

    def st_S1(i):
        sl = i % NSA
        w = [t4[i]] + ([t7[i - NSA]] if i >= NSA else [])
        C.rec("scalar", lambda e, sl=sl:
              e.activation(attb[sl][:, :], bS[sl][:, 0:512], AF.Exp,
                           scale=cn["cst"][:, 6:7]), wait=w, nops=1)
        sx1[i] = (SC, SC.n)

    def st_V4(i):
        h, wi = iters[i]
        sl = i % NSA
        mA = bmask[:, :] if wi == 2 else amask[:, 0:512]
        C.rec("vector", lambda e, sl=sl, mj=mA:
              e.tensor_tensor(attb[sl][:, :], attb[sl][:, :], mj, OP.mult),
              wait=[sx1[i]], nops=1)
        v4[i] = (VC, VC.n)

    def mk_av(i, half):
        h, wi = iters[i]
        sl = i % NSA
        k0 = (wi - 1) * 256

        def mm_av(t, sl=sl, half=half, h=h, k0=k0):
            ins = None
            for jj in range(2):
                j = 2 * half + jj
                ins = t.matmul(bV[sl][0:65, 0:256],
                               vres[:, k0 // 128 + j, 65 * h:65 * h + 65],
                               attb[sl][:, jj * 256:(jj + 1) * 256],
                               start=(j == 0), stop=(j == 3),
                               skip_group_check=True)
            return ins
        return mm_av

    def st_T5(i):
        C.rec("tensor", mk_av(i, 0), wait=[v4[i]], inc=((ta, 1),))
        t5[i] = (ta, ta.n)

    def st_T6(i):
        C.rec("tensor", mk_p2(i, 1), wait=[v3[i]], inc=((ta, 1),))
        t6[i] = (ta, ta.n)

    def st_S2(i):
        sl = i % NSA
        C.rec("scalar", lambda e, sl=sl:
              e.activation(attb[sl][:, :], bS[sl][:, 0:512], AF.Exp,
                           scale=cn["cst"][:, 6:7]), wait=[t6[i]], nops=1)
        sx2[i] = (SC, SC.n)

    def st_V5(i):
        sl = i % NSA
        C.rec("vector", lambda e, sl=sl, mj=amask[:, 512:1024]:
              e.tensor_tensor(attb[sl][:, :], attb[sl][:, :], mj, OP.mult),
              wait=[sx2[i]], nops=1)
        v5[i] = (VC, VC.n)

    def st_T7(i):
        C.rec("tensor", mk_av(i, 1), wait=[v5[i]], inc=((ta, 1),))
        t7[i] = (ta, ta.n)

    def st_V6(i):
        sl = i % NSA

        def fin1(e, sl=sl):
            e.reciprocal(rzb[sl][:, :], bV[sl][64:65, 0:256])
            return e.tensor_copy(yeb[sl][:, :], bV[sl][0:64, 0:256])
        C.rec("vector", fin1, wait=[t7[i], (st4[sl], 16 * (i // NSA))], nops=2)
        v6[i] = (VC, VC.n)

    def st_T8(i):
        sl = i % NSA
        C.rec("tensor", lambda t, sl=sl:
              t.matmul(bV[sl][0:64, 256:512], cn["ones_r"][0:1, 0:64],
                       rzb[sl][:, :], start=True, stop=True),
              wait=[v6[i]], inc=((ta, 1),))
        t8[i] = (ta, ta.n)

    def st_V7(i):
        sl = i % NSA

        def fin(e, sl=sl):
            e.tensor_tensor(yeb[sl][:, :], yeb[sl][:, :],
                            bV[sl][0:64, 256:512], OP.mult)
            return e.tensor_copy(yhb[sl][:, :], yeb[sl][:, :])
        C.rec("vector", fin, wait=[t8[i]], nops=2)
        v7[i] = dn[i] = (VC, VC.n)

    def st_St(i):
        h, wi = iters[i]
        sl = i % NSA
        chq, ro = h // 2, 64 * (h % 2)
        C.rec("sync", lambda e, sl=sl, chq=chq, ro=ro, wi=wi:
              e.dma_start(out=saw[ro:ro + 64, chq, (wi - 1) * 256:wi * 256],
                          in_=yhb[sl][:, :]),
              wait=[v7[i]], inc=((st4[sl], 16),))

    def ph1_list(p):
        a, b = 2 * p, 2 * p + 1
        out = []
        for st in (st_L, st_T1, st_V1, st_T2, st_V2, st_T3, st_V3, st_T4,
                   st_S1, st_V4):
            out.append((st, a))
            out.append((st, b))
        return out

    def ph2_list(p):
        a, b = 2 * p, 2 * p + 1
        out = []
        for st in (st_T5, st_T6, st_S2, st_V5, st_T7, st_V6, st_T8, st_V7,
                   st_St):
            out.append((st, a))
            out.append((st, b))
        return out

    NP = NIT // 2
    for p in range(NP + 1):
        l2 = ph2_list(p - 1) if p >= 1 else []
        l1 = ph1_list(p) if p < NP else []
        n = max(len(l1), len(l2))
        for k in range(n):
            if k < len(l2):
                l2[k][0](l2[k][1])
            if k < len(l1):
                l1[k][0](l1[k][1])
    be = [(s, s.n) for s in st4]
    C.sb_pop()
    C.sb_pop()

    C.sb_push()
    epi_wo = mk_epi(C, "wo", dict(dst_f=xatt_d, aux="add"))
    bf_ = stage_proj(C, "wo", aw_d, wo_d, D, D, NA, epi_wo,
                     aux_d=x_d[:, 258:258 + NA], aux_dim=D, wait=be)
    C.sb_pop()
    C.sb_push()
    sxa = fm3(xatt_d)
    fixb = C.sbs("fx_b", [128, 8, 8], F32)
    wl5 = C.sem("fx_w")
    st5 = C.sem("fx_s")
    C.rec("sync", lambda e: e.dma_start(out=fixb[:, :, :], in_=sxa[:, :, 248:256]),
          wait=bf_, inc=((wl5, 16),))

    def fixops(e):
        ins = None
        for c in range(8):
            ins = e.tensor_scalar(fixb[:, c, 7:8], fixb[:, c, 7:8],
                                  h0f[:, c:c + 1], None, OP.mult)
        return ins
    C.rec("vector", fixops, wait=[(wl5, wl5.n), ld], nops=8)
    C.rec("sync", lambda e: e.dma_start(out=sxa[:, :, 248:256], in_=fixb[:, :, :]),
          wait=[(VC, VC.n)], inc=((st5, 16),))
    bfix = [(st5, st5.n)]
    C.sb_pop()

    C.sb_push()
    w1s = w1_d.rearrange("(c p) m -> p c m", p=128)
    w2s = w2_d.rearrange("(c p) m -> p c m", p=128)
    wA = C.sbs("ff_wa", [128, 8, 2730], BF16)
    wG = C.sbs("ff_wg", [128, 8, 2730], BF16)
    w2b = C.sbs("ff_w2", [128, 22, 1024], BF16)
    wld = C.sem("ff_wld")

    def wload_all(e):
        ins = None
        for c in range(8):
            ins = e.dma_start(out=wA[:, c, :], in_=w1s[:, c, 0:2730])
            ins.then_inc(wld.h, 16)
        for c in range(8):
            ins = e.dma_start(out=wG[:, c, :], in_=w1s[:, c, 2730:5460])
            ins.then_inc(wld.h, 16)
        for c in range(22):
            ins = e.dma_start(out=w2b[:, c, :], in_=w2s[:, c, :])
            ins.then_inc(wld.h, 16)
        return ins
    C.rec("gpsimd", wload_all, wait=list(bf_) + bfix + [ld])
    C.mark(wld, 16 * 38, "gpsimd")
    wA_tok = (wld, 16 * 8)
    w1_tok = (wld, 16 * 16)
    w2_tok = (wld, 16 * 38)

    C.sb_push()
    sxs2 = fm3(xs2_d)
    ab2 = [C.sbs(f"s2_a{i}", [128, 8, 257], F32) for i in range(2)]
    ob2 = [C.sbs(f"s2_o{i}", [128, 8, 256], F32) for i in range(2)]
    wl6 = [C.sem(f"s2_w{i}") for i in range(2)]
    st6 = [C.sem(f"s2_s{i}") for i in range(2)]
    tks6 = {}
    for ti, (t0, tn) in enumerate(mk_tiles(nt, 256)):
        par = ti % 2
        lw = list(bfix) if ti == 0 else []
        if ti >= 2:
            lw.append(tks6[ti - 2])
        C.rec("sync", lambda e, a=ab2[par], t0=t0, tn=tn:
              e.dma_start(out=a[:, :, 0:tn + 1], in_=sxa[:, :, 255 + t0:256 + t0 + tn]),
              wait=lw, inc=((wl6[par], 16),))

        def sh2(e, a=ab2[par], o=ob2[par], tn=tn):
            ins = None
            for c in range(8):
                src = a[:, c, 1:1 + tn] if c < 4 else a[:, c, 0:tn]
                ins = e.tensor_copy(o[:, c, 0:tn], src)
            return ins
        C.rec("vector", sh2,
              wait=[(wl6[par], wl6[par].n), (st6[par], 16 * (ti // 2))], nops=8)
        tks6[ti] = (VC, VC.n)
        C.rec("sync", lambda e, o=ob2[par], t0=t0, tn=tn:
              e.dma_start(out=sxs2[:, :, t0:t0 + tn], in_=o[:, :, 0:tn]),
              wait=[(VC, VC.n)], inc=((st6[par], 16),))
    bg = [(s, s.n) for s in st6]
    C.sb_pop()
    C.sb_push()
    bh = stage_rms(C, "l9n", xs2_d, nt, D, dst_bf_d=x2n_d, with_mean=True,
                   scale_sb=lng, bias_sb=lnb, wait=bg + [ld], tt=256)
    C.sb_pop()
    b1a = C.sb("b1a_s", [128, 22], F32)
    b1g = C.sb("b1g_s", [128, 22], F32)
    b2s = C.sb("b2s_s", [128, 8], F32)
    C.dma("sync", b1a[:, :], b1a_d, ld)
    C.dma("sync", b1g[:, :], b1g_d, ld)
    C.dma("sync", b2s[:, :], b2_d, ld)
    mt_a = [(i * 128, 128) for i in range(21)] + [(2688, 42)]
    C.sb_push()
    sx2n = fm3(x2n_d)
    sxo = fm3(xout_o)
    TGF = 512
    NGF = nt // TGF
    xgb = [C.sbs(f"ff_x{i}", [128, 8, TGF], BF16) for i in range(2)]
    hhs = C.sbs("ff_hh", [128, 22, TGF], BF16)
    ebg2 = [C.sbs(f"ff_g{i}", [128, TGF], F32) for i in range(2)]
    auxt = [C.sbs(f"ff_t{i}", [128, TGF], F32) for i in range(2)]
    psA = [C.pss(f"ff_pa{i}", [128, TGF]) for i in range(2)]
    psG = [C.pss(f"ff_pg{i}", [128, TGF]) for i in range(2)]
    psO = [C.pss(f"ff_po{i}", [128, TGF]) for i in range(2)]
    wxf = C.sem("ff_wx")
    waf = C.sem("ff_waf")
    taf = C.sem("ff_ta")
    stf = C.sem("ff_st")
    C.rec("vector", lambda e: e.memset(hhs[:, 21, :], 0.0),
          wait=list(bh) + list(bg), nops=1)
    vstt, stoks, vepis, xtoks, t_w1last, w2done = {}, {}, {}, {}, {}, {}
    gi = 0
    oi = 0
    for gei, g in enumerate([2, 3, 0, 1]):
        g0 = g * TGF
        lw = list(bh) + [ld] if gei == 0 else \
            ([t_w1last[gei - 2]] if gei >= 2 else [])
        C.rec("sync", lambda e, b_=xgb[gei % 2], g0=g0:
              e.dma_start(out=b_[:, :, :], in_=sx2n[:, :, g0:g0 + TGF]),
              wait=lw, inc=((wxf, 16),))
        xtoks[gei] = (wxf, wxf.n)
        for mi, (m0, msz) in enumerate(mt_a):
            bank = gi % 2
            wa_ = [xtoks[gei], wA_tok]
            if gi >= 2:
                wa_.append(vstt[gi - 2])

            def mm_a(t, bank=bank, m0=m0, msz=msz, xg_=xgb[gei % 2]):
                ins = None
                for k in range(8):
                    ins = t.matmul(psA[bank][0:msz, :], wA[:, k, m0:m0 + msz],
                                   xg_[:, k, :], start=(k == 0), stop=(k == 7))
                return ins
            C.rec("tensor", mm_a, wait=wa_, inc=((taf, 1),))

            def mm_g(t, bank=bank, m0=m0, msz=msz, xg_=xgb[gei % 2]):
                ins = None
                for k in range(8):
                    ins = t.matmul(psG[bank][0:msz, :], wG[:, k, m0:m0 + msz],
                                   xg_[:, k, :], start=(k == 0), stop=(k == 7))
                return ins
            wg_ = [w1_tok]
            if gi >= 2:
                wg_.append(stoks[gi - 2])
            C.rec("tensor", mm_g, wait=wg_, inc=((taf, 1),))
            tg_tok = (taf, taf.n)
            ch = m0 // 128
            sw = [tg_tok]
            if gi >= 2:
                sw.append(vstt[gi - 2])
            C.rec("scalar", lambda e, bank=bank, msz=msz, ch=ch:
                  e.activation(ebg2[bank][0:msz, :], psG[bank][0:msz, :],
                               AF.Gelu, bias=b1g[0:msz, ch:ch + 1]),
                  wait=sw, nops=1)
            stoks[gi] = (SC, SC.n)
            vw_ = [stoks[gi]]
            if mi == 0 and gei >= 1:
                vw_.append(w2done[gei - 1])

            def hhw(e, bank=bank, msz=msz, ch=ch, mi=mi):
                e.scalar_tensor_tensor(ebg2[bank][0:msz, :], psA[bank][0:msz, :],
                                       b1a[0:msz, ch:ch + 1],
                                       ebg2[bank][0:msz, :], OP.add, OP.mult)
                return e.tensor_copy(hhs[0:msz, mi, :], ebg2[bank][0:msz, :])
            C.rec("vector", hhw, wait=vw_, nops=2)
            vstt[gi] = (VC, VC.n)
            gi += 1
        t_w1last[gei] = (taf, taf.n)
        for o in range(8):
            bank2 = oi % 2
            alw = []
            if oi >= 2:
                alw.append(vepis[oi - 2])
            if oi == 0:
                alw += bfix + list(bf_)
            C.rec("sync", lambda e, b_=auxt[bank2], o=o, g0=g0:
                  e.dma_start(out=b_[:, :], in_=sxa[:, o, 256 + g0:256 + g0 + TGF]),
                  wait=alw, inc=((waf, 16),))
            atok = (waf, waf.n)
            ww_ = [w2_tok, vstt[gi - 1]]
            if oi >= 2:
                ww_.append(vepis[oi - 2])

            def mm_o(t, bank2=bank2, o=o):
                ins = None
                for m in range(22):
                    ins = t.matmul(psO[bank2][:, :],
                                   w2b[:, m, o * 128:(o + 1) * 128],
                                   hhs[:, m, :], start=(m == 0), stop=(m == 21))
                return ins
            C.rec("tensor", mm_o, wait=ww_, inc=((taf, 1),))
            to_tok = (taf, taf.n)
            C.rec("vector", lambda e, bank2=bank2, o=o:
                  e.scalar_tensor_tensor(auxt[bank2][:, :], psO[bank2][:, :],
                                         b2s[:, o:o + 1], auxt[bank2][:, :],
                                         OP.add, OP.add),
                  wait=[to_tok, atok], nops=1)
            vepis[oi] = (VC, VC.n)
            C.rec("sync", lambda e, b_=auxt[bank2], o=o, g0=g0:
                  e.dma_start(out=sxo[:, o, g0:g0 + TGF], in_=b_[:, :]),
                  wait=[vepis[oi]], inc=((stf, 16),))
            oi += 1
        w2done[gei] = (taf, taf.n)
    C.sb_pop()
    C.sb_pop()
    C.l9_half_tok = (stf, 16 * 16)
    bk = [(stf, stf.n), (wxf, wxf.n), (waf, waf.n)]
    C.rec("gpsimd", lambda e: e.nop(), wait=bk)
    return bk



def emit_launch10(C, nt):
    s_d = C.inp("s1", (D, nt), F32)
    g_d = C.inp("gff", (D, nt), F32)
    x_d = C.inp("xl1", (D, nt), F32)
    o_d = C.out("yout", (D, nt), F32)
    cn = make_consts(C)
    ss = fm3(s_d)
    sg = fm3(g_d)
    sx = fm3(x_d)
    sd = fm3(o_d)
    VC, SC = C.ch("vector"), C.ch("scalar")
    ab = [C.sb(f"f_a{i}", [128, 8, 512], F32) for i in range(2)]
    bb2 = [C.sb(f"f_b{i}", [128, 8, 512], F32) for i in range(2)]
    cb2 = [C.sb(f"f_c{i}", [128, 8, 512], F32) for i in range(2)]
    ob = [C.sb(f"f_o{i}", [128, 8, 512], F32) for i in range(2)]
    wl = [C.sem(f"f_w{i}") for i in range(2)]
    st = [C.sem(f"f_s{i}") for i in range(2)]
    tk = {}
    for ti, (t0, tn) in enumerate(mk_tiles(nt, 512)):
        par = ti % 2
        lw = [cn["ld"]] if ti == 0 else []
        if ti >= 2:
            lw.append(tk[ti - 2])
        for src, buf in ((ss, ab[par]), (sg, bb2[par]), (sx, cb2[par])):
            C.rec("sync", lambda e, src=src, buf=buf, t0=t0, tn=tn:
                  e.dma_start(out=buf[:, :, 0:tn], in_=src[:, :, t0:t0 + tn]),
                  wait=lw, inc=((wl[par], 16),))
            lw = []
        C.rec("scalar", lambda e, b_=bb2[par], tn=tn:
              e.activation(b_[:, :, 0:tn], b_[:, :, 0:tn], AF.Silu),
              wait=[(wl[par], wl[par].n), (st[par], 16 * (ti // 2))], nops=1)

        def ops(e, a=ab[par], b_=bb2[par], c=cb2[par], o=ob[par], tn=tn):
            e.tensor_tensor(o[:, :, 0:tn], a[:, :, 0:tn], b_[:, :, 0:tn], OP.mult)
            return e.tensor_tensor(o[:, :, 0:tn], o[:, :, 0:tn], c[:, :, 0:tn],
                                   OP.add)
        C.rec("vector", ops, wait=[(SC, SC.n)], nops=2)
        tk[ti] = (VC, VC.n)
        C.rec("sync", lambda e, o=ob[par], t0=t0, tn=tn:
              e.dma_start(out=sd[:, :, t0:t0 + tn], in_=o[:, :, 0:tn]),
              wait=[(VC, VC.n)], inc=((st[par], 16),))
    C.rec("gpsimd", lambda e: e.nop(), wait=[(s_, s_.n) for s_ in st])
    return [(s_, s_.n) for s_ in st]




def build_launch1(nt_out, halo=3):
    C = Ctx("l1")
    emit_launch1(C, nt_out, halo)
    C.emit()
    return C


def build_launch2(nt):
    C = Ctx("l2")
    emit_launch2(C, nt)
    C.emit()
    return C


def build_launch3(Tfull, NU=8):
    C = Ctx("l3")
    emit_launch3(C, Tfull, NU)
    C.emit()
    return C


def build_launch4(Tfull, NU2=4):
    C = Ctx("l4")
    emit_launch4(C, Tfull, NU2)
    C.emit()
    return C


def build_launch5(nt):
    C = Ctx("l5")
    emit_launch5(C, nt)
    C.emit()
    return C


def build_launch6(nt):
    C = Ctx("l6")
    emit_launch6(C, nt)
    C.emit()
    return C


def build_launch7(nt):
    C = Ctx("l7")
    emit_launch7(C, nt)
    C.emit()
    return C


def build_launch8(Tseg, nseg=2):
    C = Ctx("l8")
    emit_launch8(C, Tseg, nseg)
    C.emit()
    return C


def build_launch9(nt):
    C = Ctx("l9")
    emit_launch9(C, nt)
    C.emit()
    return C


def build_launch10(nt):
    C = Ctx("l10")
    emit_launch10(C, nt)
    C.emit()
    return C



_CACHE = {}
_DBG = {}
_STASH = False


def _get(builder, *args):
    key = (builder.__name__,) + args
    if key not in _CACHE:
        _CACHE[key] = builder(*args)
    return _CACHE[key]


def _halo_slice(full, b, s0, lo, hi):
    blk = full[:, b * T:(b + 1) * T]
    lo_i, hi_i = s0 + lo, s0 + hi
    out = np.zeros((full.shape[0], hi_i - lo_i), full.dtype)
    a = max(0, lo_i)
    bnd = min(T, hi_i)
    if bnd > a:
        out[:, a - lo_i:bnd - lo_i] = blk[:, a:bnd]
    return np.ascontiguousarray(out)


def _asm(results, key, rows):
    return np.concatenate([np.asarray(r[key], np.float32).reshape(rows, -1)
                           for r in results], axis=1)





G4 = [[0, 1, 2, 3], [4, 5, 6, 7]]
ALL8 = [[0, 1, 2, 3, 4, 5, 6, 7]]


def _fence(C, toks):
    toks = list(toks) + [(C.ch(e), C.ch(e).n) for e in Ctx.CHAINED]
    for eng in ENGINES:
        C.rec(eng, lambda e: None, wait=list(toks))


def emit_fused():
    nt, Tb = NT, T
    NTH = nt + 3
    NQK = nt + 512
    H9 = 514
    C = Ctx("mega")
    C.scoped_mode = True
    C.setup_cfg(32)
    cn = make_consts(C)
    dt_ = lambda nm, sh, ty=F32: C.nc.dram_tensor(nm, list(sh), ty,
                                                  kind="Internal")
    ein = lambda nm, sh, ty=F32: C.nc.dram_tensor(nm, list(sh), ty,
                                                  kind="ExternalInput")

    t_csq = ein("g_csq", (128, NQK)); t_ssq = ein("g_ssq", (128, NQK))
    t_csk = ein("g_csk", (128, NQK)); t_ssk = ein("g_ssk", (128, NQK))
    t_h0f = ein("g_h0flag", (128, 8))
    t_bm0 = ein("g_bmask0", (128, 256), BF16)
    t_bm1 = ein("g_bmask1", (128, 256), BF16)

    f_xn = dt_("f_xn", (D, NTH)); f_xnbf = dt_("f_xnbf", (D, NTH), BF16)
    f_zs = dt_("f_zs", (2048, NTH)); f_dt0 = dt_("f_dt0", (32, NTH))
    f_xbc = dt_("f_xbc", (2176, nt)); f_xbcbf = dt_("f_xbcbf", (2176, nt), BF16)
    f_dt0c = dt_("f_dt0c", (32, nt)); f_bcfm = dt_("f_bcfm", (128, nt), BF16)
    g_xbcg = dt_("g_xbcg", (4 * nt, 2176), BF16)
    g_dtg = dt_("g_dtg", (128, nt))
    g_bcfmg = dt_("g_bcfmg", (512, nt), BF16)
    f_y3 = dt_("f_y3", (512, Tb)); g_y3g = dt_("g_y3g", (2048, Tb))
    f_qkv = dt_("f_qkv", (1152, nt), BF16)
    g_qkvg = dt_("g_qkvg", (4 * 1152, nt), BF16)
    f_x2 = dt_("f_x2", (D, nt))
    f_y4 = dt_("f_y4", (256, Tb)); g_y4g = dt_("g_y4g", (1024, Tb))
    y4l = dt_("y4l", (D, nt))
    f_x3 = dt_("f_x3", (D, nt)); f_x3t = dt_("f_x3t", (2, 1024))
    g_x3tg = dt_("g_x3tg", (8, 1024)); f_x3h = dt_("f_x3h", (D, nt + 1))
    f_xl1 = dt_("f_xl1", (D, nt))
    f_xn1 = dt_("f_xn1", (D, nt)); f_dmy = dt_("f_dmy", (D, nt), BF16)
    f_dt1 = dt_("f_dt1", (D, nt)); f_bt = dt_("f_bt", (16, nt))
    f_ct = dt_("f_ct", (16, nt))
    g_xn1g = dt_("g_xn1g", (8 * D, nt)); g_dt1g = dt_("g_dt1g", (8 * D, nt))
    g_btg = dt_("g_btg", (8 * 16, nt)); g_ctg = dt_("g_ctg", (8 * 16, nt))
    x8u = dt_("x8u", (128, 8 * nt)); dt8 = dt_("dt8", (128, 8 * nt))
    bt16 = dt_("bt16", (16, 8 * nt)); ct16 = dt_("ct16", (16, 8 * nt))
    f_y1 = dt_("f_y1", (128, 8 * nt)); g_s1g = dt_("g_s1g", (D, 8 * nt))
    s1l = dt_("s1l", (D, nt))
    x9e = [dt_(f"x9e{i}", (D, nt + H9)) for i in range(6)]
    agsrc9 = dt_("agsrc9", (D, 2 * H9))
    g_x9tg = dt_("g_x9tg", (4 * D, 2 * H9))
    f_gff = dt_("f_gff", (D, nt))
    out_t = C.nc.dram_tensor("out", [D, nt], F32, kind="ExternalOutput")

    zs3 = bass.AP(f_zs, 3, [[NTH, 128], [128 * NTH, 16], [1, nt]])
    xn3 = bass.AP(f_xn, 3, [[NTH, 128], [128 * NTH, 8], [1, nt]])
    xnbf3 = bass.AP(f_xnbf, 3, [[NTH, 128], [128 * NTH, 8], [1, nt]])
    qkv_q = bass.AP(f_qkv, 0, [[nt, 128], [128 * nt, 8], [1, nt]])
    qkv_kv = bass.AP(f_qkv, 1024 * nt, [[nt, 128], [128 * nt, 1], [1, nt]])

    C.begin_section("l1_")
    C.io_override = dict(xn_f=f_xn.ap(), xn_bf=f_xnbf.ap(), z_silu=f_zs.ap(),
                         dt0=f_dt0.ap(), xbc=f_xbc.ap(), xbc_bf=f_xbcbf.ap())
    C.sb_push()
    r1 = emit_launch1(C, nt, 3, fused=True)
    l1bar = r1["bar"]
    cps = C.sem("cp1")
    C.rec("sync", lambda e: e.dma_start(out=f_dt0c.ap(),
                                        in_=f_dt0.ap()[:, 3:3 + nt]),
          wait=l1bar, inc=((cps, 16),))
    C.rec("sync", lambda e: e.dma_start(out=f_bcfm.ap(),
                                        in_=f_xbcbf.ap()[2048:2176, :]),
          wait=l1bar, inc=((cps, 16),))
    cc_x = C.coll("AllGather", r1["xbc_tok"], g_xbcg.ap(), G4, wait=l1bar)
    cc_d = C.coll("AllGather", f_dt0c.ap(), g_dtg.ap(), G4, wait=[(cps, 16)])
    cc_b = C.coll("AllGather", f_bcfm.ap(), g_bcfmg.ap(), G4, wait=[(cps, 32)])
    sec_toks = l1bar + [(cps, 32), cc_x, cc_d, cc_b]

    C.sb_pop()
    C.begin_section("l3_")
    C.io_override = {}
    _fence(C, sec_toks)
    C.sb_push()
    l3toks = emit_launch3(C, Tb, 8, fused=True,
                          fz=dict(xbcg=g_xbcg, bcfmg=g_bcfmg, dtfmg=g_dtg,
                                  y3=f_y3.ap(), CFG_U0=0, CFG_DT0=1,
                                  l3_entry=[], BLK=nt))
    cc_y3 = C.coll("AllGather", f_y3.ap(), g_y3g.ap(), G4, wait=l3toks)
    sec_toks = l3toks + [cc_y3]

    C.sb_pop()
    C.begin_section("l2_")
    C.io_override = dict(z_silu=zs3, xn_f=xn3, q_bf=qkv_q, kv_bf=qkv_kv,
                         x2=f_x2.ap())
    _fence(C, sec_toks)
    C.sb_push()
    l2toks = emit_launch2(C, nt, fused=True,
                          fz=dict(y3g=g_y3g, xbc=f_xbc.ap(), y3_bar=[],
                                  CFG_S0=2, TB=Tb))
    cc_qkv = C.coll("AllGather", f_qkv.ap(), g_qkvg.ap(), G4, wait=l2toks)
    sec_toks = l2toks + [cc_qkv]

    C.sb_pop()
    C.begin_section("l4_")
    C.io_override = {}
    _fence(C, sec_toks)
    C.sb_push()
    l4toks = emit_launch4(C, Tb, 4, fused=True,
                          fz=dict(qkvg=g_qkvg, y4=f_y4.ap(), CFG_Q=3,
                                  l4_entry=[], BLK=nt))
    cc_y4 = C.coll("AllGather", f_y4.ap(), g_y4g.ap(), G4, wait=l4toks)
    s_y4l = C.sem("s_y4l")
    C.dyn_dma(y4l.ap(), g_y4g, 4, [[Tb, 1024], [1, nt]], s_y4l,
              wait=[cc_y4], eng="sync")
    sec_toks = l4toks + [cc_y4, (s_y4l, 16)]

    C.sb_pop()
    C.begin_section("l5_")
    C.io_override = dict(y_mqa_f=y4l.ap(), x2=f_x2.ap(), x3=f_x3.ap())
    _fence(C, sec_toks)
    C.sb_push()
    l5toks = emit_launch5(C, nt)
    zsb = C.sb("z1col", [128, 8], F32)
    C.rec("vector", lambda e: e.memset(zsb[:, :], 0.0), nops=1)
    zt = (C.ch("vector"), C.ch("vector").n)
    cpt = C.sem("cpx3t")
    C.rec("sync", lambda e: e.dma_start(out=f_x3t.ap()[0:1, :], in_=zsb[:, :]),
          wait=[zt], inc=((cpt, 16),))
    C.rec("sync", lambda e: e.dma_start(
        out=f_x3t.ap()[1:2, :], in_=f_x3.ap()[:, nt - 1:nt],
        allow_slow_non_contiguous=True),
        wait=l5toks, inc=((cpt, 16),))
    cc_x3t = C.coll("AllGather", f_x3t.ap(), g_x3tg.ap(), G4, wait=[(cpt, 32)])
    s_x3h = C.sem("s_x3h")
    C.dyn_dma(f_x3h.ap()[:, 0:1], g_x3tg, 5, [[1, 1024]], s_x3h,
              wait=[cc_x3t], eng="sync", slow=True)
    C.rec("sync", lambda e: e.dma_start(out=f_x3h.ap()[:, 1:1 + nt],
                                        in_=f_x3.ap()),
          wait=l5toks, inc=((cpt, 16),))
    sec_toks = l5toks + [cc_x3t, (s_x3h, 16), (cpt, 48)]

    C.sb_pop()
    C.begin_section("l6_")
    C.io_override = dict(x3h=f_x3h.ap(), xn_bf_in=xnbf3, x_l1=f_xl1.ap())
    _fence(C, sec_toks)
    C.sb_push()
    l6toks = emit_launch6(C, nt)
    sec_toks = list(l6toks)

    C.sb_pop()
    C.begin_section("l7_")
    C.io_override = dict(x_l1_in=f_xl1.ap(), xn1_f=f_xn1.ap(),
                         xn1_bf=f_dmy.ap(), dt1=f_dt1.ap(), bt1=f_bt.ap(),
                         ct1=f_ct.ap())
    _fence(C, sec_toks)
    C.sb_push()
    l7toks = emit_launch7(C, nt)
    cc_xn1 = C.coll("AllGather", f_xn1.ap(), g_xn1g.ap(), ALL8, wait=l7toks)
    cc_dt1 = C.coll("AllGather", f_dt1.ap(), g_dt1g.ap(), ALL8, wait=l7toks)
    cc_bt = C.coll("AllGather", f_bt.ap(), g_btg.ap(), ALL8, wait=l7toks)
    cc_ct = C.coll("AllGather", f_ct.ap(), g_ctg.ap(), ALL8, wait=l7toks)
    s_rp8 = C.sem("s_rp8")
    s_rp8h = C.sem("s_rp8h")
    C.dyn_dma(x8u.ap(), g_xn1g, 6, [[nt, 128], [D * nt, 8], [1, nt]],
              s_rp8, wait=[cc_xn1])
    C.dyn_dma(dt8.ap(), g_dt1g, 6, [[nt, 128], [D * nt, 8], [1, nt]],
              s_rp8, wait=[cc_dt1])
    C.rec("sync", lambda e: e.dma_start(
        out=bt16.ap(), in_=bass.AP(g_btg, 0, [[nt, 16], [16 * nt, 8], [1, nt]])),
        wait=[cc_bt], inc=((s_rp8h, 16),))
    C.rec("sync", lambda e: e.dma_start(
        out=ct16.ap(), in_=bass.AP(g_ctg, 0, [[nt, 16], [16 * nt, 8], [1, nt]])),
        wait=[cc_ct], inc=((s_rp8h, 16),))
    C.rec("sync", lambda e: e.dma_start(out=x9e[0].ap()[:, H9:H9 + nt],
                                        in_=f_xn1.ap()),
          wait=l7toks, inc=((s_rp8h, 16),))
    zsb9 = C.sb("z9", [128, H9], F32)
    C.rec("vector", lambda e: e.memset(zsb9[:, :], 0.0), nops=1)
    zt9 = (C.ch("vector"), C.ch("vector").n)
    for ch in range(8):
        C.rec("sync", lambda e, ch=ch: e.dma_start(
            out=agsrc9.ap()[128 * ch:128 * (ch + 1), 0:H9], in_=zsb9[:, :]),
            wait=[zt9], inc=((s_rp8h, 16),))
    sec_toks = l7toks + [cc_xn1, cc_dt1, cc_bt, cc_ct, (s_rp8, s_rp8.n),
                         (s_rp8h, s_rp8h.n)]

    C.sb_pop()
    C.begin_section("l8_")
    C.io_override = dict(u_sl=x8u.ap(), dt_sl=dt8.ap(), bt_in=bt16.ap(),
                         ct_in=ct16.ap(), y1_sl=f_y1.ap())
    _fence(C, sec_toks)
    C.sb_push()
    l8toks = emit_launch8(C, Tb, 2)
    cc_s1 = C.coll("AllGather", f_y1.ap(), g_s1g.ap(), ALL8, wait=l8toks)
    s_s1l = C.sem("s_s1l")
    C.dyn_dma(s1l.ap(), g_s1g, 7, [[8 * nt, 1024], [1, nt]], s_s1l,
              wait=[cc_s1], eng="sync")
    sec_toks = l8toks + [cc_s1, (s_s1l, 16)]

    halo_tok = {}
    for li in range(6):
        C.sb_pop()
        C.begin_section(f"l9{li}_")
        ovr = dict(csq=t_csq.ap(), ssq=t_ssq.ap(), csk=t_csk.ap(),
                   ssk=t_ssk.ap(), h0flag=t_h0f.ap(), bmask0=t_bm0.ap(),
                   bmask1=t_bm1.ap(), x9=x9e[li].ap())
        if li < 5:
            ovr["x9o"] = bass.AP(x9e[li + 1], H9,
                                 [[nt + H9, 128], [128 * (nt + H9), 8], [1, nt]])
        else:
            ovr["x9o"] = f_gff.ap()
        C.io_override = ovr
        _fence(C, sec_toks)
        C.sb_push()
        if li == 0:
            s_t9 = C.sem("s_t9")
            C.rec("sync", lambda e: e.dma_start(
                out=agsrc9.ap()[:, H9:2 * H9], in_=x9e[0].ap()[:, nt:nt + H9]),
                wait=[], inc=((s_t9, 16),))
            cc9 = C.coll("AllGather", agsrc9.ap(), g_x9tg.ap(), G4,
                         wait=[(s_t9, 16)])
            s_p9 = C.sem("s_p9")
            C.dyn_dma(x9e[0].ap()[:, 0:H9], g_x9tg, 8,
                      [[2 * H9, 1024], [1, H9]], s_p9, wait=[cc9], eng="sync")
            _fence(C, [(s_p9, 16)])
        else:
            _fence(C, [halo_tok[li]])
        l9toks = emit_launch9(C, nt)
        if li < 5:
            s_t9b = C.sem("s_t9b")
            C.rec("gpsimd", lambda e, li=li: e.dma_start(
                out=agsrc9.ap()[:, H9:2 * H9],
                in_=x9e[li + 1].ap()[:, nt:nt + H9]),
                wait=[C.l9_half_tok], inc=((s_t9b, 16),))
            cc9b = C.coll("AllGather", agsrc9.ap(), g_x9tg.ap(), G4,
                          wait=[(s_t9b, 16)])
            s_p9b = C.sem("s_p9b")
            C.dyn_dma(x9e[li + 1].ap()[:, 0:H9], g_x9tg, 8,
                      [[2 * H9, 1024], [1, H9]], s_p9b, wait=[cc9b])
            halo_tok[li + 1] = (s_p9b, 16)
        sec_toks = list(l9toks)

    C.sb_pop()
    C.begin_section("l10_")
    C.io_override = dict(s1=s1l.ap(), gff=f_gff.ap(), xl1=f_xl1.ap(),
                         yout=out_t.ap())
    _fence(C, sec_toks + [cc_s1, (s_s1l, 16)])
    C.sb_push()
    emit_launch10(C, nt)
    C.sb_pop()
    C.io_override = {}
    C.emit()
    return C


def _kernel_fused(inp):
    f32 = np.float32
    g = lambda k: np.asarray(inp[k])
    x = g("L0_attn_x")
    x_fm = np.ascontiguousarray(x.reshape(B * T, D).T)
    rs = lambda v, k: np.ascontiguousarray(np.asarray(v, f32).reshape(k, 128).T)
    nt = NT
    H9 = 514
    NQK = nt + 512
    C = _get(emit_fused)

    half = np.arange(0, 64, 2, dtype=f32)
    inv_freq = 1.0 / 10000.0 ** (half / 64)
    base_scale = (half + 0.4 * 64) / (1.4 * 64)
    mt_a9 = [(i * 128, 128) for i in range(21)] + [(2688, 42)]
    b1as, b1gs = [], []
    for li in range(6):
        b1v = np.asarray(g("L1_ffn_b1")[li], f32)
        b1a = np.zeros((128, 22), f32)
        b1g = np.zeros((128, 22), f32)
        for i, (m0, ms) in enumerate(mt_a9):
            b1a[0:ms, i] = b1v[m0:m0 + ms]
            b1g[0:ms, i] = b1v[2730 + m0:2730 + m0 + ms]
        b1as.append(b1a)
        b1gs.append(b1g)

    provs = []
    for c in range(NCORES):
        b, j = c // 4, c % 4
        s0 = j * nt
        sl = slice(c * nt, (c + 1) * nt)
        h0_l3 = (8 * c) % 32
        pos = np.arange(s0 - 512, s0 + nt).astype(f32)
        fr = pos[:, None] * inv_freq[None, :]
        fr = np.concatenate([fr, fr], -1)
        power = (pos - T // 2) / 128.0
        sc = base_scale[None, :] ** power[:, None]
        sc = np.concatenate([sc, sc], -1)
        cfg = np.zeros((1, 32), np.int32)
        cfg[0, 0] = h0_l3 * 64
        cfg[0, 1] = h0_l3 * nt
        cfg[0, 2] = j * nt
        cfg[0, 3] = (4 * j) * 64 * nt
        cfg[0, 4] = j * nt
        cfg[0, 5] = 0 if j == 0 else ((j - 1) * 2 + 1) * 1024
        cfg[0, 6] = 128 * c * nt
        cfg[0, 7] = c * nt
        cfg[0, 8] = 0 if j == 0 else (j - 1) * 1024 * 2 * H9 + H9
        drep = np.repeat(np.asarray(g("L0_attn_D"), f32), 64).reshape(16, 128).T
        m = {
            "cfg": cfg,
            "l1_x": _halo_slice(x_fm, b, s0, -3, nt),
            "l1_w_inproj": g("L0_attn_in_proj_w"),
            "l1_dt_bias_r": g("L0_attn_dt_bias").reshape(32, 1),
            "l1_conv_w_r": g("L0_attn_conv_w").reshape(17, 128, 4),
            "l1_conv_b_r": np.ascontiguousarray(
                g("L0_attn_conv_b").reshape(17, 128).T),
            "l3_alog_r": np.ascontiguousarray(np.tile(
                g("L0_attn_A_log")[[(h0_l3 + i) % 32 for i in range(8)]],
                (128, 1)).astype(f32)),
            "l3_d_r": np.ascontiguousarray(np.tile(
                g("L0_attn_D")[[(h0_l3 + i) % 32 for i in range(8)]],
                (128, 1)).astype(f32)),
            "l2_mnorm_r": rs(g("L0_attn_mnorm_w"), 16),
            "l2_w_outproj": g("L0_attn_out_proj_w"),
            "l2_w_attn": g("L0_attn_attn_w"),
            "l2_w_attn_kv": np.ascontiguousarray(g("L0_attn_attn_w")[:, 1024:]),
            "l2_d_rep": np.ascontiguousarray(drep),
            "l5_w_proj": g("L0_attn_proj_w"),
            "l6_x_in": np.ascontiguousarray(x_fm[:, sl]),
            "l6_maa_k_r": rs(g("L0_attn_time_maa_k"), 8),
            "l6_maa_r_r": rs(g("L0_attn_time_maa_r"), 8),
            "l6_w_key": g("L0_attn_key_w"),
            "l6_w_recept": g("L0_attn_recept_w"),
            "l6_w_value": g("L0_attn_value_w"),
            "l6_w1T": np.ascontiguousarray(g("L0_ffn_w1_w").T),
            "l6_w2T": np.ascontiguousarray(g("L0_ffn_w2_w").T),
            "l6_h1T": np.ascontiguousarray(g("L0_ffn_h1_w").T),
            "l6_h2T": np.ascontiguousarray(g("L0_ffn_h2_w").T),
            "l6_w3T": np.ascontiguousarray(g("L0_ffn_w3_w").T),
            "l6_w1s_r": rs(g("L0_ffn_w1_s"), 32),
            "l6_w2s_r": rs(g("L0_ffn_w2_s"), 32),
            "l6_h1s_r": rs(g("L0_ffn_h1_s"), 32),
            "l6_h2s_r": rs(g("L0_ffn_h2_s"), 32),
            "l6_w3s_r": rs(g("L0_ffn_w3_s"), 8),
            "l7_w_xdtT": np.ascontiguousarray(g("L1_attn_W_xdt").T),
            "l7_b_xdt_r": g("L1_attn_b_xdt").reshape(64, 1),
            "l7_w_dtT": np.ascontiguousarray(g("L1_attn_W_dt").T),
            "l7_b_dt_r": rs(g("L1_attn_b_dt"), 8),
            "l7_w_bT": np.ascontiguousarray(g("L1_attn_W_B").T),
            "l7_b_b_r": g("L1_attn_b_B").reshape(16, 1),
            "l7_w_cT": np.ascontiguousarray(g("L1_attn_W_C").T),
            "l7_b_c_r": g("L1_attn_b_C").reshape(16, 1),
            "l8_a_sl": np.ascontiguousarray(
                g("L1_attn_A_log")[128 * c:128 * (c + 1), :]),
            "l8_d_sl": np.ascontiguousarray(
                g("L1_attn_D")[128 * c:128 * (c + 1)].reshape(128, 1)),
            "g_csq": np.ascontiguousarray(np.tile((np.cos(fr) * sc).T, (2, 1)).astype(f32)),
            "g_ssq": np.ascontiguousarray(np.tile((np.sin(fr) * sc).T, (2, 1)).astype(f32)),
            "g_csk": np.ascontiguousarray(np.tile((np.cos(fr) / sc).T, (2, 1)).astype(f32)),
            "g_ssk": np.ascontiguousarray(np.tile((np.sin(fr) / sc).T, (2, 1)).astype(f32)),
            "g_h0flag": np.full((128, 8), 0.0 if j == 0 else 1.0, f32),
            "g_bmask0": np.ascontiguousarray(
                (np.zeros((128, 256)) if j == 0 else
                 (np.arange(256)[None, :] <= np.arange(128)[:, None])
                 ).astype(ml_dtypes.bfloat16)),
            "g_bmask1": np.ascontiguousarray(
                (np.zeros((128, 256)) if j == 0 else
                 ((np.arange(256)[None, :] >= np.arange(128)[:, None] - 128)
                  & (np.arange(256)[None, :] <= np.arange(128)[:, None] + 128))
                 ).astype(ml_dtypes.bfloat16)),
        }
        for li in range(6):
            p = f"l9{li}_"
            m[p + "wqkv"] = np.ascontiguousarray(g("L1_ffn_Wqkv")[li])
            m[p + "wo"] = np.ascontiguousarray(g("L1_ffn_Wo")[li])
            m[p + "qs_r"] = np.ascontiguousarray(
                np.tile(np.tile(np.asarray(g("L1_ffn_q_scale")[li], f32), 2)
                        .reshape(128, 1), (1, 8)))
            m[p + "ks_r"] = np.ascontiguousarray(
                np.tile(np.tile(np.asarray(g("L1_ffn_k_scale")[li], f32), 2)
                        .reshape(128, 1), (1, 8)))
            m[p + "ln_g_r"] = rs(g("L1_ffn_ln_g")[li], 8)
            m[p + "ln_b_r"] = rs(g("L1_ffn_ln_b")[li], 8)
            m[p + "w1l"] = np.ascontiguousarray(g("L1_ffn_W1")[li])
            m[p + "b1a_r"] = b1as[li]
            m[p + "b1g_r"] = b1gs[li]
            m[p + "w2lT"] = np.ascontiguousarray(
                np.pad(np.asarray(g("L1_ffn_W2")[li], f32), ((0, 86), (0, 0))))
            m[p + "b2_r"] = rs(g("L1_ffn_b2")[li], 8)
        provs.append(m)
    res = C.run(extra_by_core=provs)
    out_fm = np.concatenate([np.asarray(r["out"], f32).reshape(D, nt)
                             for r in res], axis=1)
    return np.ascontiguousarray(out_fm.T.reshape(B, T, D)).astype(np.float32)


_FUSED = True


def kernel(**inp):
    if _FUSED:
        return _kernel_fused(inp)
    nt = NT
    f32 = np.float32
    g = lambda k: np.asarray(inp[k])
    x = g("L0_attn_x")
    x_fm = np.ascontiguousarray(x.reshape(B * T, D).T)

    rs = lambda v, k: np.ascontiguousarray(np.asarray(v, f32).reshape(k, 128).T)

    C1 = _get(build_launch1, nt, 3)
    provs = []
    for c in range(NCORES):
        b, s0 = c // 4, (c % 4) * nt
        provs.append({
            "x": _halo_slice(x_fm, b, s0, -3, nt),
            "w_inproj": g("L0_attn_in_proj_w"),
            "dt_bias_r": g("L0_attn_dt_bias").reshape(32, 1),
            "conv_w_r": g("L0_attn_conv_w").reshape(17, 128, 4),
            "conv_b_r": np.ascontiguousarray(g("L0_attn_conv_b").reshape(17, 128).T),
        })
    r1 = C1.run(extra_by_core=provs)
    xn_full = _asm(r1, "xn_f", D)[:, _drop3()]
    xnbf_full = np.concatenate(
        [np.asarray(r["xn_bf"]).reshape(D, -1)[:, 3:] for r in r1], axis=1)
    zs_full = _asm(r1, "z_silu", 2048)[:, _drop3()]
    dt_full = _asm(r1, "dt0", 32)[:, _drop3()]
    xbc_full = _asm(r1, "xbc", 2176)
    xbcbf_full = np.concatenate(
        [np.asarray(r["xbc_bf"]).reshape(2176, -1) for r in r1], axis=1)

    NCH = T // 128
    C3 = _get(build_launch3, T, 8)
    provs = []
    for c in range(NCORES):
        b = c // 4
        hs = [(8 * c + i) % 32 for i in range(8)]
        bsl = slice(b * T, (b + 1) * T)
        xs_bf = np.stack([xbcbf_full[64 * h:64 * h + 64, bsl].T
                          .reshape(NCH, 128, 64).transpose(1, 0, 2)
                          for h in hs], axis=1)
        xs_f = np.stack([xbc_full[64 * h:64 * h + 64, bsl].T
                         .reshape(NCH, 128, 64).transpose(1, 0, 2)
                         for h in hs], axis=1).astype(f32)
        btok = (xbcbf_full[2048:2112, bsl].T.reshape(NCH, 128, 64)
                .transpose(1, 0, 2))
        provs.append({
            "xs_bf": np.ascontiguousarray(xs_bf),
            "xs_f": np.ascontiguousarray(xs_f),
            "b_tok": np.ascontiguousarray(btok),
            "b_fm": np.ascontiguousarray(xbcbf_full[2048:2112, bsl]),
            "c_fm": np.ascontiguousarray(xbcbf_full[2112:2176, bsl]),
            "dt_tok": np.ascontiguousarray(
                np.stack([dt_full[h, bsl].reshape(NCH, 128).T for h in hs],
                         axis=1)),
            "alog_r": np.ascontiguousarray(
                np.tile(g("L0_attn_A_log")[hs], (128, 1)).astype(f32)),
            "d_r": np.ascontiguousarray(
                np.tile(g("L0_attn_D")[hs], (128, 1)).astype(f32)),
        })
    r3 = C3.run(extra_by_core=provs)
    y2_full = np.zeros((2048, B * T), f32)
    for c in range(NCORES):
        b = c // 4
        yt = np.asarray(r3[c]["y_tok"], f32).reshape(128, 8, NCH, 64)
        for i in range(8):
            h = (8 * c + i) % 32
            y2_full[64 * h:64 * h + 64, b * T:(b + 1) * T] = \
                yt[:, i, :, :].transpose(1, 0, 2).reshape(T, 64).T
    if _STASH: _DBG['y2'] = y2_full.copy()
    del r3

    C2 = _get(build_launch2, nt)
    provs = []
    for c in range(NCORES):
        sl = slice(c * nt, (c + 1) * nt)
        provs.append({
            "y": np.ascontiguousarray(y2_full[:, sl]),
            "z_silu": np.ascontiguousarray(zs_full[:, sl]),
            "xn_f": np.ascontiguousarray(xn_full[:, sl]),
            "mnorm_r": rs(g("L0_attn_mnorm_w"), 16),
            "w_outproj": g("L0_attn_out_proj_w"),
            "w_attn": g("L0_attn_attn_w"),
            "w_attn_kv": np.ascontiguousarray(g("L0_attn_attn_w")[:, 1024:]),
        })
    r2 = C2.run(extra_by_core=provs)
    x2_full = _asm(r2, "x2", D)
    qbf_full = np.concatenate([np.asarray(r["q_bf"]).reshape(D, -1) for r in r2],
                              axis=1)
    kvbf_full = np.concatenate([np.asarray(r["kv_bf"]).reshape(128, -1)
                                for r in r2], axis=1)
    if _STASH: _DBG.update(x2=x2_full, q=qbf_full, kv=kvbf_full)
    del r2, y2_full

    NST = T // 128
    C4 = _get(build_launch4, T, 4)
    provs = []
    for c in range(NCORES):
        b = c // 4
        bsl = slice(b * T, (b + 1) * T)
        hs = [4 * (c % 4) + i for i in range(4)]
        vtok = kvbf_full[64:128, bsl].T
        vo = np.concatenate([vtok, np.ones((T, 1), vtok.dtype)], 1)
        provs.append({
            "q_in": np.ascontiguousarray(
                np.stack([qbf_full[64 * h:64 * h + 64, bsl] for h in hs], axis=1)),
            "k_in": np.ascontiguousarray(kvbf_full[0:64, bsl]),
            "vo_tok": np.ascontiguousarray(
                vo.reshape(NST, 128, 65).transpose(1, 0, 2)),
        })
    r4 = C4.run(extra_by_core=provs)
    ymqa_full = np.zeros((D, B * T), f32)
    for c in range(NCORES):
        b = c // 4
        ym = np.asarray(r4[c]["y_mqa"], f32).reshape(64, 4, T)
        for i in range(4):
            h = 4 * (c % 4) + i
            ymqa_full[64 * h:64 * h + 64, b * T:(b + 1) * T] = ym[:, i, :]
    if _STASH: _DBG['ymqa'] = ymqa_full.copy()
    del r4

    C5 = _get(build_launch5, nt)
    provs = []
    for c in range(NCORES):
        sl = slice(c * nt, (c + 1) * nt)
        provs.append({"y_mqa_f": np.ascontiguousarray(ymqa_full[:, sl]),
                      "x2": np.ascontiguousarray(x2_full[:, sl]),
                      "w_proj": g("L0_attn_proj_w")})
    r5 = C5.run(extra_by_core=provs)
    x3_full = _asm(r5, "x3", D)
    if _STASH: _DBG['x3'] = x3_full.copy()
    del r5, ymqa_full

    C6 = _get(build_launch6, nt)
    provs = []
    for c in range(NCORES):
        b, s0 = c // 4, (c % 4) * nt
        sl = slice(c * nt, (c + 1) * nt)
        provs.append({
            "x3h": _halo_slice(x3_full, b, s0, -1, nt),
            "x_in": np.ascontiguousarray(x_fm[:, sl]),
            "xn_bf_in": np.ascontiguousarray(xnbf_full[:, sl]),
            "maa_k_r": rs(g("L0_attn_time_maa_k"), 8),
            "maa_r_r": rs(g("L0_attn_time_maa_r"), 8),
            "w_key": g("L0_attn_key_w"),
            "w_recept": g("L0_attn_recept_w"),
            "w_value": g("L0_attn_value_w"),
            "w1T": np.ascontiguousarray(g("L0_ffn_w1_w").T),
            "w2T": np.ascontiguousarray(g("L0_ffn_w2_w").T),
            "h1T": np.ascontiguousarray(g("L0_ffn_h1_w").T),
            "h2T": np.ascontiguousarray(g("L0_ffn_h2_w").T),
            "w3T": np.ascontiguousarray(g("L0_ffn_w3_w").T),
            "w1s_r": rs(g("L0_ffn_w1_s"), 32),
            "w2s_r": rs(g("L0_ffn_w2_s"), 32),
            "h1s_r": rs(g("L0_ffn_h1_s"), 32),
            "h2s_r": rs(g("L0_ffn_h2_s"), 32),
            "w3s_r": rs(g("L0_ffn_w3_s"), 8),
        })
    r6 = C6.run(extra_by_core=provs)
    xl1_full = _asm(r6, "x_l1", D)
    if _STASH: _DBG['xl1'] = xl1_full.copy()
    del r6, x3_full

    C7 = _get(build_launch7, nt)
    provs = []
    for c in range(NCORES):
        sl = slice(c * nt, (c + 1) * nt)
        provs.append({
            "x_l1_in": np.ascontiguousarray(xl1_full[:, sl]),
            "w_xdtT": np.ascontiguousarray(g("L1_attn_W_xdt").T),
            "b_xdt_r": g("L1_attn_b_xdt").reshape(64, 1),
            "w_dtT": np.ascontiguousarray(g("L1_attn_W_dt").T),
            "b_dt_r": rs(g("L1_attn_b_dt"), 8),
            "w_bT": np.ascontiguousarray(g("L1_attn_W_B").T),
            "b_b_r": g("L1_attn_b_B").reshape(16, 1),
            "w_cT": np.ascontiguousarray(g("L1_attn_W_C").T),
            "b_c_r": g("L1_attn_b_C").reshape(16, 1),
        })
    r7 = C7.run(extra_by_core=provs)
    xn1_full = _asm(r7, "xn1_f", D)
    dt1_full = _asm(r7, "dt1", D)
    bt_full = _asm(r7, "bt1", 16)
    ct_full = _asm(r7, "ct1", 16)
    if _STASH: _DBG.update(xn1=xn1_full, dt1=dt1_full, bt=bt_full, ct=ct_full)
    del r7

    C8 = _get(build_launch8, T, 2)
    provs = []
    for c in range(NCORES):
        rsl = slice(128 * c, 128 * (c + 1))
        provs.append({
            "u_sl": np.ascontiguousarray(xn1_full[rsl, :]),
            "dt_sl": np.ascontiguousarray(dt1_full[rsl, :]),
            "a_sl": np.ascontiguousarray(g("L1_attn_A_log")[rsl, :]),
            "d_sl": np.ascontiguousarray(g("L1_attn_D")[rsl].reshape(128, 1)),
            "bt_in": bt_full, "ct_in": ct_full,
        })
    r8 = C8.run(extra_by_core=provs)
    s1_full = np.concatenate([np.asarray(r8[c]["y1_sl"], f32).reshape(128, -1)
                              for c in range(NCORES)], axis=0)
    if _STASH: _DBG['s1'] = s1_full.copy()
    del r8, dt1_full

    H9 = 514
    C9 = _get(build_launch9, nt)
    half = np.arange(0, 64, 2, dtype=f32)
    inv_freq = 1.0 / 10000.0 ** (half / 64)
    base_scale = (half + 0.4 * 64) / (1.4 * 64)
    tabs = []
    for c in range(NCORES):
        s0 = (c % 4) * nt
        pos = np.arange(s0 - 512, s0 + nt).astype(f32)
        fr = pos[:, None] * inv_freq[None, :]
        fr = np.concatenate([fr, fr], -1)
        power = (pos - T // 2) / 128.0
        sc = base_scale[None, :] ** power[:, None]
        sc = np.concatenate([sc, sc], -1)
        tabs.append((np.tile((np.cos(fr) * sc).T, (2, 1)).astype(f32),
                     np.tile((np.sin(fr) * sc).T, (2, 1)).astype(f32),
                     np.tile((np.cos(fr) / sc).T, (2, 1)).astype(f32),
                     np.tile((np.sin(fr) / sc).T, (2, 1)).astype(f32)))
    mt_a9 = [(i * 128, 128) for i in range(21)] + [(2688, 42)]
    x_tr = xn1_full
    for li in range(6):
        b1v = np.asarray(g("L1_ffn_b1")[li], f32)
        b1a = np.zeros((128, 22), f32)
        b1g = np.zeros((128, 22), f32)
        for i, (m0, ms) in enumerate(mt_a9):
            b1a[0:ms, i] = b1v[m0:m0 + ms]
            b1g[0:ms, i] = b1v[2730 + m0:2730 + m0 + ms]
        provs = []
        for c in range(NCORES):
            b, s0 = c // 4, (c % 4) * nt
            csq, ssq, csk, ssk = tabs[c]
            provs.append({
                "x9": _halo_slice(x_tr, b, s0, -H9, nt),
                "wqkv": np.ascontiguousarray(g("L1_ffn_Wqkv")[li]),
                "wo": np.ascontiguousarray(g("L1_ffn_Wo")[li]),
                "qs_r": np.ascontiguousarray(
                    np.tile(np.tile(np.asarray(g("L1_ffn_q_scale")[li], f32), 2)
                            .reshape(128, 1), (1, 8))),
                "ks_r": np.ascontiguousarray(
                    np.tile(np.tile(np.asarray(g("L1_ffn_k_scale")[li], f32), 2)
                            .reshape(128, 1), (1, 8))),
                "csq": csq, "ssq": ssq, "csk": csk, "ssk": ssk,
                "h0flag": np.full((128, 8), 0.0 if c % 4 == 0 else 1.0, f32),
                "bmask0": (np.zeros((128, 256)) if c % 4 == 0 else
                           (np.arange(256)[None, :] <= np.arange(128)[:, None])
                           ).astype(ml_dtypes.bfloat16),
                "bmask1": (np.zeros((128, 256)) if c % 4 == 0 else
                           ((np.arange(256)[None, :] >= np.arange(128)[:, None] - 128)
                            & (np.arange(256)[None, :] <= np.arange(128)[:, None] + 128))
                           ).astype(ml_dtypes.bfloat16),
                "ln_g_r": rs(g("L1_ffn_ln_g")[li], 8),
                "ln_b_r": rs(g("L1_ffn_ln_b")[li], 8),
                "w1l": np.ascontiguousarray(g("L1_ffn_W1")[li]),
                "b1a_r": b1a, "b1g_r": b1g,
                "w2lT": np.ascontiguousarray(
                    np.pad(np.asarray(g("L1_ffn_W2")[li], f32), ((0, 86), (0, 0)))),
                "b2_r": rs(g("L1_ffn_b2")[li], 8),
            })
        r9 = C9.run(extra_by_core=provs)
        x_tr = _asm(r9, "x9o", D)
        if _STASH: _DBG[f'tr{li}'] = x_tr.copy()
        del r9

    C10 = _get(build_launch10, nt)
    provs = []
    for c in range(NCORES):
        sl = slice(c * nt, (c + 1) * nt)
        provs.append({"s1": np.ascontiguousarray(s1_full[:, sl]),
                      "gff": np.ascontiguousarray(x_tr[:, sl]),
                      "xl1": np.ascontiguousarray(xl1_full[:, sl])})
    r10 = C10.run(extra_by_core=provs)
    out_fm = _asm(r10, "yout", D)
    if _STASH: _DBG.update(xn=xn_full, xbc=xbc_full, xbcbf=xbcbf_full, zs=zs_full, dt0=dt_full, out=out_fm)
    return np.ascontiguousarray(out_fm.T.reshape(B, T, D)).astype(np.float32)


def _drop3():
    w = NT + 3
    keep = []
    for c in range(NCORES):
        keep.extend(range(c * w + 3, (c + 1) * w))
    return np.array(keep)


# version 13 (final)
# speedup vs baseline: 1.0363x; 1.0363x over previous
import contextlib
import numpy as np
import ml_dtypes
import concourse.bass as bass
import concourse.mybir as mybir
from concourse.bass_utils import run_bass_kernel_spmd

F32 = mybir.dt.float32
BF16 = mybir.dt.bfloat16
AF = mybir.ActivationFunctionType
OP = mybir.AluOpType
AX = mybir.AxisListType
BF = ml_dtypes.bfloat16

NCORES = 8
B, T, D = 2, 8192, 1024
NT = (B * T) // NCORES
ENGINES = ("sync", "tensor", "vector", "scalar", "gpsimd")


def cdiv(a, b):
    return (a + b - 1) // b


def fm3(d, p=128):
    if len(d.ap) == 3:
        return d
    return d.rearrange("(c p) t -> p c t", p=p)


def mk_tiles(n, tt):
    out = []
    t0 = 0
    while t0 < n:
        rem = n - t0
        if rem <= tt:
            out.append((t0, rem))
            break
        if rem < tt + 16:
            h = rem // 2
            out.append((t0, h))
            out.append((t0 + h, rem - h))
            break
        out.append((t0, tt))
        t0 += tt
    return out


class Sem:
    __slots__ = ("n", "cls", "sec", "inc_secs", "h", "base", "nm")

    def __init__(self, sec, nm=""):
        self.n = 0
        self.cls = None
        self.sec = sec
        self.inc_secs = set()
        self.h = None
        self.base = 0
        self.nm = nm


class Ctx:

    def __init__(self, name="k"):
        self.nc = bass.Bass("TRN2", target_bir_lowering=False)
        self.es = contextlib.ExitStack()
        self.inputs = {}
        self._sems = {}
        self.items = {e: [] for e in ENGINES}
        self._uid = 0
        self.pre = ""
        self.io_override = {}
        self.sec_idx = 0
        self.all_sems = []
        self.scoped_mode = False

    def begin_section(self, pre):
        self.pre = pre
        self.sec_idx += 1

    def uid(self):
        self._uid += 1
        return self._uid

    def sem(self, name):
        if not name.startswith("chain_"):
            name = self.pre + name
        if name not in self._sems:
            s = Sem(self.sec_idx, name)
            self._sems[name] = s
            self.all_sems.append(s)
        return self._sems[name]

    def inp(self, name, shape, dtype, provider=None):
        if name in self.io_override:
            return self.io_override[name]
        name = self.pre + name
        t = self.nc.dram_tensor(name, list(shape), dtype, kind="ExternalInput")
        self.inputs[name] = provider
        return t.ap()

    def scratch(self, name, shape, dtype):
        if name in self.io_override:
            return self.io_override[name]
        return self.nc.dram_tensor(self.pre + name, list(shape), dtype,
                                   kind="Internal").ap()

    def out(self, name, shape, dtype):
        if name in self.io_override:
            return self.io_override[name]
        return self.nc.dram_tensor(self.pre + name, list(shape), dtype,
                                   kind="ExternalOutput").ap()

    def const(self, name, arr):
        return self.nc.inline_tensor(np.ascontiguousarray(arr),
                                     name=self.pre + name).ap()

    def sb(self, name, shape, dtype):
        if self.scoped_mode and getattr(self, "_scopes", None):
            return self.sbs(name, shape, dtype)
        return self.nc.alloc_sbuf_tensor(self.pre + name, list(shape), dtype)

    def sb_push(self):
        if not hasattr(self, "_scopes"):
            self._scopes = []
        self._scopes.append(contextlib.ExitStack())

    def sbs(self, name, shape, dtype):
        return self._scopes[-1].enter_context(
            self.nc.sbuf_tensor(self.pre + name, list(shape), dtype))

    def sb_pop(self):
        self._scopes.pop().close()

    def pss(self, name, shape, dtype=F32):
        return self._scopes[-1].enter_context(
            self.nc.psum_tensor(self.pre + name, list(shape), dtype))

    def ps(self, name, shape, dtype=F32):
        if self.scoped_mode and getattr(self, "_scopes", None):
            return self.pss(name, shape, dtype)
        return self.nc.alloc_psum_tensor(self.pre + name, list(shape), dtype)

    CHAINED = ("vector", "scalar", "gpsimd")

    def ch(self, eng):
        return self.sem("chain_" + eng)

    def rec(self, eng, fn, wait=(), inc=(), nops=None):
        if eng in self.CHAINED:
            if nops is None:
                nops = 0
            assert nops == 0 or not inc, (eng, nops, inc)
            ch = self.ch(eng)
            ch.n += nops
            if nops:
                ch.cls = "CP"
                ch.inc_secs.add(self.sec_idx)
        ws = []
        for w in wait:
            if isinstance(w, tuple):
                ws.append((w[0], w[1]))
            else:
                ws.append((w, w.n))
        incs = []
        for s, k in inc:
            s.n += k
            cls = ("SW" if (eng == "gpsimd" and k == 16) else
                   "HW" if k == 16 else "CP")
            if s.cls is None:
                s.cls = cls
            assert s.cls == cls, f"sem {s.nm} class {s.cls} vs {cls}"
            s.inc_secs.add(self.sec_idx)
            incs.append((s, k))
        exp_chain = self.ch(eng).n if eng in self.CHAINED else None
        self.items[eng].append((ws, fn, incs, exp_chain))

    def mark(self, s, k, eng):
        cls = ("SW" if (eng == "gpsimd" and k >= 16) else
               "HW" if k >= 16 else "CP")
        if s.cls is None:
            s.cls = cls
        assert s.cls == cls, (s.nm, s.cls, cls)
        s.inc_secs.add(self.sec_idx)
        s.n += k

    def dma(self, eng, out, in_, sem, k=16):
        self.rec(eng, lambda e: e.dma_start(out=out, in_=in_), inc=((sem, k),))

    def setup_cfg(self, ncfg):
        saved = self.pre
        self.pre = ""
        cfg_t = self.nc.dram_tensor("cfg", [1, ncfg], mybir.dt.int32,
                                    kind="ExternalInput")
        self.inputs["cfg"] = None
        self.cfg_sb = self.sb("cfg_sb", [1, ncfg], mybir.dt.int32)
        s = self.sem("cfgld")
        self.rec("gpsimd", lambda e: e.dma_start(out=self.cfg_sb[:, :],
                                                 in_=cfg_t.ap()),
                 inc=((s, 16),))
        self.cfg_tok = (s, 16)
        self.pre = saved

    def dyn_dma(self, dst_ap, src_t, cfg_idx, pattern, sem, wait=(), addc=0,
                eng="gpsimd", slow=False):
        if not hasattr(self, "_dyn_engines"):
            self._dyn_engines = set()
        self._dyn_engines.add(eng)
        def fn(e):
            r = self.dynregs[eng]
            e.reg_load(r, self.cfg_sb[0:1, cfg_idx:cfg_idx + 1])
            if addc:
                e.reg_add(r, r, addc)
            return e.dma_start(out=dst_ap, in_=bass.AP(src_t, r, pattern),
                               allow_slow_non_contiguous=slow)
        self.rec(eng, fn, wait=list(wait) + [self.cfg_tok],
                 inc=((sem, 16),))

    def dyn_dma_out(self, dst_t, cfg_idx, pattern, src_ap, sem, wait=(), addc=0,
                    eng="gpsimd"):
        if not hasattr(self, "_dyn_engines"):
            self._dyn_engines = set()
        self._dyn_engines.add(eng)
        def fn(e):
            r = self.dynregs[eng]
            e.reg_load(r, self.cfg_sb[0:1, cfg_idx:cfg_idx + 1])
            if addc:
                e.reg_add(r, r, addc)
            return e.dma_start(out=bass.AP(dst_t, r, pattern), in_=src_ap)
        self.rec(eng, fn, wait=list(wait) + [self.cfg_tok],
                 inc=((sem, 16),))

    def coll(self, kind, src_ap, dst_ap, groups, wait=()):
        s = self.sem(f"cc{self.uid()}")
        self.rec("gpsimd", lambda e: e.collective_compute(
            kind, mybir.AluOpType.bypass, replica_groups=groups,
            ins=[src_ap], outs=[dst_ap]), wait=wait, inc=((s, 1),))
        s.cls = "CC"
        return (s, 1)

    def barrier_val(self, sem):
        return (sem, sem.n)

    CHAIN_SET = frozenset((
        "tensor_tensor", "tensor_scalar", "tensor_copy", "tensor_add",
        "tensor_sub", "tensor_mul", "tensor_max", "scalar_tensor_tensor",
        "tensor_tensor_scan", "tensor_reduce", "reduce_max", "reduce_sum",
        "reciprocal", "select", "iota", "memset", "memzero", "activation",
        "square", "sqrt", "mul", "add", "copy", "sign", "transpose",
        "tensor_single_scalar", "tensor_scalar_add", "tensor_scalar_sub",
        "tensor_scalar_mul", "tensor_scalar_max", "tensor_scalar_min",
        "tensor_relu", "bn_stats", "bn_aggr", "stream_shuffle",
        "partition_broadcast", "affine_select", "copy_predicated",
    ))

    def _assign_sems(self):
        by_sec = {}
        for s in self.all_sems:
            if s.cls is None:
                continue
            dedicated = (s.sec == 0) or (len(s.inc_secs - {s.sec}) > 0)
            if dedicated:
                s.h = self.es.enter_context(self.nc.semaphore("d_" + s.nm))
                s.base = 0
            else:
                by_sec.setdefault(s.sec, []).append(s)
        pools = {"SW": [], "HW": [], "CP": [], "CC": []}
        nsl = [0]
        for sec in sorted(by_sec):
            used = set()
            for s in by_sec[sec]:
                pool = pools[s.cls]
                slot = next((p for p in pool if id(p) not in used), None)
                if slot is None:
                    slot = {"h": self.es.enter_context(
                        self.nc.semaphore(f"g{nsl[0]}_{s.cls}")), "acc": 0}
                    nsl[0] += 1
                    pool.append(slot)
                used.add(id(slot))
                s.h = slot["h"]
                s.base = slot["acc"]
                slot["acc"] += s.n

    def emit(self):
        self._assign_sems()
        chain_set = self.CHAIN_SET

        class ChainEng:

            def __init__(self, e, sem):
                self._e = e
                self._sem = sem
                self._n = 0

            def __getattr__(self, name):
                f = getattr(self._e, name)
                if name in chain_set and callable(f):
                    def g(*a, **k):
                        ins = f(*a, **k)
                        if self._n > 0:
                            ins._wait_ge(self._sem, self._n)
                        ins.then_inc(self._sem, 1)
                        self._n += 1
                        return ins
                    return g
                return f

        with self.nc.Block() as block:
            for eng in ENGINES:
                items = self.items[eng]
                if not items:
                    continue

                def body(e, items=items, eng=eng):
                    ce = None
                    if eng in self.CHAINED:
                        chs = self.ch(eng)
                        if chs.h is None:
                            chs.h = self.es.enter_context(
                                self.nc.semaphore("d_" + chs.nm))
                        ce = ChainEng(e, chs.h)
                    if eng in getattr(self, "_dyn_engines", ()):
                        if not hasattr(self, "dynregs"):
                            self.dynregs = {}
                        self.dynregs[eng] = self.es.enter_context(
                            e.register("dynr_" + eng))
                    for ws, fn, incs, exp_chain in items:
                        for s, v in ws:
                            if v > 0 and s.h is not None:
                                e.wait_ge(s.h, s.base + v)
                        ins = fn(ce if ce is not None else e)
                        for s, k in incs:
                            ins.then_inc(s.h, k)
                        if exp_chain is not None:
                            assert ce._n == exp_chain, \
                                f"{eng}: nops mismatch got {ce._n} want {exp_chain}"

                getattr(block, eng)(body)

    def finish_sem(self):
        return self.sem("st_done")

    def run(self, extra_by_core=None, n_cores=NCORES):
        in_maps = []
        for c in range(n_cores):
            m = {k: np.ascontiguousarray(v(c)) for k, v in self.inputs.items()
                 if v is not None}
            if extra_by_core:
                m.update(extra_by_core[c])
            in_maps.append(m)
        res = run_bass_kernel_spmd(self.nc, in_maps, core_ids=list(range(n_cores)))
        return res.results



def load_weight_bf(C, name, dram_ap, din, dout, ld):
    nk = din // 128
    w = C.sb(name, [128, nk, dout], BF16)
    src = dram_ap.rearrange("(c p) m -> p c m", p=128)
    for c in range(nk):
        C.dma("gpsimd", w[:, c, :], src[:, c, :], ld)
    return w


def make_consts(C):
    if getattr(C, "consts", None) is not None:
        return C.consts
    saved_pre = C.pre
    C.pre = ""
    cvals = np.zeros((128, 16), np.float32)
    cvals[:, 0] = 1.0
    cvals[:, 1] = 1.0 / 1024.0
    cvals[:, 2] = 1e-5
    cvals[:, 3] = 0.125
    cvals[:, 4] = 0.5
    cvals[:, 5] = 1.0 / 2048.0
    cvals[:, 6] = 8.0
    cvals[:, 7] = -1.0
    cvals[:, 8] = np.sqrt(2.0 * np.pi)
    cvals[:, 9] = -np.sqrt(0.5) * np.sqrt(2.0 * np.pi)
    cvals[:, 10] = 1.0 / 64.0
    cvals[:, 11] = 1e-12
    cst_d = C.const("cst_d", cvals)
    ones_c_d = C.const("ones_c_d", np.ones((128, 1), np.float32))
    ones_r_d = C.const("ones_r_d", np.ones((1, 128), np.float32))
    cst = C.sb("cst", [128, 16], F32)
    ones_c = C.sb("ones_c", [128, 1], F32)
    ones_r = C.sb("ones_r", [1, 128], F32)
    ld = C.sem("cld")
    C.dma("sync", cst[:, :], cst_d, ld)
    C.dma("sync", ones_c[:, :], ones_c_d, ld)
    C.dma("sync", ones_r[:, :], ones_r_d, ld)
    C.consts = dict(cst=cst, ones_c=ones_c, ones_r=ones_r, ld=ld)
    C.pre = saved_pre
    return C.consts





def emit_launch1(C, nt_out, halo=3, fused=False):
    ntok = nt_out + halo
    DIN, CONVD = 2048, 2176
    MOUT = 4256
    x_d = C.inp("x", (D, ntok), F32)
    w_d = C.inp("w_inproj", (D, MOUT), F32)
    dtb_d = C.inp("dt_bias_r", (32, 1), F32)
    convw_d = C.inp("conv_w_r", (17, 128, 4), F32)
    convb_d = C.inp("conv_b_r", (128, 17), F32)

    xn_o = C.out("xn_f", (D, ntok), F32)
    xnbf_o = C.out("xn_bf", (D, ntok), BF16)
    zs_o = C.out("z_silu", (DIN, ntok), F32)
    dt_o = C.out("dt0", (32, ntok), F32)
    xbc_o = C.out("xbc", (CONVD, nt_out), F32)
    xbcbf_o = C.out("xbc_bf", (CONVD, nt_out), BF16)
    xbcpre_d = C.scratch("xbc_pre", (CONVD, ntok), F32)

    cn = make_consts(C)
    ld = C.sem("ld")
    ldw = C.sem("ldw")
    w_sb = load_weight_bf(C, "w_sb", w_d, D, MOUT, ldw)
    dtb = C.sb("dtb", [32, 1], F32)
    convw = C.sb("convw", [128, 17, 4], F32)
    convb = C.sb("convb", [128, 17], F32)
    C.dma("sync", dtb[:, :], dtb_d, ld)
    C.dma("sync", convw[:, :, :], convw_d.rearrange("c p k -> p c k"), ld)
    C.dma("sync", convb[:, :], convb_d, ld)

    VC, SC, GC = C.ch("vector"), C.ch("scalar"), C.ch("gpsimd")
    tt = 512
    ntt = cdiv(ntok, tt)
    xsrc = fm3(x_d)
    xnf_dst = fm3(xn_o)
    xnbf_dst = fm3(xnbf_o)
    zs_dst = fm3(zs_o)
    xbcpre_dst = fm3(xbcpre_d)

    C.sb_push()
    xbuf = [C.sbs(f"xb{i}", [128, 8, tt], F32) for i in range(2)]
    xsq = C.sbs("xsq", [128, 8, tt], F32)
    xnf2 = [C.sbs(f"xnf{i}", [128, 8, tt], F32) for i in range(2)]
    xnb = [C.sbs(f"xnb{i}", [128, 8, tt], BF16) for i in range(2)]
    tmp = C.sbs("tmp", [128, tt], F32)
    rows = C.sbs("rows", [1, 4 * tt], F32)
    ebuf = [C.sbs(f"eb{i}", [128, tt], F32) for i in range(3)]
    ebbf = C.sbs("ebbf", [32, tt], F32)

    ps_s = C.ps("ps_s", [1, tt])
    ps_q = C.ps("ps_q", [1, tt])
    ps_mb = C.ps("ps_mb", [128, tt])
    ps_rb = C.ps("ps_rb", [128, tt])
    NB = 3
    psm = [C.ps(f"psm{i}", [128, tt]) for i in range(NB)]

    wxs = [C.sem("wx0"), C.sem("wx1")]
    sxns = [C.sem("sxn0"), C.sem("sxn1")]
    sxgs = [C.sem(f"sxg{i}") for i in range(3)]
    tstat = C.sem("tstat")
    tbc = C.sem("tbc")
    te = C.sem("te")

    mtiles = [(i * 128, 128, "z") for i in range(16)]
    mtiles += [(2048 + i * 128, 128, "xbc") for i in range(17)]
    mtiles += [(4224, 32, "dt")]
    NJ = len(mtiles)

    tk = {}
    for ti, (t0, tn) in enumerate(mk_tiles(ntok, tt)):
        par = ti % 2
        wx = wxs[par]
        xb = xbuf[par]
        xnbf = xnb[par]
        lw = [ld, ldw, cn["ld"]] if ti == 0 else []
        if ti >= 2:
            lw += [tk[ti - 2]["xn_done"], tk[ti - 2]["sq_done"], (tstat, ti - 1)]
        C.rec("sync", lambda e, xb=xb, t0=t0, tn=tn:
              e.dma_start(out=xb[:, :, 0:tn], in_=xsrc[:, :, t0:t0 + tn]),
              wait=lw, inc=((wx, 16),))
        C.rec("scalar", lambda e, xb=xb, tn=tn:
              e.activation(xsq[:, :, 0:tn], xb[:, :, 0:tn], AF.Square),
              wait=[(wx, wx.n)], nops=1)
        tk[ti] = dict(sq_done=(SC, SC.n))

        def mm_stats(t, xb=xb, tn=tn):
            for c in range(8):
                t.matmul(ps_s[0:1, 0:tn], cn["ones_c"][:, :], xb[:, c, 0:tn],
                         start=(c == 0), stop=(c == 7))
            ins = None
            for c in range(8):
                ins = t.matmul(ps_q[0:1, 0:tn], cn["ones_c"][:, :], xsq[:, c, 0:tn],
                               start=(c == 0), stop=(c == 7))
            return ins
        C.rec("tensor", mm_stats, wait=[(SC, SC.n), (wx, wx.n)],
              inc=((tstat, 1),))

        def row_ops(e, tn=tn):
            mean = rows[:, 0:tn]
            var = rows[:, tt:tt + tn]
            e.tensor_scalar(mean, ps_s[0:1, 0:tn], cn["cst"][0:1, 1:2], None, OP.mult)
            e.tensor_tensor(var, mean, mean, OP.mult)
            e.scalar_tensor_tensor(var, ps_q[0:1, 0:tn], cn["cst"][0:1, 1:2], var,
                                   OP.mult, OP.subtract)
            return e.tensor_scalar(var, var, cn["cst"][0:1, 2:3], None, OP.add)
        C.rec("vector", row_ops, wait=[(tstat, tstat.n)], nops=4)
        C.rec("scalar", lambda e, tn=tn:
              e.activation(rows[:, 3 * tt:3 * tt + tn], rows[:, tt:tt + tn], AF.Sqrt),
              wait=[(VC, VC.n)], nops=1)
        C.rec("vector", lambda e, tn=tn:
              e.reciprocal(rows[:, 2 * tt:2 * tt + tn], rows[:, 3 * tt:3 * tt + tn]),
              wait=[(SC, SC.n)], nops=1)

        def mm_bc(t, tn=tn):
            t.matmul(ps_mb[:, 0:tn], cn["ones_r"][:, :], rows[0:1, 0:tn],
                     start=True, stop=True)
            return t.matmul(ps_rb[:, 0:tn], cn["ones_r"][:, :],
                            rows[0:1, 2 * tt:2 * tt + tn], start=True, stop=True)
        C.rec("tensor", mm_bc, wait=[(VC, VC.n)], inc=((tbc, 1),))

        xnf = xnf2[par]
        def xn_ops(e, xb=xb, xnbf=xnbf, xnf=xnf, tn=tn):
            ins = None
            for c in range(8):
                e.tensor_tensor(tmp[:, 0:tn], xb[:, c, 0:tn], ps_mb[:, 0:tn],
                                OP.subtract)
                e.tensor_tensor(xnf[:, c, 0:tn], tmp[:, 0:tn], ps_rb[:, 0:tn],
                                OP.mult)
                ins = e.tensor_copy(xnbf[:, c, 0:tn], xnf[:, c, 0:tn])
            return ins
        C.rec("vector", xn_ops,
              wait=[(tbc, tbc.n), (te, NJ * max(0, ti - 1)),
                    (sxns[par], 32 * (ti // 2))], nops=24)
        tk[ti]["xn_done"] = (VC, VC.n)

        C.rec("sync", lambda e, xnf=xnf, t0=t0, tn=tn:
              e.dma_start(out=xnf_dst[:, :, t0:t0 + tn], in_=xnf[:, :, 0:tn]),
              wait=[(VC, VC.n)], inc=((sxns[par], 16),))
        C.rec("sync", lambda e, xnbf=xnbf, t0=t0, tn=tn:
              e.dma_start(out=xnbf_dst[:, :, t0:t0 + tn], in_=xnbf[:, :, 0:tn]),
              wait=[(VC, VC.n)], inc=((sxns[par], 16),))

        for mi, (moff, msz, kind) in enumerate(mtiles):
            j = ti * NJ + mi
            bank = j % NB

            def mm_proj(t, moff=moff, msz=msz, bank=bank, tn=tn, xnbf=xnbf):
                ins = None
                for k in range(8):
                    ins = t.matmul(psm[bank][0:msz, 0:tn], w_sb[:, k, moff:moff + msz],
                                   xnbf[:, k, 0:tn], start=(k == 0), stop=(k == 7))
                return ins
            C.rec("tensor", mm_proj,
                  wait=[tk[ti]["xn_done"], (sxgs[bank], 16 * (j // NB))],
                  inc=((te, 1),))

            ebw = [(te, te.n), (sxgs[bank], 16 * (j // NB))]
            if kind == "z":
                C.rec("scalar", lambda e, bank=bank, msz=msz, tn=tn:
                      e.activation(ebuf[bank][0:msz, 0:tn], psm[bank][0:msz, 0:tn],
                                   AF.Silu),
                      wait=ebw, nops=1)
                C.rec("gpsimd", lambda e, bank=bank, moff=moff, t0=t0, tn=tn:
                      e.dma_start(out=zs_dst[:, moff // 128, t0:t0 + tn],
                                  in_=ebuf[bank][:, 0:tn]),
                      wait=[(SC, SC.n)], inc=((sxgs[bank], 16),))
            elif kind == "xbc":
                C.rec("scalar", lambda e, bank=bank, msz=msz, tn=tn:
                      e.activation(ebuf[bank][0:msz, 0:tn], psm[bank][0:msz, 0:tn],
                                   AF.Copy),
                      wait=ebw, nops=1)
                C.rec("gpsimd", lambda e, bank=bank, moff=moff, t0=t0, tn=tn:
                      e.dma_start(out=xbcpre_dst[:, (moff - 2048) // 128, t0:t0 + tn],
                                  in_=ebuf[bank][:, 0:tn]),
                      wait=[(SC, SC.n)], inc=((sxgs[bank], 16),))
            else:
                def epi_dt(e, bank=bank, tn=tn):
                    e.activation(ebbf[0:32, 0:tn], psm[bank][0:32, 0:tn],
                                 AF.Exp, bias=dtb[0:32, 0:1])
                    return e.activation(ebuf[bank][0:32, 0:tn], ebbf[0:32, 0:tn],
                                        AF.Ln, bias=cn["ones_c"][0:32, 0:1])
                C.rec("scalar", epi_dt, wait=ebw, nops=2)
                C.rec("gpsimd", lambda e, bank=bank, t0=t0, tn=tn:
                      e.dma_start(out=dt_o[0:32, t0:t0 + tn],
                                  in_=ebuf[bank][0:32, 0:tn]),
                      wait=[(SC, SC.n)], inc=((sxgs[bank], 16),))

    store_barrier = [(s, s.n) for s in sxgs + sxns]
    C.sb_pop()
    cbuf = [C.sb(f"cb{i}", [128, 17, tt + 3], F32) for i in range(2)]
    cacc = [C.sb(f"cacc{i}", [128, tt], F32) for i in range(4)]
    cout = [C.sb(f"cout{i}", [128, tt], F32) for i in range(2)]
    coutb = [C.sb(f"coutb{i}", [128, tt], BF16) for i in range(2)]
    xbc_dst = fm3(xbc_o)
    xbcbf_dst = fm3(xbcbf_o)
    wcvs = [C.sem("wcv0"), C.sem("wcv1")]
    dacks = [C.sem("dack0"), C.sem("dack1")]

    ntc = cdiv(nt_out, tt)
    cc = 0
    ctk = {}
    for ti in range(ntc):
        t0 = ti * tt
        tn = min(tt, nt_out - t0)
        cpar = ti % 2
        wcv = wcvs[cpar]
        cb = cbuf[cpar]
        lw = list(store_barrier)
        if ti >= 2:
            lw.append(ctk[ti - 2])
        C.rec("sync", lambda e, cb=cb, t0=t0, tn=tn:
              e.dma_start(out=cb[:, :, 0:tn + 3], in_=xbcpre_dst[:, :, t0:t0 + tn + 3]),
              wait=lw, inc=((wcv, 16),))
        for c in range(17):
            p = cc % 2
            a0, a1 = cacc[2 * p], cacc[2 * p + 1]
            co, cob = cout[p], coutb[p]

            def conv_ops(e, cb=cb, c=c, tn=tn, a0=a0, a1=a1):
                e.tensor_scalar(a0[:, 0:tn], cb[:, c, 0:tn], convw[:, c, 0:1],
                                None, OP.mult)
                e.scalar_tensor_tensor(a1[:, 0:tn], cb[:, c, 1:1 + tn],
                                       convw[:, c, 1:2], a0[:, 0:tn], OP.mult, OP.add)
                e.scalar_tensor_tensor(a0[:, 0:tn], cb[:, c, 2:2 + tn],
                                       convw[:, c, 2:3], a1[:, 0:tn], OP.mult, OP.add)
                return e.scalar_tensor_tensor(a1[:, 0:tn], cb[:, c, 3:3 + tn],
                                              convw[:, c, 3:4], a0[:, 0:tn],
                                              OP.mult, OP.add)
            cw = [(wcv, wcv.n), ld]
            if cc >= 2:
                cw.append(ctk[("s", cc - 2)])
            C.rec("vector", conv_ops, wait=cw, nops=4)

            def conv_acts(e, c=c, tn=tn, a1=a1, co=co, cob=cob):
                e.activation(co[:, 0:tn], a1[:, 0:tn], AF.Silu,
                             bias=convb[:, c:c + 1])
                return e.activation(cob[:, 0:tn], co[:, 0:tn], AF.Copy)
            sw = [(VC, VC.n)]
            if cc >= 2:
                sw.append((dacks[p], 32 * (cc // 2)))
            C.rec("scalar", conv_acts, wait=sw, nops=2)
            ctk[("s", cc)] = (SC, SC.n)
            C.rec("gpsimd", lambda e, c=c, t0=t0, tn=tn, co=co:
                  e.dma_start(out=xbc_dst[:, c, t0:t0 + tn], in_=co[:, 0:tn]),
                  wait=[(SC, SC.n)], inc=((dacks[p], 16),))
            C.rec("gpsimd", lambda e, c=c, t0=t0, tn=tn, cob=cob:
                  e.dma_start(out=xbcbf_dst[:, c, t0:t0 + tn], in_=cob[:, 0:tn]),
                  wait=[(SC, SC.n)], inc=((dacks[p], 16),))
            cc += 1
        ctk[ti] = (VC, VC.n)

    bar1 = [(s, s.n) for s in sxns + sxgs + dacks]
    if not fused:
        C.rec("gpsimd", lambda e: e.nop(), wait=bar1)
        return None
    xbct_o = C.scratch("xbc_tok", (nt_out, 2176), BF16)
    sxt = xbct_o.rearrange("(tt p) j -> p tt j", p=128)
    VC = C.ch("vector")
    C.sb_push()
    tb = [C.sbs(f"tt_a{i}", [128, 128], BF16) for i in range(2)]
    te_ = [C.sbs(f"tt_e{i}", [128, 128], BF16) for i in range(2)]
    ibf1 = C.sbs("tt_ibf", [128, 128], BF16)
    pst = [C.pss(f"tt_ps{i}", [128, 128], BF16) for i in range(2)]
    ib_d = C.const("tt_ibf_c", np.eye(128).astype(ml_dtypes.bfloat16))
    ldt = C.sem("tt_ld")
    C.dma("sync", ibf1[:, :], ib_d, ldt)
    wv = [C.sem(f"tt_w{i}") for i in range(2)]
    sv = [C.sem(f"tt_s{i}") for i in range(2)]
    ttm = C.sem("tt_m")
    it = 0
    toks = {}
    for ch in range(17):
        for tt_i in range(nt_out // 128):
            sl = it % 2
            lw = bar1 + [(ldt, ldt.n)] if it == 0 else []
            if it >= 2:
                lw.append(toks[it - 2])
            C.rec("sync", lambda e, sl=sl, ch=ch, tt_i=tt_i:
                  e.dma_start(out=tb[sl][:, :],
                              in_=xbcbf_dst[:, ch, tt_i * 128:(tt_i + 1) * 128]),
                  wait=lw + [(sv[sl], 16 * (it // 2))], inc=((wv[sl], 16),))
            C.rec("tensor", lambda t, sl=sl:
                  t.transpose(pst[sl][:, :], tb[sl][:, :], ibf1[:, :]),
                  wait=[(wv[sl], wv[sl].n), (VC, VC.n)], inc=((ttm, 1),))
            C.rec("vector", lambda e, sl=sl:
                  e.tensor_copy(te_[sl][:, :], pst[sl][:, :]),
                  wait=[(ttm, ttm.n)], nops=1)
            toks[it] = (VC, VC.n)
            C.rec("sync", lambda e, sl=sl, ch=ch, tt_i=tt_i:
                  e.dma_start(out=sxt[:, tt_i, ch * 128:(ch + 1) * 128],
                              in_=te_[sl][:, :]),
                  wait=[(VC, VC.n)], inc=((sv[sl], 16),))
            it += 1
    C.sb_pop()
    return dict(xbc_tok=xbct_o, bar=[(s, s.n) for s in sv] + bar1)



def stage_rms(C, uid, src_d, ntok, dim, dst_bf_d=None, dst_f_d=None,
              scale_sb=None, bias_sb=None, eps_col=2, wait=(), with_mean=False,
              tt=512):
    cn = C.consts
    VC, SC = C.ch("vector"), C.ch("scalar")
    nk = dim // 128
    ntt = cdiv(ntok, tt)
    ssrc = fm3(src_d)
    sdst_bf = fm3(dst_bf_d) if dst_bf_d is not None else None
    sdst_f = fm3(dst_f_d) if dst_f_d is not None else None
    xb = [C.sbs(f"r{uid}_x{i}", [128, nk, tt], F32) for i in range(2)]
    ob = [C.sbs(f"r{uid}_o{i}", [128, nk, tt], BF16) for i in range(2)]
    of = [C.sbs(f"r{uid}_f{i}", [128, nk, tt], F32) for i in range(2)] if dst_f_d is not None else None
    xsq = C.sbs(f"r{uid}_sq", [128, nk, tt], F32)
    rows = C.sbs(f"r{uid}_rows", [1, 3 * tt], F32)
    ps_s = C.pss(f"r{uid}_pss", [1, tt]) if with_mean else None
    ps_mb = C.pss(f"r{uid}_psmb", [128, tt]) if with_mean else None
    ps_q = C.pss(f"r{uid}_psq", [1, tt])
    ps_rb = C.pss(f"r{uid}_psrb", [128, tt])
    wx = [C.sem(f"r{uid}_wx{i}") for i in range(2)]
    st = [C.sem(f"r{uid}_st{i}") for i in range(2)]
    tstat = C.sem(f"r{uid}_ts")
    tbc = C.sem(f"r{uid}_tb")
    tk = {}
    nst_per_tile = (1 if dst_bf_d is not None else 0) + (1 if dst_f_d is not None else 0)
    for ti, (t0, tn) in enumerate(mk_tiles(ntok, tt)):
        par = ti % 2
        lw = list(wait) if ti == 0 else []
        lw.append((C.consts["ld"], C.consts["ld"].n))
        if ti >= 2:
            lw += [tk[ti - 2]["done"], tk[ti - 2]["sq"], (tstat, ti - 1)]
        C.rec("sync", lambda e, xb=xb[par], t0=t0, tn=tn:
              e.dma_start(out=xb[:, :, 0:tn], in_=ssrc[:, :, t0:t0 + tn]),
              wait=lw, inc=((wx[par], 16),))
        C.rec("scalar", lambda e, xb=xb[par], tn=tn:
              e.activation(xsq[:, :, 0:tn], xb[:, :, 0:tn], AF.Square),
              wait=[(wx[par], wx[par].n)], nops=1)
        tk[ti] = dict(sq=(SC, SC.n))

        def mm_stats(t, tn=tn, xb=xb[par]):
            ins = None
            if with_mean:
                for c in range(nk):
                    t.matmul(ps_s[0:1, 0:tn], cn["ones_c"][:, :], xb[:, c, 0:tn],
                             start=(c == 0), stop=(c == nk - 1))
            for c in range(nk):
                ins = t.matmul(ps_q[0:1, 0:tn], cn["ones_c"][:, :], xsq[:, c, 0:tn],
                               start=(c == 0), stop=(c == nk - 1))
            return ins
        C.rec("tensor", mm_stats, wait=[(SC, SC.n), (wx[par], wx[par].n)],
              inc=((tstat, 1),))

        inv_dim_col = 1 if dim == 1024 else 5
        if with_mean:
            def row_ops(e, tn=tn, icol=inv_dim_col):
                mrow = rows[:, 2 * tt:2 * tt + tn]
                e.tensor_scalar(mrow, ps_s[0:1, 0:tn],
                                cn["cst"][0:1, icol:icol + 1], None, OP.mult)
                e.tensor_tensor(rows[:, 0:tn], mrow, mrow, OP.mult)
                e.scalar_tensor_tensor(rows[:, 0:tn], ps_q[0:1, 0:tn],
                                       cn["cst"][0:1, icol:icol + 1],
                                       rows[:, 0:tn], OP.mult, OP.subtract)
                e.tensor_scalar(rows[:, 0:tn], rows[:, 0:tn],
                                cn["cst"][0:1, eps_col:eps_col + 1], None, OP.add)
                return e.tensor_copy(rows[:, tt:tt + tn], rows[:, 0:tn])
            C.rec("vector", row_ops, wait=[(tstat, tstat.n)], nops=5)
        else:
            def row_ops(e, tn=tn, icol=inv_dim_col):
                e.tensor_scalar(rows[:, 0:tn], ps_q[0:1, 0:tn],
                                cn["cst"][0:1, icol:icol + 1],
                                cn["cst"][0:1, eps_col:eps_col + 1], OP.mult, OP.add)
                return e.tensor_copy(rows[:, tt:tt + tn], rows[:, 0:tn])
            C.rec("vector", row_ops, wait=[(tstat, tstat.n)], nops=2)
        C.rec("scalar", lambda e, tn=tn:
              e.activation(rows[:, 0:tn], rows[:, tt:tt + tn], AF.Sqrt),
              wait=[(VC, VC.n)], nops=1)
        C.rec("vector", lambda e, tn=tn:
              e.reciprocal(rows[:, tt:tt + tn], rows[:, 0:tn]),
              wait=[(SC, SC.n)], nops=1)
        def mm_bc(t, tn=tn):
            if with_mean:
                t.matmul(ps_mb[:, 0:tn], cn["ones_r"][:, :],
                         rows[0:1, 2 * tt:2 * tt + tn], start=True, stop=True)
            return t.matmul(ps_rb[:, 0:tn], cn["ones_r"][:, :],
                            rows[0:1, tt:tt + tn], start=True, stop=True)
        C.rec("tensor", mm_bc, wait=[(VC, VC.n)], inc=((tbc, 1),))

        def app(e, xb=xb[par], ob=ob[par], ofb=(of[par] if of else None), tn=tn):
            ins = None
            for c in range(nk):
                if with_mean:
                    e.tensor_tensor(xsq[:, c, 0:tn], xb[:, c, 0:tn],
                                    ps_mb[:, 0:tn], OP.subtract)
                    xb_eff = xsq
                else:
                    xb_eff = xb
                if scale_sb is not None:
                    e.tensor_tensor(xsq[:, c, 0:tn], xb_eff[:, c, 0:tn],
                                    ps_rb[:, 0:tn], OP.mult)
                    src = xsq[:, c, 0:tn]
                    bias1 = bias_sb[:, c:c + 1] if bias_sb is not None else None
                    op1 = OP.add if bias_sb is not None else OP.bypass
                    if ofb is not None:
                        e.tensor_scalar(ofb[:, c, 0:tn], src, scale_sb[:, c:c + 1],
                                        bias1, OP.mult, op1)
                        src = ofb[:, c, 0:tn]
                        ins = e.tensor_copy(ob[:, c, 0:tn], src)
                    else:
                        ins = e.tensor_scalar(ob[:, c, 0:tn], src,
                                              scale_sb[:, c:c + 1], bias1,
                                              OP.mult, op1)
                else:
                    if ofb is not None:
                        e.tensor_tensor(ofb[:, c, 0:tn], xb_eff[:, c, 0:tn],
                                        ps_rb[:, 0:tn], OP.mult)
                        ins = e.tensor_copy(ob[:, c, 0:tn], ofb[:, c, 0:tn])
                    else:
                        ins = e.tensor_tensor(ob[:, c, 0:tn], xb_eff[:, c, 0:tn],
                                              ps_rb[:, 0:tn], OP.mult)
            return ins
        per_c = (1 + (1 if scale_sb is not None else 0) + (1 if of else 0)
                 + (1 if with_mean else 0))
        C.rec("vector", app,
              wait=[(tbc, tbc.n), (st[par], 16 * nst_per_tile * (ti // 2))],
              nops=nk * per_c)
        tk[ti]["done"] = (VC, VC.n)
        if sdst_bf is not None:
            C.rec("sync", lambda e, ob=ob[par], t0=t0, tn=tn:
                  e.dma_start(out=sdst_bf[:, :, t0:t0 + tn], in_=ob[:, :, 0:tn]),
                  wait=[(VC, VC.n)], inc=((st[par], 16),))
        if sdst_f is not None:
            C.rec("sync", lambda e, ofb=of[par], t0=t0, tn=tn:
                  e.dma_start(out=sdst_f[:, :, t0:t0 + tn], in_=ofb[:, :, 0:tn]),
                  wait=[(VC, VC.n)], inc=((st[par], 16),))
    return [(s, s.n) for s in st + wx]


def stage_proj(C, uid, src_d, w_d, d_in, d_out, ntok, epi, *,
               tg=None, wslab=None, aux_d=None, aux_dim=None, wait=(),
               mtiles=None, tt=512, st_per_job=1, aux2_d=None, aux2_dim=None,
               aux2_off=0):
    cn = C.consts
    kp = 128 if d_in % 128 == 0 else d_in
    nk = d_in // kp
    tg = tg or ntok
    wslab = wslab or d_out
    ngr = cdiv(ntok, tg)
    nsl = cdiv(d_out, wslab)
    ssrc = fm3(src_d, kp)
    wsrc = w_d.rearrange("(c p) m -> p c m", p=kp)
    saux = fm3(aux_d) if aux_d is not None else None
    saux2 = fm3(aux2_d) if aux2_d is not None else None
    xg = C.sbs(f"p{uid}_x", [kp, nk, tg], BF16)
    nka = (aux_dim // 128) if aux_dim else 0
    axg = C.sbs(f"p{uid}_a", [128, nka, tg], F32) if aux_d is not None else None
    nka2 = (aux2_dim // 128) if aux2_dim else 0
    axg2 = (C.sbs(f"p{uid}_a2", [128, nka2, tg + aux2_off], F32)
            if aux2_d is not None else None)
    wbuf = [C.sbs(f"p{uid}_w{i}", [kp, nk, wslab], BF16) for i in range(2 if nsl > 1 or ngr > 1 else 1)]
    NB = 3
    psm = [C.pss(f"p{uid}_ps{i}", [128, tt]) for i in range(NB)]
    wx = C.sem(f"p{uid}_wx")
    wa = C.sem(f"p{uid}_wa")
    wws = [C.sem(f"p{uid}_ww{i}") for i in range(2)]
    te = C.sem(f"p{uid}_te")
    sxgs = [C.sem(f"p{uid}_sx{i}") for i in range(NB)]
    if mtiles is None:
        mtiles = [(m0, min(128, d_out - m0)) for m0 in range(0, d_out, 128)]
    j = 0
    wldw_tok = {}
    for g in range(ngr):
        g0 = g * tg
        gn = min(tg, ntok - g0)
        C.rec("sync", lambda e, g0=g0, gn=gn:
              e.dma_start(out=xg[:, :, 0:gn], in_=ssrc[:, :, g0:g0 + gn]),
              wait=(list(wait) + [cn["ld"]]) if g == 0 else [(te, te.n)],
              inc=((wx, 16),))
        if aux_d is not None:
            C.rec("sync", lambda e, g0=g0, gn=gn:
                  e.dma_start(out=axg[:, :, 0:gn], in_=saux[:, :, g0:g0 + gn]),
                  wait=[] if g == 0 else [(s, s.n) for s in sxgs],
                  inc=((wa, 16),))
        if aux2_d is not None:
            C.rec("sync", lambda e, g0=g0, gn=gn:
                  e.dma_start(out=axg2[:, :, 0:gn + aux2_off],
                              in_=saux2[:, :, g0:g0 + gn + aux2_off]),
                  wait=[] if g == 0 else [(s, s.n) for s in sxgs],
                  inc=((wa, 16),))
        for sl in range(nsl):
            s0 = sl * wslab
            sn = min(wslab, d_out - s0)
            wslot = (g * nsl + sl) % len(wbuf)
            if g == 0 or len(wbuf) > 1:
                gi = g * nsl + sl
                lw2 = list(wait) if gi < 2 else []
                if gi >= 2 and len(wbuf) == 2:
                    lw2 = [wldw_tok[gi - 2]]
                wsem = wws[wslot]
                def wload(e, wslot=wslot, s0=s0, sn=sn, wsem=wsem):
                    ins = None
                    for c in range(nk):
                        ins = e.dma_start(out=wbuf[wslot][:, c, 0:sn],
                                          in_=wsrc[:, c, s0:s0 + sn])
                        ins.then_inc(wsem.h, 16)
                    return ins
                C.rec("gpsimd", wload, wait=lw2)
                C.mark(wsem, 16 * nk, "gpsimd")
            for (moff, msz) in [mt for mt in mtiles if s0 <= mt[0] < s0 + sn]:
                for (t0, tn) in mk_tiles(gn, tt):
                    bank = j % NB

                    def mm(t, moff=moff, msz=msz, bank=bank, t0=t0, tn=tn,
                           wslot=wslot, s0=s0):
                        ins = None
                        for k in range(nk):
                            ins = t.matmul(psm[bank][0:msz, 0:tn],
                                           wbuf[wslot][:, k, moff - s0:moff - s0 + msz],
                                           xg[:, k, t0:t0 + tn],
                                           start=(k == 0), stop=(k == nk - 1))
                        return ins
                    mmw = [(wx, wx.n), (wws[wslot], wws[wslot].n),
                           (sxgs[bank], 16 * st_per_job * (j // NB))]
                    if aux_d is not None or aux2_d is not None:
                        mmw.append((wa, wa.n))
                    C.rec("tensor", mm, wait=mmw, inc=((te, 1),))
                    info = dict(ps=psm[bank], bank=bank, moff=moff, msz=msz,
                                t0=t0, tn=tn, g0=g0, aux=axg, aux2=axg2,
                                a2off=aux2_off, sxg=sxgs[bank],
                                tew=(te, te.n),
                                ebw=(sxgs[bank], 16 * st_per_job * (j // NB)))
                    epi(C, info)
                    j += 1
            wldw_tok[g * nsl + sl] = (te, te.n)
    return [(s, s.n) for s in sxgs] + [(wx, wx.n)]


def mk_epi(C, uid, spec):
    NB = 3
    dst_f = spec.get("dst_f")
    dst_bf = spec.get("dst_bf")

    def _fm(d):
        if len(d.ap) == 3:
            return d
        p = 128 if d.shape[0] % 128 == 0 else d.shape[0]
        return d.rearrange("(c p) t -> p c t", p=p)
    df = _fm(dst_f) if dst_f is not None else None
    db = _fm(dst_bf) if dst_bf is not None else None
    eb = [C.sbs(f"e{uid}_f{i}", [128, 512], F32) for i in range(NB)]
    ebb = [C.sbs(f"e{uid}_b{i}", [128, 512], BF16) for i in range(NB)] if dst_bf is not None else None
    SC, VC = C.ch("scalar"), C.ch("vector")

    def epi(C, info):
        bank, msz, tn = info["bank"], info["msz"], info["tn"]
        ch = info["moff"] // 128
        act = spec.get("act") or AF.Copy
        kw = {}
        if "scale_sb" in spec:
            kw["scale"] = spec["scale_sb"][0:msz, ch:ch + 1]
        if "bias_sb" in spec:
            kw["bias"] = spec["bias_sb"][0:msz, ch:ch + 1]
        if "scale_cols" in spec:
            c0 = spec["scale_cols"][0]
            kw["scale"] = C.consts["cst"][0:msz, c0:c0 + 1]
        if "bias_cols" in spec:
            c0 = spec["bias_cols"][0]
            kw["bias"] = C.consts["cst"][0:msz, c0:c0 + 1]
        if spec.get("softplus"):
            def sp_ops(e, bank=bank, msz=msz, tn=tn, kw=kw):
                e.activation(eb[bank][0:msz, 0:tn], info["ps"][0:msz, 0:tn],
                             AF.Exp, **kw)
                return e.activation(eb[bank][0:msz, 0:tn], eb[bank][0:msz, 0:tn],
                                    AF.Ln, bias=C.consts["ones_c"][0:msz, 0:1])
            C.rec("scalar", sp_ops, wait=[info["tew"], info["ebw"]], nops=2)
            last = (SC, SC.n)
        elif spec.get("act") is None and "bias" in kw:
            C.rec("vector", lambda e, bank=bank, msz=msz, tn=tn, kw=kw:
                  e.tensor_scalar(eb[bank][0:msz, 0:tn], info["ps"][0:msz, 0:tn],
                                  kw["bias"], None, OP.add),
                  wait=[info["tew"], info["ebw"]], nops=1)
            last = (VC, VC.n)
        else:
            C.rec("scalar", lambda e, bank=bank, msz=msz, tn=tn, act=act, kw=kw:
                  e.activation(eb[bank][0:msz, 0:tn], info["ps"][0:msz, 0:tn],
                               act, **kw),
                  wait=[info["tew"], info["ebw"]], nops=1)
            last = (SC, SC.n)
        nv = 0
        if spec.get("aux") in ("add", "mul"):
            aop = OP.add if spec["aux"] == "add" else OP.mult
            C.rec("vector", lambda e, bank=bank, msz=msz, tn=tn, ch=ch, aop=aop:
                  e.tensor_tensor(eb[bank][0:msz, 0:tn], eb[bank][0:msz, 0:tn],
                                  info["aux"][0:msz, ch, info["t0"]:info["t0"] + tn],
                                  aop),
                  wait=[last], nops=1)
            nv += 1
            last = (VC, VC.n)
        if spec.get("aux2") == "add":
            C.rec("vector", lambda e, bank=bank, msz=msz, tn=tn, ch=ch:
                  e.tensor_tensor(eb[bank][0:msz, 0:tn], eb[bank][0:msz, 0:tn],
                                  info["aux2"][0:msz, ch,
                                               info["a2off"] + info["t0"]:
                                               info["a2off"] + info["t0"] + tn],
                                  OP.add),
                  wait=[last], nops=1)
            nv += 1
            last = (VC, VC.n)
        if spec.get("post"):
            nv2 = spec["post_n"]
            C.rec("vector", lambda e, bank=bank, tn=tn:
                  spec["post"](e, eb[bank], info),
                  wait=[last], nops=nv2)
            last = (VC, VC.n)
        if dst_bf is not None:
            C.rec("vector", lambda e, bank=bank, msz=msz, tn=tn:
                  e.tensor_copy(ebb[bank][0:msz, 0:tn], eb[bank][0:msz, 0:tn]),
                  wait=[last], nops=1)
            last = (VC, VC.n)
        tabs = info["g0"] + info["t0"]
        if dst_f is not None:
            C.rec("gpsimd", lambda e, bank=bank, msz=msz, tn=tn, ch=ch, tabs=tabs:
                  e.dma_start(out=df[0:msz, ch, tabs:tabs + tn],
                              in_=eb[bank][0:msz, 0:tn]),
                  wait=[last], inc=((info["sxg"], 16),))
        if dst_bf is not None:
            C.rec("gpsimd", lambda e, bank=bank, msz=msz, tn=tn, ch=ch, tabs=tabs:
                  e.dma_start(out=db[0:msz, ch, tabs:tabs + tn],
                              in_=ebb[bank][0:msz, 0:tn]),
                  wait=[last], inc=((info["sxg"], 16),))
    return epi


def stage_elem2(C, uid, a_d, b_d, dst_d, dim, ntok, op=OP.mult, wait=()):
    nk = dim // 128
    tt = 512
    ntt = cdiv(ntok, tt)
    sa = fm3(a_d)
    sb_ = fm3(b_d)
    sd = fm3(dst_d)
    ab = [C.sbs(f"el{uid}_a{i}", [128, nk, tt], F32) for i in range(2)]
    bb = [C.sbs(f"el{uid}_b{i}", [128, nk, tt], F32) for i in range(2)]
    ob = [C.sbs(f"el{uid}_o{i}", [128, nk, tt], F32) for i in range(2)]
    wl = [C.sem(f"el{uid}_w{i}") for i in range(2)]
    st = [C.sem(f"el{uid}_s{i}") for i in range(2)]
    VC = C.ch("vector")
    tk = {}
    for ti, (t0, tn) in enumerate(mk_tiles(ntok, tt)):
        par = ti % 2
        lw = list(wait) if ti == 0 else []
        if ti >= 2:
            lw.append(tk[ti - 2])
        C.rec("sync", lambda e, a=ab[par], t0=t0, tn=tn:
              e.dma_start(out=a[:, :, 0:tn], in_=sa[:, :, t0:t0 + tn]),
              wait=lw, inc=((wl[par], 16),))
        C.rec("sync", lambda e, b=bb[par], t0=t0, tn=tn:
              e.dma_start(out=b[:, :, 0:tn], in_=sb_[:, :, t0:t0 + tn]),
              wait=[], inc=((wl[par], 16),))
        C.rec("vector", lambda e, a=ab[par], b=bb[par], o=ob[par], tn=tn:
              e.tensor_tensor(o[:, :, 0:tn], a[:, :, 0:tn], b[:, :, 0:tn], op),
              wait=[(wl[par], wl[par].n), (st[par], 16 * (ti // 2))], nops=1)
        tk[ti] = (VC, VC.n)
        C.rec("sync", lambda e, o=ob[par], t0=t0, tn=tn:
              e.dma_start(out=sd[:, :, t0:t0 + tn], in_=o[:, :, 0:tn]),
              wait=[(VC, VC.n)], inc=((st[par], 16),))
    return [(s, s.n) for s in st]



def emit_g2f(C, nt, fz, zs_d, g2_d):
    VC = C.ch("vector")
    cn = C.consts
    y3g_t = fz["y3g"]
    xbc_d = fz["xbc"]
    drep_d = C.inp("d_rep", (128, 16), F32)
    ld = C.sem("g2f_ld")
    drep = C.sb("g2f_dr", [128, 16], F32)
    C.dma("sync", drep[:, :], drep_d, ld)
    sxs = fm3(xbc_d)
    szs = fm3(zs_d)
    sg2 = fm3(g2_d)
    g2tt = 256
    yb = [C.sb(f"g2f_y{i}", [128, 16, g2tt], F32) for i in range(2)]
    xb = [C.sb(f"g2f_x{i}", [128, 16, g2tt], F32) for i in range(2)]
    zb = [C.sb(f"g2f_z{i}", [128, 16, g2tt], F32) for i in range(2)]
    ob = [C.sb(f"g2f_o{i}", [128, 16, g2tt], F32) for i in range(2)]
    tmp = C.sb("g2f_t", [128, g2tt], F32)
    wl = [C.sem(f"g2f_w{i}") for i in range(2)]
    st = [C.sem(f"g2f_s{i}") for i in range(2)]
    toks = {}
    for ti, (t0, tn) in enumerate(mk_tiles(nt, g2tt)):
        par = ti % 2
        lw = list(fz["y3_bar"]) + [(ld, ld.n), cn["ld"]] if ti == 0 else []
        if ti >= 2:
            lw.append(toks[ti - 2])
        C.dyn_dma(yb[par][:, :, 0:tn], y3g_t, fz["CFG_S0"],
                  [[fz["TB"], 128], [128 * fz["TB"], 16], [1, tn]],
                  wl[par], wait=lw, addc=t0, eng="scalar")
        C.rec("sync", lambda e, b=xb[par], t0=t0, tn=tn:
              e.dma_start(out=b[:, :, 0:tn], in_=sxs[:, 0:16, t0:t0 + tn]),
              wait=[], inc=((wl[par], 16),))
        C.rec("sync", lambda e, b=zb[par], t0=t0, tn=tn:
              e.dma_start(out=b[:, :, 0:tn], in_=szs[:, 0:16, t0:t0 + tn]),
              wait=[], inc=((wl[par], 16),))

        def ops(e, y=yb[par], x=xb[par], z=zb[par], o=ob[par], tn=tn):
            ins = None
            for ch in range(16):
                e.scalar_tensor_tensor(tmp[:, 0:tn], x[:, ch, 0:tn],
                                       drep[:, ch:ch + 1], y[:, ch, 0:tn],
                                       OP.mult, OP.add)
                ins = e.tensor_tensor(o[:, ch, 0:tn], tmp[:, 0:tn],
                                      z[:, ch, 0:tn], OP.mult)
            return ins
        C.rec("vector", ops, wait=[(wl[par], wl[par].n),
                                   (st[par], 16 * (ti // 2))], nops=32)
        toks[ti] = (VC, VC.n)
        C.rec("sync", lambda e, o=ob[par], t0=t0, tn=tn:
              e.dma_start(out=sg2[:, :, t0:t0 + tn], in_=o[:, :, 0:tn]),
              wait=[(VC, VC.n)], inc=((st[par], 16),))
    return [(s, s.n) for s in st + wl]


def emit_launch2(C, nt, fused=False, fz=None):
    DIN = 2048
    y_d = C.inp("y", (DIN, nt), F32) if not fused else None
    zs_d = C.inp("z_silu", (DIN, nt), F32)
    xn_d = C.inp("xn_f", (D, nt), F32)
    mnorm_d = C.inp("mnorm_r", (128, 16), F32)
    wout_d = C.inp("w_outproj", (DIN, D), F32)
    wattn_d = C.inp("w_attn", (D, 1152), F32)

    x2_o = C.out("x2", (D, nt), F32)
    q_o = C.out("q_bf", (D, nt), BF16)
    kv_o = C.out("kv_bf", (128, nt), BF16)
    g2_d = C.scratch("g2", (DIN, nt), F32)
    rms_d = C.scratch("rms_bf", (DIN, nt), BF16)
    r2_d = C.scratch("r2_bf", (D, nt), BF16)

    cn = make_consts(C)
    ld = C.sem("ld")
    mnorm = C.sb("mnorm", [128, 16], F32)
    C.dma("sync", mnorm[:, :], mnorm_d, ld)

    C.sb_push()
    if not fused:
        b1 = stage_elem2(C, "g2", y_d, zs_d, g2_d, DIN, nt, wait=[])
    else:
        b1 = emit_g2f(C, nt, fz, zs_d, g2_d)
    C.sb_pop()
    C.sb_push()
    b2 = stage_rms(C, "m", g2_d, nt, DIN, dst_bf_d=rms_d, scale_sb=mnorm,
                   wait=b1 + [ld])
    C.sb_pop()
    C.sb_push()
    epi1 = mk_epi(C, "o", dict(dst_f=x2_o, aux="add"))
    b3 = stage_proj(C, "o", rms_d, wout_d, DIN, D, nt, epi1,
                    aux_d=xn_d, aux_dim=D, wait=b2)
    C.sb_pop()
    C.sb_push()
    b4 = stage_rms(C, "r2", x2_o, nt, D, dst_bf_d=r2_d, wait=b3)
    C.sb_pop()
    C.sb_push()
    epi2 = mk_epi(C, "q", dict(dst_bf=q_o))
    mt_q = [(m0, 128) for m0 in range(0, 1024, 128)]
    b5 = stage_proj(C, "q", r2_d, wattn_d, D, 1024, nt, epi2, wait=b4,
                    mtiles=mt_q)
    epi3 = mk_epi(C, "kv", dict(dst_bf=kv_o))
    b6 = stage_proj(C, "kv", r2_d,
                    C.inp("w_attn_kv", (D, 128), F32), D, 128, nt, epi3,
                    wait=b4)
    C.sb_pop()
    C.rec("gpsimd", lambda e: e.nop(), wait=b5 + b6)
    return b5 + b6



def emit_launch3(C, Tfull, NU=8, fused=False, fz=None):
    L = 128
    NCH = Tfull // L
    if not fused:
        xs_bf_d = C.inp("xs_bf", (128, NU, NCH, 64), BF16)
        xs_f_d = C.inp("xs_f", (128, NU, NCH, 64), F32)
        btok_d = C.inp("b_tok", (128, NCH, 64), BF16)
        bfm_d = C.inp("b_fm", (64, Tfull), BF16)
        cfm_d = C.inp("c_fm", (64, Tfull), BF16)
        dttok_d = C.inp("dt_tok", (128, NU, NCH), F32)
        y_o = C.out("y_tok", (128, NU, NCH, 64), F32)
    else:
        y3_ap = fz["y3"]
    alog_d = C.inp("alog_r", (128, NU), F32)
    dvec_d = C.inp("d_r", (128, NU), F32)

    cn = make_consts(C)
    tri_d = C.const("tri_c", np.triu(np.ones((128, 128), np.float32)))
    i128_d = C.const("i128_c", np.eye(128, dtype=np.float32))
    mneg = np.where(np.arange(128)[:, None] <= np.arange(128)[None, :],
                    0.0, -1e30).astype(np.float32)
    mask_d = C.const("mask_c", mneg)

    ld = C.sem("ld")
    ldg = C.sem("ldg")
    if fused:
        xs_sb = C.sb("xs_bf_s", [128, NCH, NU, 64], BF16)
        xsl = lambda u, c: xs_sb[:, c, u, :]
    else:
        xs_sb = C.sb("xs_bf_s", [128, NU, NCH, 64], BF16)
        xsl = lambda u, c: xs_sb[:, u, c, :]
    xs_bf = xs_sb
    btok = C.sb("btok_s", [128, NCH, 64], BF16)
    bfm = C.sb("bfm_s", [64, Tfull], BF16)
    cfm = C.sb("cfm_s", [64, Tfull], BF16)
    alog = C.sb("alog_s", [128, NU], F32)
    avals = C.sb("avals_s", [128, NU], F32)
    dvec = C.sb("dvec_s", [128, NU], F32)
    tri = C.sb("tri_s", [128, 128], F32)
    i128 = C.sb("i128_s", [128, 128], F32)
    mask = C.sb("mask_s", [128, 128], F32)
    if not fused:
        dttok = C.sb("dttok_s", [128, NU, NCH], F32)
        for u in range(NU):
            C.dma("sync", xs_bf[:, u, :, :], xs_bf_d[:, u, :, :], ld)
        C.dma("sync", btok[:, :, :], btok_d, ld)
        C.dma("sync", bfm[:, :], bfm_d, ld)
        C.dma("sync", cfm[:, :], cfm_d, ld)
        C.dma("sync", dttok[:, :, :], dttok_d, ld)
    else:
        xbcg_t = fz["xbcg"]
        bcfmg_t = fz["bcfmg"]
        dtfmg_t = fz["dtfmg"]
        entry = list(fz["l3_entry"])
        BLK = fz.get("BLK", 2048)
        CPB = BLK // 128
        NBL = Tfull // BLK
        dtrows = C.sb("dtrows_s", [1, NU * BLK], F32)
        ldq = C.sem("ldq")
        C.dyn_dma(xs_sb[:, :, :, :], xbcg_t, fz["CFG_U0"],
                  [[2176, 128], [128 * 2176, NCH], [1, NU * 64]],
                  ldq, wait=entry)
        for bl in range(NBL):
            C.rec("sync", lambda e, bl=bl:
                  e.dma_start(out=btok[:, bl * CPB:(bl + 1) * CPB, :],
                              in_=bass.AP(xbcg_t, bl * BLK * 2176 + 2048,
                                          [[2176, 128], [128 * 2176, CPB], [1, 64]])),
                  wait=entry, inc=((ld, 16),))
            C.rec("sync", lambda e, bl=bl:
                  e.dma_start(out=bfm[:, bl * BLK:(bl + 1) * BLK],
                              in_=bass.AP(bcfmg_t, bl * 128 * BLK,
                                          [[BLK, 64], [1, BLK]])),
                  wait=entry, inc=((ld, 16),))
            C.rec("sync", lambda e, bl=bl:
                  e.dma_start(out=cfm[:, bl * BLK:(bl + 1) * BLK],
                              in_=bass.AP(bcfmg_t, (bl * 128 + 64) * BLK,
                                          [[BLK, 64], [1, BLK]])),
                  wait=entry, inc=((ld, 16),))
    C.dma("sync", alog[:, :], alog_d, ld)
    C.dma("sync", dvec[:, :], dvec_d, ld)
    C.dma("sync", tri[:, :], tri_d, ld)
    C.dma("sync", i128[:, :], i128_d, ld)
    C.dma("sync", mask[:, :], mask_d, ld)

    VC, SC = C.ch("vector"), C.ch("scalar")
    C.rec("scalar", lambda e: e.activation(avals[:, :], alog[:, :], AF.Exp),
          wait=[(ld, ld.n), cn["ld"]], nops=1)
    C.rec("vector", lambda e: e.tensor_scalar(avals[:, :], avals[:, :],
                                              cn["cst"][:, 7:8], None, OP.mult),
          wait=[(SC, SC.n)], nops=1)
    a_ready = (VC, VC.n)

    hT = [C.sb(f"hT{u}", [64, 64], F32) for u in range(NU)]
    hTb = [C.sb(f"hTb{u}", [64, 64], BF16) for u in range(NU)]
    for u in range(NU):
        C.rec("vector", lambda e, u=u: (e.memset(hT[u][:, :], 0.0),
                                        e.memset(hTb[u][:, :], 0.0))[-1], nops=2)
    h0_ready = (VC, VC.n)

    NS = 4
    bA = [C.ps(f"pbA{r}", [128, 512]) for r in range(NS)]
    bB = [C.ps(f"pbB{r}", [128, 512]) for r in range(NS)]
    ps_d = [bA[r][:, 0:128] for r in range(NS)]
    ps_gt = [bA[r][:, 128:256] for r in range(NS)]
    ps_y = [bA[r][:, 256:320] for r in range(NS)]
    ps_s = [bA[r][0:64, 320:384] for r in range(NS)]
    ps_cd = [bA[r][:, 384:385] for r in range(NS)]
    ps_pl = [bA[r][0:64, 385:386] for r in range(NS)]
    ps_pb = [bB[r][0:64, 0:128] for r in range(NS)]
    ps_row = [bB[r][0:1, 128:256] for r in range(NS)]
    t_lp = [C.sb(f"tlp{r}", [128, 1], F32) for r in range(NS)]
    t_lr = [C.sb(f"tlr{r}", [1, 128], F32) for r in range(NS)]
    t_ln = [C.sb(f"tln{r}", [1, 128], F32) for r in range(NS)]
    t_pr = [C.sb(f"tpr{r}", [1, 128], F32) for r in range(NS)]
    t_dm = [C.sb(f"tdm{r}", [128, 128], F32) for r in range(NS)]
    t_e = [C.sb(f"te{r}", [128, 128], F32) for r in range(NS)]
    t_wt = [C.sb(f"twt{r}", [128, 128], F32) for r in range(NS)]
    t_wtb = [C.sb(f"twtb{r}", [128, 128], BF16) for r in range(NS)]
    t_wc = [C.sb(f"twc{r}", [128, 1], F32) for r in range(NS)]
    t_xw = [C.sb(f"txw{r}", [128, 64], BF16) for r in range(NS)]
    t_cb = [C.sb(f"tcb{r}", [64, 128], BF16) for r in range(NS)]
    t_pl = [C.sb(f"tpl{r}", [64, 1], F32) for r in range(NS)]
    t_y = [C.sb(f"ty{r}", [128, 64], F32) for r in range(NS)]
    t_xf = [C.sb(f"txf{r}", [128, 64], F32) for r in range(NS)] if not fused else None
    t_dtc = [C.sb(f"tdtc{r}", [128, 1], F32) for r in range(NS)] if fused else None
    ps_dtc = [bB[r][:, 384:385] for r in range(NS)]
    ps_y2 = [bB[r][0:64, 256:384] for r in range(NS)]
    t_y2 = [C.sb(f"ty2{r}", [64, 128], F32) for r in range(NS)] if fused else None

    tmm = C.sem("tmm")
    sst = [C.sem(f"sst{r}") for r in range(NS)]
    wxf = [C.sem(f"wxf{r}") for r in range(NS)]

    done_tok = {}
    MM_PER_IT = 6
    it = 0
    dtw = C.sem("dtw") if fused else None
    if fused:
        BLK = fz.get("BLK", 2048)
        CPB = BLK // 128
    for c in range(NCH):
        if fused and c % CPB == 0:
            bl = c // CPB
            C.dyn_dma(dtrows[0:1, :], dtfmg_t, fz["CFG_DT0"],
                      [[1, NU * BLK]],
                      dtw, wait=[(tmm, tmm.n)] + (list(fz["l3_entry"]) if c == 0 else []),
                      addc=bl * 32 * BLK)
        for u in range(NU):
            r = it % NS
            guard = [done_tok[it - NS]] if it >= NS else [a_ready, h0_ready]
            if not fused:
                C.rec("sync", lambda e, r=r, u=u, c=c:
                      e.dma_start(out=t_xf[r][:, :], in_=xs_f_d[:, u, c, :]),
                      wait=guard + [(sst[r], 16 * (it // NS))], inc=((wxf[r], 16),))
            if fused:
                C.rec("tensor", lambda t, r=r, u=u, c=c:
                      t.matmul(ps_dtc[r][:, :],
                               dtrows[0:1, u * BLK + (c % CPB) * 128:
                                      u * BLK + (c % CPB) * 128 + 128],
                               cn["ones_c"][0:1, 0:1], start=True, stop=True),
                      wait=guard + [(ld, ld.n), (dtw, dtw.n), (ldq, 16)],
                      inc=((tmm, 1),))
                C.rec("vector", lambda e, r=r:
                      e.tensor_copy(t_dtc[r][:, :], ps_dtc[r][:, :]),
                      wait=[(tmm, tmm.n)], nops=1)
                C.rec("tensor", lambda t, r=r:
                      t.matmul(ps_cd[r][:, :], tri[:, :], t_dtc[r][:, :],
                               start=True, stop=True),
                      wait=[(VC, VC.n)], inc=((tmm, 1),))
            else:
                C.rec("tensor", lambda t, r=r, u=u, c=c:
                      t.matmul(ps_cd[r][:, :], tri[:, :], dttok[:, u, c:c + 1],
                               start=True, stop=True),
                      wait=guard + [(ld, ld.n)], inc=((tmm, 1),))
            C.rec("vector", lambda e, r=r, u=u:
                  e.tensor_scalar(t_lp[r][:, :], ps_cd[r][:, :],
                                  avals[:, u:u + 1], None, OP.mult),
                  wait=[(tmm, tmm.n), a_ready] + guard, nops=1)
            C.rec("tensor", lambda t, r=r:
                  t.matmul(ps_row[r][:, :], t_lp[r][:, :], i128[:, :],
                           start=True, stop=True),
                  wait=[(VC, VC.n)], inc=((tmm, 1),))
            def rowops(e, r=r):
                e.tensor_copy(t_lr[r][:, :], ps_row[r][:, :])
                e.tensor_scalar(t_ln[r][:, :], ps_row[r][:, :],
                                cn["cst"][0:1, 7:8], None, OP.mult)
                return e
            C.rec("vector", lambda e, r=r: (
                e.tensor_copy(t_lr[r][:, :], ps_row[r][:, :]),
                e.tensor_scalar(t_ln[r][:, :], ps_row[r][:, :],
                                cn["cst"][0:1, 7:8], None, OP.mult))[-1],
                wait=[(tmm, tmm.n)], nops=2)
            def mm_delta(t, r=r, c=c):
                t.matmul(ps_d[r][:, :], t_ln[r][:, :], cn["ones_r"][:, :],
                         start=True, stop=False)
                t.matmul(ps_d[r][:, :], cn["ones_r"][:, :], t_lr[r][:, :],
                         start=False, stop=True)
                return t.matmul(ps_gt[r][:, :], bfm[:, c * L:(c + 1) * L],
                                cfm[:, c * L:(c + 1) * L], start=True, stop=True)
            C.rec("tensor", mm_delta, wait=[(VC, VC.n)], inc=((tmm, 1),))
            C.rec("vector", lambda e, r=r:
                  e.tensor_tensor(t_dm[r][:, :], ps_d[r][:, :], mask[:, :], OP.add),
                  wait=[(tmm, tmm.n)], nops=1)
            def acts1(e, r=r):
                e.activation(t_e[r][:, :], t_dm[r][:, :], AF.Exp)
                return e.activation(t_pr[r][:, :], t_lr[r][:, :], AF.Exp)
            C.rec("scalar", acts1, wait=[(VC, VC.n)], nops=2)
            dtsc = (t_dtc[r][:, :] if fused else dttok[:, u, c:c + 1])
            def wv1(e, r=r, dtsc=dtsc):
                e.tensor_tensor(t_wt[r][:, :], t_e[r][:, :], ps_gt[r][:, :],
                                OP.mult)
                return e.tensor_scalar(t_wtb[r][:, :], t_wt[r][:, :],
                                       dtsc, None, OP.mult)
            C.rec("vector", wv1, wait=[(SC, SC.n)], nops=2)
            C.rec("scalar", lambda e, r=r:
                  e.activation(t_wc[r][:, :], t_dm[r][:, 127:128], AF.Exp),
                  wait=[(VC, VC.n)], nops=1)
            def wv2(e, r=r, u=u, c=c):
                e.tensor_scalar(t_wc[r][:, :], t_wc[r][:, :],
                                dttok[:, u, c:c + 1], None, OP.mult)
                e.tensor_scalar(t_xw[r][:, :], xsl(u, c),
                                t_wc[r][:, :], None, OP.mult)
                return e
            C.rec("vector", lambda e, r=r, u=u, c=c, dtsc=dtsc: (
                e.tensor_scalar(t_wc[r][:, :], t_wc[r][:, :],
                                dtsc, None, OP.mult),
                e.tensor_scalar(t_xw[r][:, :], xsl(u, c),
                                t_wc[r][:, :], None, OP.mult))[-1],
                wait=[(SC, SC.n)], nops=2)
            def mm_p(t, r=r):
                t.matmul(ps_pb[r][:, :], cn["ones_r"][0:1, 0:64], t_pr[r][:, :],
                         start=True, stop=True)
                return t.matmul(ps_pl[r][:, :], cn["ones_r"][0:1, 0:64],
                                t_pr[r][:, 127:128], start=True, stop=True)
            C.rec("tensor", mm_p, wait=[(VC, VC.n)], inc=((tmm, 1),))
            def wv3(e, r=r, c=c):
                e.tensor_tensor(t_cb[r][:, :], cfm[:, c * L:(c + 1) * L],
                                ps_pb[r][:, :], OP.mult)
                return e.tensor_copy(t_pl[r][:, :], ps_pl[r][:, :])
            C.rec("vector", wv3, wait=[(tmm, tmm.n)], nops=2)
            if fused:
                def mm_y(t, r=r, u=u, c=c):
                    t.matmul(ps_y2[r][:, :], xsl(u, c), t_wtb[r][:, :],
                             start=True, stop=False)
                    t.matmul(ps_y2[r][:, :], hTb[u][:, :], t_cb[r][:, :],
                             start=False, stop=True)
                    return t.matmul(ps_s[r][:, :], btok[:, c, :], t_xw[r][:, :],
                                    start=True, stop=True)
            else:
                def mm_y(t, r=r, u=u, c=c):
                    t.matmul(ps_y[r][:, :], t_wtb[r][:, :], xsl(u, c),
                             start=True, stop=False)
                    t.matmul(ps_y[r][:, :], t_cb[r][:, :], hTb[u][:, :],
                             start=False, stop=True)
                    return t.matmul(ps_s[r][:, :], btok[:, c, :], t_xw[r][:, :],
                                    start=True, stop=True)
            C.rec("tensor", mm_y, wait=[(VC, VC.n)], inc=((tmm, 1),))
            if fused:
                def wv4(e, r=r, u=u):
                    e.tensor_copy(t_y2[r][:, :], ps_y2[r][:, :])
                    e.scalar_tensor_tensor(hT[u][:, :], hT[u][:, :], t_pl[r][:, :],
                                           ps_s[r][:, :], OP.mult, OP.add)
                    return e.tensor_copy(hTb[u][:, :], hT[u][:, :])
                C.rec("vector", wv4, wait=[(tmm, tmm.n),
                                           (sst[r], 16 * (it // NS))], nops=3)
                done_tok[it] = (VC, VC.n)
                C.rec("gpsimd", lambda e, r=r, u=u, c=c:
                      e.dma_start(out=y3_ap[64 * u:64 * u + 64,
                                           c * 128:(c + 1) * 128],
                                  in_=t_y2[r][:, :]),
                      wait=[(VC, VC.n)], inc=((sst[r], 16),))
            else:
                def wv4(e, r=r, u=u):
                    e.scalar_tensor_tensor(t_y[r][:, :], t_xf[r][:, :],
                                           dvec[:, u:u + 1], ps_y[r][:, :],
                                           OP.mult, OP.add)
                    e.scalar_tensor_tensor(hT[u][:, :], hT[u][:, :], t_pl[r][:, :],
                                           ps_s[r][:, :], OP.mult, OP.add)
                    return e.tensor_copy(hTb[u][:, :], hT[u][:, :])
                C.rec("vector", wv4, wait=[(tmm, tmm.n), (wxf[r], wxf[r].n)], nops=3)
                done_tok[it] = (VC, VC.n)
                C.rec("gpsimd", lambda e, r=r, u=u, c=c:
                      e.dma_start(out=y_o[:, u, c, :], in_=t_y[r][:, :]),
                      wait=[(VC, VC.n)], inc=((sst[r], 16),))
            it += 1

    if fused:
        return [(s, s.n) for s in sst]
    C.rec("gpsimd", lambda e: e.nop(), wait=[(s, s.n) for s in sst])



def emit_launch4(C, Tfull, NU2=4, fused=False, fz=None):
    NST = Tfull // 128
    NQB = Tfull // 512
    if not fused:
        q_d = C.inp("q_in", (64, NU2, Tfull), BF16)
        k_d = C.inp("k_in", (64, Tfull), BF16)
        vo_d = C.inp("vo_tok", (128, NST, 65), BF16)
        y_o = C.out("y_mqa", (64, NU2, Tfull), F32)
    else:
        qkvg_t = fz["qkvg"]
        y4_ap = fz["y4"]
        entry = list(fz["l4_entry"])
        BLK = fz.get("BLK", 2048)
        NBL = Tfull // BLK

    cn = make_consts(C)
    masks = []
    for j in range(4):
        m = (np.arange(128)[:, None] + 128 * j <= np.arange(512)[None, :])
        masks.append(C.const(f"mq{j}", m.astype(ml_dtypes.bfloat16)))

    ld = C.sem("ld")
    q_sb = C.sb("q_sb", [64, NU2, Tfull], BF16)
    k_sb = C.sb("k_sb", [64, Tfull], BF16)
    vo_sb = C.sb("vo_sb", [128, NST, 65], BF16)
    msk = [C.sb(f"msk{j}", [128, 512], BF16) for j in range(4)]
    if not fused:
        for u in range(NU2):
            C.dma("sync", q_sb[:, u, :], q_d[:, u, :], ld)
        C.dma("sync", k_sb[:, :], k_d, ld)
        C.dma("sync", vo_sb[:, :, :], vo_d, ld)
    else:
        ldq = C.sem("l4ldq")
        for u in range(NU2):
            C.dyn_dma(q_sb[:, u, :], qkvg_t, fz["CFG_Q"],
                      [[BLK, 64], [1152 * BLK, NBL], [1, BLK]], ldq,
                      wait=(entry if u == 0 else []), addc=u * 64 * BLK)
        vfm = C.sb("l4_vfm", [64, Tfull], BF16)
        for bl in range(NBL):
            C.rec("sync", lambda e, bl=bl: e.dma_start(
                out=k_sb[:, bl * BLK:(bl + 1) * BLK],
                in_=bass.AP(qkvg_t, (bl * 1152 + 1024) * BLK,
                            [[BLK, 64], [1, BLK]])),
                wait=entry, inc=((ld, 16),))
            C.rec("sync", lambda e, bl=bl: e.dma_start(
                out=vfm[:, bl * BLK:(bl + 1) * BLK],
                in_=bass.AP(qkvg_t, (bl * 1152 + 1088) * BLK,
                            [[BLK, 64], [1, BLK]])),
                wait=entry, inc=((ld, 16),))
    for j in range(4):
        C.dma("sync", msk[j][:, :], masks[j], ld)

    VC, SC = C.ch("vector"), C.ch("scalar")
    tmm = C.sem("tmm")
    sst = [C.sem(f"sst{i}") for i in range(2)]
    if fused:
        ib4_d = C.const("l4_ib_c", np.eye(128).astype(ml_dtypes.bfloat16))
        ib4 = C.sb("l4_ib", [128, 128], BF16)
        C.dma("sync", ib4[:, :], ib4_d, ld)
        pvt = C.ps("l4_pvt", [128, 256], BF16)
        C.rec("vector", lambda e: e.memset(vo_sb[:, :, 64:65], 1.0), nops=1)
        ttm4 = C.sem("ttm4")
        for st in range(NST):
            sl = st % 2
            C.rec("tensor", lambda t, st=st, sl=sl:
                  t.transpose(pvt[:, sl * 128:sl * 128 + 64],
                              vfm[:, st * 128:(st + 1) * 128], ib4[0:64, 0:64]),
                  wait=[(ld, ld.n), (VC, VC.n)], inc=((ttm4, 1),))
            C.rec("vector", lambda e, st=st, sl=sl:
                  e.tensor_copy(vo_sb[:, st, 0:64],
                                pvt[:, sl * 128:sl * 128 + 64]),
                  wait=[(ttm4, ttm4.n)], nops=1)

    sq = C.sb("sq", [64, 512], F32)
    srow = C.sb("srow", [1, Tfull], F32)
    stat = C.sb("stat", [1, 16], F32)
    cb = C.sb("cbias", [128, NU2], F32)
    ps_st = C.ps("ps_st", [1, 512])
    ps_cb = C.ps("ps_cb", [128, NU2])

    exq = [(ldq, ldq.n)] if fused else []

    def norm_row(idx, src_ap):
        for st in range(Tfull // 512):
            C.rec("scalar", lambda e, st=st:
                  e.activation(sq[:, :], src_ap[:, st * 512:(st + 1) * 512],
                               AF.Square),
                  wait=[(ld, ld.n), cn["ld"], (VC, VC.n)] + exq, nops=1)
            C.rec("tensor", lambda t, st=st:
                  t.matmul(ps_st[:, :], cn["ones_c"][0:64, 0:1], sq[:, :],
                           start=True, stop=True),
                  wait=[(SC, SC.n)], inc=((tmm, 1),))
            C.rec("vector", lambda e, st=st:
                  e.tensor_copy(srow[:, st * 512:(st + 1) * 512], ps_st[:, :]),
                  wait=[(tmm, tmm.n)], nops=1)
        C.rec("vector", lambda e, idx=idx:
              e.reduce_max(stat[0:1, idx:idx + 1], srow[:, :], AX.X, None),
              wait=[(VC, VC.n)], nops=1)

    for u in range(NU2):
        norm_row(u, q_sb[:, u, :])
    norm_row(8, k_sb[:, :])
    def sqrts(e):
        e.activation(stat[0:1, 0:NU2], stat[0:1, 0:NU2], AF.Sqrt)
        return e.activation(stat[0:1, 8:9], stat[0:1, 8:9], AF.Sqrt)
    C.rec("scalar", sqrts, wait=[(VC, VC.n)], nops=2)

    def cb_ops(e):
        e.tensor_scalar(stat[0:1, 0:NU2], stat[0:1, 0:NU2],
                        stat[0:1, 8:9], None, OP.mult)
        return e.tensor_scalar(stat[0:1, 0:NU2], stat[0:1, 0:NU2],
                               cn["cst"][0:1, 3:4], None, OP.mult)
    C.rec("vector", cb_ops, wait=[(SC, SC.n)], nops=2)
    C.rec("tensor", lambda t: t.matmul(ps_cb[:, :], cn["ones_r"][:, :],
                                       stat[0:1, 0:NU2], start=True, stop=True),
          wait=[(VC, VC.n)], inc=((tmm, 1),))
    C.rec("vector", lambda e: e.tensor_scalar(cb[:, :], ps_cb[:, :],
                                              cn["cst"][:, 7:8], None, OP.mult),
          wait=[(tmm, tmm.n)], nops=1)
    cb_ready = (VC, VC.n)

    ps_sc = [C.ps(f"ps_sc{i}", [128, 512]) for i in range(2)]
    ps_y = [C.ps(f"ps_y{i}", [65, 512]) for i in range(2)]
    ps_rb = C.ps("ps_rb", [64, 512])
    att = [C.sb(f"att{i}", [128, 512], BF16) for i in range(2)]
    rz = C.sb("rz", [1, 512], F32)
    yu = [C.sb(f"yu{i}", [64, 512], F32) for i in range(2)]
    ye = [C.sb(f"ye{i}", [64, 512], F32) for i in range(2)]

    att_free = {}
    it = 0
    for u in range(NU2):
        for qb in range(NQB):
            qs = qb * 512
            nst = 4 * qb + 4
            ypar = it % 2
            for st in range(nst):
                sl = st % 2

                def mm_sc(t, sl=sl, st=st, u=u, qs=qs):
                    return t.matmul(ps_sc[sl][:, :], k_sb[:, st * 128:(st + 1) * 128],
                                    q_sb[:, u, qs:qs + 512], start=True, stop=True)
                C.rec("tensor", mm_sc, wait=[cb_ready, (SC, SC.n)],
                      inc=((tmm, 1),))
                j = st - 4 * qb
                C.rec("scalar", lambda e, sl=sl, u=u:
                      e.activation(att[sl][:, :], ps_sc[sl][:, :], AF.Exp,
                                   bias=cb[:, u:u + 1], scale=cn["cst"][:, 3:4]),
                      wait=[(tmm, tmm.n)], nops=1)
                nv = 0
                if 0 <= j < 4:
                    C.rec("vector", lambda e, sl=sl, j=j:
                          e.tensor_tensor(att[sl][:, :], att[sl][:, :],
                                          msk[j][:, :], OP.mult),
                          wait=[(SC, SC.n)], nops=1)
                    nv = 1
                rdy = (VC, VC.n) if nv else (SC, SC.n)

                def mm_av(t, sl=sl, st=st, ypar=ypar, first=(st == 0),
                          last=(st == nst - 1)):
                    return t.matmul(ps_y[ypar][:, :], vo_sb[:, st, :],
                                    att[sl][:, :], start=first, stop=last,
                                    skip_group_check=True)
                C.rec("tensor", mm_av, wait=[rdy], inc=((tmm, 1),))
                att_free[(u, qb, st)] = ((tmm, tmm.n), None)
            C.rec("vector", lambda e, ypar=ypar:
                  e.reciprocal(rz[:, :], ps_y[ypar][64:65, :]),
                  wait=[(tmm, tmm.n), (sst[ypar], 16 * (it // 2))], nops=1)
            C.rec("vector", lambda e, ypar=ypar:
                  e.tensor_copy(ye[ypar][:, :], ps_y[ypar][0:64, :]),
                  wait=[], nops=1)
            C.rec("tensor", lambda t: t.matmul(ps_rb[:, :], cn["ones_r"][0:1, 0:64],
                                               rz[:, :], start=True, stop=True),
                  wait=[(VC, VC.n)], inc=((tmm, 1),))
            C.rec("vector", lambda e, ypar=ypar:
                  e.tensor_tensor(yu[ypar][:, :], ye[ypar][:, :], ps_rb[:, :],
                                  OP.mult),
                  wait=[(tmm, tmm.n)], nops=1)
            if fused:
                C.rec("gpsimd", lambda e, ypar=ypar, u=u, qs=qs:
                      e.dma_start(out=y4_ap[u * 64:(u + 1) * 64, qs:qs + 512],
                                  in_=yu[ypar][:, :]),
                      wait=[(VC, VC.n)], inc=((sst[ypar], 16),))
            else:
                C.rec("gpsimd", lambda e, ypar=ypar, u=u, qs=qs:
                      e.dma_start(out=y_o[:, u, qs:qs + 512], in_=yu[ypar][:, :]),
                      wait=[(VC, VC.n)], inc=((sst[ypar], 16),))
            it += 1

    if fused:
        return [(s, s.n) for s in sst]
    C.rec("gpsimd", lambda e: e.nop(), wait=[(s, s.n) for s in sst])


def stage_cast(C, uid, src_d, dst_d, dim, ntok, wait=()):
    nk = dim // 128
    tt = 512
    ntt = cdiv(ntok, tt)
    sa = fm3(src_d)
    sd = fm3(dst_d)
    ab = [C.sbs(f"cs{uid}_a{i}", [128, nk, tt], F32) for i in range(2)]
    ob = [C.sbs(f"cs{uid}_o{i}", [128, nk, tt], BF16) for i in range(2)]
    wl = [C.sem(f"cs{uid}_w{i}") for i in range(2)]
    st = [C.sem(f"cs{uid}_s{i}") for i in range(2)]
    VC = C.ch("vector")
    tk = {}
    for ti, (t0, tn) in enumerate(mk_tiles(ntok, tt)):
        par = ti % 2
        lw = list(wait) if ti == 0 else []
        if ti >= 2:
            lw.append(tk[ti - 2])
        C.rec("sync", lambda e, a=ab[par], t0=t0, tn=tn:
              e.dma_start(out=a[:, :, 0:tn], in_=sa[:, :, t0:t0 + tn]),
              wait=lw, inc=((wl[par], 16),))
        C.rec("vector", lambda e, a=ab[par], o=ob[par], tn=tn:
              e.tensor_copy(o[:, :, 0:tn], a[:, :, 0:tn]),
              wait=[(wl[par], wl[par].n), (st[par], 16 * (ti // 2))], nops=1)
        tk[ti] = (VC, VC.n)
        C.rec("sync", lambda e, o=ob[par], t0=t0, tn=tn:
              e.dma_start(out=sd[:, :, t0:t0 + tn], in_=o[:, :, 0:tn]),
              wait=[(VC, VC.n)], inc=((st[par], 16),))
    return [(s, s.n) for s in st]


def stage_muladd(C, uid, s_d, g_d, x_d, dst_d, dim, ntok, wait=()):
    nk = dim // 128
    tt = 512
    ntt = cdiv(ntok, tt)
    ss = fm3(s_d)
    sg = fm3(g_d)
    sx = fm3(x_d)
    sd = fm3(dst_d)
    ab = [C.sbs(f"ma{uid}_a{i}", [128, nk, tt], F32) for i in range(2)]
    bb = [C.sbs(f"ma{uid}_b{i}", [128, nk, tt], F32) for i in range(2)]
    cb = [C.sbs(f"ma{uid}_c{i}", [128, nk, tt], F32) for i in range(2)]
    ob = [C.sbs(f"ma{uid}_o{i}", [128, nk, tt], F32) for i in range(2)]
    wl = [C.sem(f"ma{uid}_w{i}") for i in range(2)]
    st = [C.sem(f"ma{uid}_s{i}") for i in range(2)]
    VC = C.ch("vector")
    tk = {}
    for ti, (t0, tn) in enumerate(mk_tiles(ntok, tt)):
        par = ti % 2
        lw = list(wait) if ti == 0 else []
        if ti >= 2:
            lw.append(tk[ti - 2])
        for src, buf in ((ss, ab[par]), (sg, bb[par]), (sx, cb[par])):
            C.rec("sync", lambda e, src=src, buf=buf, t0=t0, tn=tn:
                  e.dma_start(out=buf[:, :, 0:tn], in_=src[:, :, t0:t0 + tn]),
                  wait=lw, inc=((wl[par], 16),))
            lw = []
        def ops(e, a=ab[par], b=bb[par], c=cb[par], o=ob[par], tn=tn):
            e.tensor_tensor(o[:, :, 0:tn], a[:, :, 0:tn], b[:, :, 0:tn], OP.mult)
            return e.tensor_tensor(o[:, :, 0:tn], o[:, :, 0:tn], c[:, :, 0:tn],
                                   OP.add)
        C.rec("vector", ops,
              wait=[(wl[par], wl[par].n), (st[par], 16 * (ti // 2))], nops=2)
        tk[ti] = (VC, VC.n)
        C.rec("sync", lambda e, o=ob[par], t0=t0, tn=tn:
              e.dma_start(out=sd[:, :, t0:t0 + tn], in_=o[:, :, 0:tn]),
              wait=[(VC, VC.n)], inc=((st[par], 16),))
    return [(s, s.n) for s in st]



def emit_launch5(C, nt):
    y_d = C.inp("y_mqa_f", (D, nt), F32)
    x2_d = C.inp("x2", (D, nt), F32)
    wproj_d = C.inp("w_proj", (D, D), F32)
    x3_o = C.out("x3", (D, nt), F32)
    yb_d = C.scratch("y_bf", (D, nt), BF16)
    cn = make_consts(C)
    C.sb_push()
    b1 = stage_cast(C, "y", y_d, yb_d, D, nt)
    C.sb_pop()
    C.sb_push()
    epi = mk_epi(C, "p", dict(dst_f=x3_o, aux="add"))
    b2 = stage_proj(C, "p", yb_d, wproj_d, D, D, nt, epi, aux_d=x2_d, aux_dim=D,
                    wait=b1)
    C.sb_pop()
    C.rec("gpsimd", lambda e: e.nop(), wait=b2)
    return b2



def emit_launch6(C, nt):
    ntok = nt + 1
    x3_d = C.inp("x3h", (D, ntok), F32)
    x_d = C.inp("x_in", (D, nt), F32)
    xnbf_d = C.inp("xn_bf_in", (D, nt), BF16)
    maak_d = C.inp("maa_k_r", (128, 8), F32)
    maar_d = C.inp("maa_r_r", (128, 8), F32)
    wkey_d = C.inp("w_key", (D, 4096), F32)
    wrec_d = C.inp("w_recept", (D, D), F32)
    wval_d = C.inp("w_value", (4096, D), F32)
    w1_d = C.inp("w1T", (D, 4096), F32)
    w2_d = C.inp("w2T", (D, 4096), F32)
    h1_d = C.inp("h1T", (4096, 4096), F32)
    h2_d = C.inp("h2T", (4096, 4096), F32)
    w3_d = C.inp("w3T", (4096, D), F32)
    s1_d = C.inp("w1s_r", (128, 32), F32)
    s2_d = C.inp("w2s_r", (128, 32), F32)
    sh1_d = C.inp("h1s_r", (128, 32), F32)
    sh2_d = C.inp("h2s_r", (128, 32), F32)
    s3_d = C.inp("w3s_r", (128, 8), F32)

    xl1_o = C.out("x_l1", (D, nt), F32)
    r3_d = C.scratch("r3_f", (D, ntok), F32)
    xk_d = C.scratch("xk_bf", (D, nt), BF16)
    xr_d = C.scratch("xr_bf", (D, nt), BF16)
    sig_d = C.scratch("sig_f", (D, nt), F32)
    kk_d = C.scratch("kk_bf", (4096, nt), BF16)
    s_d = C.scratch("s_f", (D, nt), F32)
    t1_d = C.scratch("t1_f", (4096, nt), F32)
    hh_d = C.scratch("h_bf", (4096, nt), BF16)
    g1_d = C.scratch("g1_bf", (4096, nt), BF16)
    g2_d = C.scratch("g2_bf", (4096, nt), BF16)
    g_d = C.scratch("g_f", (D, nt), F32)

    cn = make_consts(C)
    ld = C.sem("ld")
    maak = C.sb("maak", [128, 8], F32)
    maar = C.sb("maar", [128, 8], F32)
    s1s = C.sb("s1s", [128, 32], F32)
    s2s = C.sb("s2s", [128, 32], F32)
    sh1s = C.sb("sh1s", [128, 32], F32)
    sh2s = C.sb("sh2s", [128, 32], F32)
    s3s = C.sb("s3s", [128, 8], F32)
    for sb_, d_ in ((maak, maak_d), (maar, maar_d), (s1s, s1_d), (s2s, s2_d),
                    (sh1s, sh1_d), (sh2s, sh2_d), (s3s, s3_d)):
        C.dma("sync", sb_[:, :], d_, ld)

    C.sb_push()
    b1 = stage_rms(C, "r3", x3_d, ntok, D, dst_f_d=r3_d, wait=[ld])
    C.sb_pop()

    C.sb_push()
    sr3 = fm3(r3_d)
    sxk = fm3(xk_d)
    sxr = fm3(xr_d)
    tt = 512
    ntt = cdiv(nt, tt)
    rb = [C.sbs(f"km_a{i}", [128, 8, tt + 1], F32) for i in range(2)]
    xxb = C.sbs("km_xx", [128, 8, tt], F32)
    okb = [C.sbs(f"km_k{i}", [128, 8, tt], BF16) for i in range(2)]
    orb = [C.sbs(f"km_r{i}", [128, 8, tt], BF16) for i in range(2)]
    wl = [C.sem(f"km_w{i}") for i in range(2)]
    stk = [C.sem(f"km_s{i}") for i in range(2)]
    VC, SC = C.ch("vector"), C.ch("scalar")
    tkx = {}
    for ti, (t0, tn) in enumerate(mk_tiles(nt, tt)):
        par = ti % 2
        lw = list(b1) + [ld] if ti == 0 else []
        if ti >= 2:
            lw.append(tkx[ti - 2])
        C.rec("sync", lambda e, a=rb[par], t0=t0, tn=tn:
              e.dma_start(out=a[:, :, 0:tn + 1], in_=sr3[:, :, t0:t0 + tn + 1]),
              wait=lw, inc=((wl[par], 16),))
        def ops(e, a=rb[par], ok=okb[par], orr=orb[par], tn=tn):
            ins = None
            for c in range(8):
                cur = a[:, c, 1:1 + tn]
                prv = a[:, c, 0:tn]
                e.tensor_tensor(xxb[:, c, 0:tn], prv, cur, OP.subtract)
                e.scalar_tensor_tensor(ok[:, c, 0:tn], xxb[:, c, 0:tn],
                                       maak[:, c:c + 1], cur, OP.mult, OP.add)
                ins = e.scalar_tensor_tensor(orr[:, c, 0:tn], xxb[:, c, 0:tn],
                                             maar[:, c:c + 1], cur, OP.mult, OP.add)
            return ins
        C.rec("vector", ops,
              wait=[(wl[par], wl[par].n), (stk[par], 32 * (ti // 2))], nops=24)
        tkx[ti] = (VC, VC.n)
        C.rec("sync", lambda e, ok=okb[par], t0=t0, tn=tn:
              e.dma_start(out=sxk[:, :, t0:t0 + tn], in_=ok[:, :, 0:tn]),
              wait=[(VC, VC.n)], inc=((stk[par], 16),))
        C.rec("sync", lambda e, orr=orb[par], t0=t0, tn=tn:
              e.dma_start(out=sxr[:, :, t0:t0 + tn], in_=orr[:, :, 0:tn]),
              wait=[(VC, VC.n)], inc=((stk[par], 16),))
    b2 = [(s, s.n) for s in stk]
    C.sb_pop()

    C.sb_push()
    epi_r = mk_epi(C, "rc", dict(dst_f=sig_d, act=AF.Sigmoid))
    b3 = stage_proj(C, "rc", xr_d, wrec_d, D, D, nt, epi_r, wait=b2)
    def post_k(e, eb, info):
        return e.tensor_scalar(eb[0:info["msz"], 0:info["tn"]],
                               eb[0:info["msz"], 0:info["tn"]],
                               cn["cst"][0:info["msz"], 4:5],
                               cn["cst"][0:info["msz"], 4:5], OP.mult, OP.add)
    epi_k = mk_epi(C, "kk", dict(dst_bf=kk_d, act=AF.Erf,
                                 scale_cols=(8, 8), bias_cols=(9, 9),
                                 post=post_k, post_n=1))
    b4 = stage_proj(C, "kk", xk_d, wkey_d, D, 4096, nt, epi_k, wait=b2)
    C.sb_pop()
    C.sb_push()
    epi_v = mk_epi(C, "vv", dict(dst_f=s_d, aux="mul", aux2="add"))
    b5 = stage_proj(C, "vv", kk_d, wval_d, 4096, D, nt, epi_v,
                    aux_d=sig_d, aux_dim=D, aux2_d=x3_d, aux2_dim=D,
                    aux2_off=1, wait=b3 + b4, tg=min(nt, 1024), wslab=256)
    C.sb_pop()

    C.sb_push()
    epi_t1 = mk_epi(C, "t1", dict(dst_f=t1_d, act=AF.Silu, scale_sb=s1s))
    b6 = stage_proj(C, "t1", xnbf_d, w1_d, D, 4096, nt, epi_t1, wait=b5 + [ld])
    C.sb_pop()
    C.sb_push()
    epi_t2 = mk_epi(C, "t2", dict(dst_bf=hh_d, scale_sb=s2s, aux="mul"))
    b7 = stage_proj(C, "t2", xnbf_d, w2_d, D, 4096, nt, epi_t2,
                    aux_d=t1_d, aux_dim=4096, wait=b6, tg=min(nt, 512), wslab=1024)
    C.sb_pop()
    C.sb_push()
    epi_h1 = mk_epi(C, "h1", dict(dst_bf=g1_d, act=AF.Gelu, scale_sb=sh1s))
    b8 = stage_proj(C, "h1", hh_d, h1_d, 4096, 4096, nt, epi_h1,
                    tg=min(nt, 1024), wslab=512, wait=b7)
    C.sb_pop()
    C.sb_push()
    epi_h2 = mk_epi(C, "h2", dict(dst_bf=g2_d, act=AF.Gelu, scale_sb=sh2s))
    b9 = stage_proj(C, "h2", g1_d, h2_d, 4096, 4096, nt, epi_h2,
                    tg=min(nt, 1024), wslab=512, wait=b8)
    C.sb_pop()
    C.sb_push()
    epi_w3 = mk_epi(C, "w3", dict(dst_f=g_d, act=AF.Silu, scale_sb=s3s))
    b10 = stage_proj(C, "w3", g2_d, w3_d, 4096, D, nt, epi_w3, wait=b9,
                     tg=min(nt, 1024), wslab=512)
    C.sb_pop()
    C.sb_push()
    b11 = stage_muladd(C, "fin", s_d, g_d, x_d, xl1_o, D, nt, wait=b5 + b10)
    C.sb_pop()
    C.rec("gpsimd", lambda e: e.nop(), wait=b11)
    return b11



def emit_launch7(C, nt):
    xl1_d = C.inp("x_l1_in", (D, nt), F32)
    wxdt_d = C.inp("w_xdtT", (D, 64), F32)
    bxdt_d = C.inp("b_xdt_r", (64, 1), F32)
    wdt_d = C.inp("w_dtT", (64, D), F32)
    bdt_d = C.inp("b_dt_r", (128, 8), F32)
    wb_d = C.inp("w_bT", (D, 16), F32)
    bb_d = C.inp("b_b_r", (16, 1), F32)
    wc_d = C.inp("w_cT", (D, 16), F32)
    bc_d = C.inp("b_c_r", (16, 1), F32)

    xn1f_o = C.out("xn1_f", (D, nt), F32)
    xn1b_o = C.out("xn1_bf", (D, nt), BF16)
    dt_o = C.out("dt1", (D, nt), F32)
    bt_o = C.out("bt1", (16, nt), F32)
    ct_o = C.out("ct1", (16, nt), F32)
    xdt_d = C.scratch("xdt_bf", (64, nt), BF16)

    cn = make_consts(C)
    ld = C.sem("ld")
    bxdt = C.sb("bxdt", [64, 1], F32)
    bdt = C.sb("bdt", [128, 8], F32)
    bbs = C.sb("bbs", [16, 1], F32)
    bcs = C.sb("bcs", [16, 1], F32)
    for sb_, d_ in ((bxdt, bxdt_d), (bdt, bdt_d), (bbs, bb_d), (bcs, bc_d)):
        C.dma("sync", sb_[:, :], d_, ld)

    C.sb_push()
    b1 = stage_rms(C, "ln", xl1_d, nt, D, dst_bf_d=xn1b_o, dst_f_d=xn1f_o,
                   with_mean=True, wait=[ld])
    C.sb_pop()
    C.sb_push()
    epi_x = mk_epi(C, "xd", dict(dst_bf=xdt_d, bias_sb=bxdt))
    b2 = stage_proj(C, "xd", xn1b_o, wxdt_d, D, 64, nt, epi_x, wait=b1)
    C.sb_pop()
    C.sb_push()
    epi_dt = mk_epi(C, "dt", dict(softplus=True, bias_sb=bdt, dst_f=dt_o))
    b3 = stage_proj(C, "dt", xdt_d, wdt_d, 64, D, nt, epi_dt, wait=b2)
    C.sb_pop()
    C.sb_push()
    epi_b = mk_epi(C, "bt", dict(bias_sb=bbs, dst_f=bt_o))
    b4 = stage_proj(C, "bt", xn1b_o, wb_d, D, 16, nt, epi_b, wait=b3)
    epi_c = mk_epi(C, "ct", dict(bias_sb=bcs, dst_f=ct_o))
    b5 = stage_proj(C, "ct", xn1b_o, wc_d, D, 16, nt, epi_c, wait=b3)
    C.sb_pop()
    C.rec("gpsimd", lambda e: e.nop(), wait=b3 + b4 + b5)
    return b3 + b4 + b5



def emit_launch8(C, Tseg, nseg=2):
    TT_ = Tseg * nseg
    u_d = C.inp("u_sl", (128, TT_), F32)
    dt_d = C.inp("dt_sl", (128, TT_), F32)
    a_d = C.inp("a_sl", (128, 16), F32)
    dv_d = C.inp("d_sl", (128, 1), F32)
    bt_d = C.inp("bt_in", (16, TT_), F32)
    ct_d = C.inp("ct_in", (16, TT_), F32)
    y_o = C.out("y1_sl", (128, TT_), F32)

    cn = make_consts(C)
    sel_np = np.zeros((16, 16 * 128), np.float32)
    for n in range(16):
        sel_np[n, n * 128:(n + 1) * 128] = 1.0
    sel_d = C.const("sel_c", sel_np)

    ld = C.sem("ld")
    avals = C.sb("avals", [128, 16], F32)
    dvs = C.sb("dvs", [128, 1], F32)
    sel = C.sb("sel", [16, 16 * 128], F32)
    C.dma("sync", avals[:, :], a_d, ld)
    C.dma("sync", dvs[:, :], dv_d, ld)
    C.dma("sync", sel[:, :], sel_d, ld)
    VC, SC = C.ch("vector"), C.ch("scalar")
    C.rec("scalar", lambda e: e.activation(avals[:, :], avals[:, :], AF.Exp),
          wait=[(ld, ld.n), cn["ld"]], nops=1)
    C.rec("vector", lambda e: e.tensor_scalar(avals[:, :], avals[:, :],
                                              cn["cst"][:, 7:8], None, OP.mult),
          wait=[(SC, SC.n)], nops=1)
    a_ready = (VC, VC.n)

    tt = 512
    NSL = 2
    ub = [C.sb(f"l8u{i}", [128, tt], F32) for i in range(NSL)]
    dtb = [C.sb(f"l8dt{i}", [128, tt], F32) for i in range(NSL)]
    btb = [C.sb(f"l8bt{i}", [16, tt], F32) for i in range(NSL)]
    ctb = [C.sb(f"l8ct{i}", [16, tt], F32) for i in range(NSL)]
    yb = [C.sb(f"l8y{i}", [128, tt], F32) for i in range(NSL)]
    du = C.sb("l8du", [128, tt], F32)
    wt = [C.sb(f"l8wt{i}", [128, tt], F32) for i in range(2)]
    da = [C.sb(f"l8da{i}", [128, tt], F32) for i in range(2)]
    hb = C.sb("l8h", [128, tt], F32)
    yt = C.sb("l8yt", [128, tt], F32)
    hlast = C.sb("l8hl", [128, 16], F32)
    ps_bb = [C.ps(f"l8pbb{i}", [128, 512]) for i in range(2)]
    ps_cb = [C.ps(f"l8pcb{i}", [128, 512]) for i in range(2)]
    wl = [C.sem(f"l8w{i}") for i in range(NSL)]
    sst = [C.sem(f"l8s{i}") for i in range(NSL)]
    tmm = C.sem("l8tmm")

    gi = 0
    tile_toks = []
    sc_done = {}
    for b in range(nseg):
        for (t0, tn) in mk_tiles(Tseg, tt):
            g0 = b * Tseg + t0
            sl = gi % NSL
            lw = [a_ready] if gi == 0 else []
            if gi >= NSL:
                lw.append(tile_toks[gi - NSL])
            for src, dst in ((u_d, ub[sl]), (dt_d, dtb[sl]), (bt_d, btb[sl]),
                            (ct_d, ctb[sl])):
                def ld_fn(e, src=src, dst=dst, g0=g0, tn=tn):
                    if len(src.ap) == 3:
                        bw = src.ap[2][1]
                        return e.dma_start(out=dst[:, 0:tn],
                                           in_=src[:, g0 // bw,
                                                   g0 % bw:g0 % bw + tn])
                    return e.dma_start(
                        out=dst[:, 0:tn] if dst.shape[0] == 128 else dst[0:16, 0:tn],
                        in_=src[:, g0:g0 + tn] if src.shape[0] == 128 else src[0:16, g0:g0 + tn])
                C.rec("sync", ld_fn, wait=lw, inc=((wl[sl], 16),))
                lw = []
            C.rec("vector", lambda e, sl=sl, tn=tn:
                  e.tensor_tensor(du[:, 0:tn], dtb[sl][:, 0:tn], ub[sl][:, 0:tn],
                                  OP.mult),
                  wait=[(wl[sl], wl[sl].n), (sst[sl], 16 * (gi // NSL))], nops=1)
            mm_tok, da_tok = {}, {}
            for np2 in range(8):
                for n in (2 * np2, 2 * np2 + 1):
                    bk = n % 2
                    gx = gi * 16 + n
                    w = [(wl[sl], wl[sl].n)]
                    if gx >= 2:
                        w.append(sc_done[gx - 2])

                    def mm_bc(t, sl=sl, n=n, tn=tn, bk=bk):
                        t.matmul(ps_bb[bk][:, 0:tn], sel[:, n * 128:(n + 1) * 128],
                                 btb[sl][0:16, 0:tn], start=True, stop=True)
                        return t.matmul(ps_cb[bk][:, 0:tn],
                                        sel[:, n * 128:(n + 1) * 128],
                                        ctb[sl][0:16, 0:tn], start=True, stop=True)
                    C.rec("tensor", mm_bc, wait=w, inc=((tmm, 1),))
                    mm_tok[n] = (tmm, tmm.n)
                for n in (2 * np2, 2 * np2 + 1):
                    bk = n % 2
                    gx = gi * 16 + n
                    dw = [(wl[sl], wl[sl].n), a_ready]
                    if gx >= 2:
                        dw.append(sc_done[gx - 2])
                    C.rec("scalar", lambda e, sl=sl, n=n, tn=tn, bk=bk:
                          e.activation(da[bk][:, 0:tn], dtb[sl][:, 0:tn], AF.Exp,
                                       scale=avals[:, n:n + 1]),
                          wait=dw, nops=1)
                    da_tok[n] = (SC, SC.n)
                for n in (2 * np2, 2 * np2 + 1):
                    bk = n % 2
                    C.rec("vector", lambda e, tn=tn, bk=bk:
                          e.tensor_tensor(wt[bk][:, 0:tn], du[:, 0:tn],
                                          ps_bb[bk][:, 0:tn], OP.mult),
                          wait=[mm_tok[n]], nops=1)
                for n in (2 * np2, 2 * np2 + 1):
                    bk = n % 2
                    gx = gi * 16 + n

                    def scan_ops(e, sl=sl, n=n, tn=tn, t0=t0, bk=bk):
                        init = 0.0 if t0 == 0 else hlast[:, n:n + 1]
                        e.tensor_tensor_scan(hb[:, 0:tn], da[bk][:, 0:tn],
                                             wt[bk][:, 0:tn], init, OP.mult,
                                             OP.add)
                        e.tensor_copy(hlast[:, n:n + 1], hb[:, tn - 1:tn])
                        if n == 0:
                            return e.tensor_tensor(yb[sl][:, 0:tn], hb[:, 0:tn],
                                                   ps_cb[bk][:, 0:tn], OP.mult)
                        e.tensor_tensor(yt[:, 0:tn], hb[:, 0:tn],
                                        ps_cb[bk][:, 0:tn], OP.mult)
                        return e.tensor_tensor(yb[sl][:, 0:tn], yb[sl][:, 0:tn],
                                               yt[:, 0:tn], OP.add)
                    C.rec("vector", scan_ops, wait=[da_tok[n]],
                          nops=3 if n == 0 else 4)
                    sc_done[gx] = (VC, VC.n)
            C.rec("vector", lambda e, sl=sl, tn=tn:
                  e.scalar_tensor_tensor(yb[sl][:, 0:tn], ub[sl][:, 0:tn],
                                         dvs[:, 0:1], yb[sl][:, 0:tn],
                                         OP.mult, OP.add),
                  wait=[], nops=1)
            tile_toks.append((VC, VC.n))

            def st_fn(e, sl=sl, g0=g0, tn=tn):
                if len(y_o.ap) == 3:
                    bw = y_o.ap[2][1]
                    return e.dma_start(out=y_o[:, g0 // bw, g0 % bw:g0 % bw + tn],
                                       in_=yb[sl][:, 0:tn])
                return e.dma_start(out=y_o[:, g0:g0 + tn], in_=yb[sl][:, 0:tn])
            C.rec("gpsimd", st_fn, wait=[(VC, VC.n)], inc=((sst[sl], 16),))
            gi += 1

    C.rec("gpsimd", lambda e: e.nop(), wait=[(s, s.n) for s in sst])
    return [(s, s.n) for s in sst]



def emit_launch9(C, nt):
    H = 514
    ntok = nt + H
    NQK = nt + 512
    NA = nt + 256
    NW = nt // 256 + 1
    x_d = C.inp("x9", (D, ntok), F32)
    wqkv_d = C.inp("wqkv", (D, 3072), F32)
    wo_d = C.inp("wo", (D, D), F32)
    qs_d = C.inp("qs_r", (128, 8), F32)
    ks_d = C.inp("ks_r", (128, 8), F32)
    csq_d = C.inp("csq", (128, NQK), F32)
    ssq_d = C.inp("ssq", (128, NQK), F32)
    csk_d = C.inp("csk", (128, NQK), F32)
    ssk_d = C.inp("ssk", (128, NQK), F32)
    h0f_d = C.inp("h0flag", (128, 8), F32)
    bm0_d = C.inp("bmask0", (128, 256), BF16)
    bm1_d = C.inp("bmask1", (128, 256), BF16)
    lng_d = C.inp("ln_g_r", (128, 8), F32)
    lnb_d = C.inp("ln_b_r", (128, 8), F32)
    w1_d = C.inp("w1l", (D, 5460), F32)
    b1a_d = C.inp("b1a_r", (128, 22), F32)
    b1g_d = C.inp("b1g_r", (128, 22), F32)
    w2_d = C.inp("w2lT", (2816, D), F32)
    b2_d = C.inp("b2_r", (128, 8), F32)
    xout_o = C.out("x9o", (D, nt), F32)

    cn = make_consts(C)
    rot_np = np.zeros((128, 128), np.float32)
    for h2 in range(2):
        b0 = 64 * h2
        for i in range(32):
            rot_np[b0 + i, b0 + 32 + i] = -1.0
            rot_np[b0 + 32 + i, b0 + i] = 1.0
    rotT_d = C.const("rotT", np.ascontiguousarray(rot_np.T.astype(ml_dtypes.bfloat16)))
    hsel_np = np.zeros((128, 8, 16), np.float32)
    for c in range(8):
        hsel_np[0:64, c, 2 * c] = 1.0
        hsel_np[64:128, c, 2 * c + 1] = 1.0
    hsel_d = C.const("hsel", hsel_np)
    hexp_np = np.zeros((16, 8, 128), np.float32)
    for c in range(8):
        hexp_np[2 * c, c, 0:64] = 1.0
        hexp_np[2 * c + 1, c, 64:128] = 1.0
    hexp_d = C.const("hexp", hexp_np)
    i128bf_d = C.const("i128bf", np.eye(128, dtype=np.float32))
    ibf_d = C.const("ibf", np.eye(128).astype(ml_dtypes.bfloat16))
    onesbf_d = C.const("onesbf", np.ones((128, 1), ml_dtypes.bfloat16))
    amask_np = []
    for j in range(4):
        p = np.arange(128)[:, None]
        q = np.arange(256)[None, :]
        if j == 0:
            m = (q <= p)
        elif j == 1:
            m = (q >= p - 128) & (q <= p + 128)
        elif j == 2:
            m = (q >= p)
        else:
            m = (q >= p + 128)
        amask_np.append(m)
    amask_d = [C.const(f"am{j}", amask_np[j].astype(ml_dtypes.bfloat16))
               for j in range(4)]

    ld = C.sem("ld")
    rotT = C.sb("rotT_s", [128, 128], BF16)
    hsel = C.sb("hsel_s", [128, 8, 16], F32)
    hexp = C.sb("hexp_s", [16, 8, 128], F32)
    i128 = C.sb("i128_s", [128, 128], F32)
    ibf = C.sb("ibf_s", [128, 128], BF16)
    onesbf = C.sb("onesbf_s", [128, 1], BF16)
    amask = C.sb("am_s", [128, 1024], BF16)
    bmask = C.sb("bm_s", [128, 512], BF16)
    qs = C.sb("qs_s", [128, 8], F32)
    ks = C.sb("ks_s", [128, 8], F32)
    h0f = C.sb("h0f_s", [128, 8], F32)
    lng = C.sb("lng_s", [128, 8], F32)
    lnb = C.sb("lnb_s", [128, 8], F32)
    for sb_, d_ in ((rotT, rotT_d), (hsel, hsel_d), (hexp, hexp_d),
                    (i128, i128bf_d), (ibf, ibf_d), (onesbf, onesbf_d),
                    (qs, qs_d), (ks, ks_d), (h0f, h0f_d), (lng, lng_d),
                    (lnb, lnb_d)):
        C.dma("sync", sb_[...], d_, ld)
    for j in range(4):
        C.dma("sync", amask[:, j * 256:(j + 1) * 256], amask_d[j], ld)
    C.dma("sync", bmask[:, 0:256], bm0_d, ld)
    C.dma("sync", bmask[:, 256:512], bm1_d, ld)

    xs_d = C.scratch("xs9_bf", (D, NQK), BF16)
    qkv_d = C.scratch("qkv9_bf", (3072, NQK), BF16)
    qp_d = C.scratch("qp9_bf", (D, NQK), BF16)
    kp_d = C.scratch("kp9_bf", (D, NQK), BF16)
    aw_d = C.scratch("aw9_bf", (D, NA), BF16)
    xatt_d = C.scratch("xatt9", (D, NA), F32)
    xs2_d = C.scratch("xs29", (D, nt), F32)
    x2n_d = C.scratch("x2n9_bf", (D, nt), BF16)
    VC, SC = C.ch("vector"), C.ch("scalar")

    C.sb_push()
    sx9 = fm3(x_d)
    sxs = fm3(xs_d)
    ab = [C.sbs(f"sa_a{i}", [128, 8, 512 + 1], F32) for i in range(2)]
    ob = [C.sbs(f"sa_o{i}", [128, 8, 512], BF16) for i in range(2)]
    wl = [C.sem(f"sa_w{i}") for i in range(2)]
    stq = [C.sem(f"sa_s{i}") for i in range(2)]
    tks = {}
    for ti, (t0, tn) in enumerate(mk_tiles(NQK, 512)):
        par = ti % 2
        lw = [ld, cn["ld"]] if ti == 0 else []
        if ti >= 2:
            lw.append(tks[ti - 2])
        C.rec("sync", lambda e, a=ab[par], t0=t0, tn=tn:
              e.dma_start(out=a[:, :, 0:tn + 1], in_=sx9[:, :, t0 + 1:t0 + 2 + tn]),
              wait=lw, inc=((wl[par], 16),))

        def sh_ops(e, a=ab[par], o=ob[par], tn=tn):
            ins = None
            for c in range(8):
                src = a[:, c, 1:1 + tn] if c < 4 else a[:, c, 0:tn]
                ins = e.tensor_copy(o[:, c, 0:tn], src)
            return ins
        C.rec("vector", sh_ops,
              wait=[(wl[par], wl[par].n), (stq[par], 16 * (ti // 2))], nops=8)
        tks[ti] = (VC, VC.n)
        C.rec("sync", lambda e, o=ob[par], t0=t0, tn=tn:
              e.dma_start(out=sxs[:, :, t0:t0 + tn], in_=o[:, :, 0:tn]),
              wait=[(VC, VC.n)], inc=((stq[par], 16),))
    ba = [(s, s.n) for s in stq]
    C.sb_pop()

    C.sb_push()
    epi_qkv = mk_epi(C, "qk", dict(dst_bf=qkv_d))
    bb = stage_proj(C, "qk", xs_d, wqkv_d, D, 3072, NQK, epi_qkv, wait=ba)
    C.sb_pop()

    C.sb_push()
    sqkv = fm3(qkv_d)
    sqp = fm3(qp_d)
    skp = fm3(kp_d)
    scsq = csq_d
    sssq = ssq_d
    scsk = csk_d
    sssk = ssk_d
    qb = [C.sbs(f"nc_q{i}", [128, 8, 512], BF16) for i in range(2)]
    tbl = [C.sbs(f"nc_t{i}", [128, 4, 512], F32) for i in range(2)]
    sqb = C.sbs("nc_sq", [128, 8, 512], F32)
    rqrow = C.sbs("nc_rq", [16, 512], F32)
    qn = C.sbs("nc_qn", [128, 512], F32)
    qnb = C.sbs("nc_qnb", [128, 512], BF16)
    t1b = C.sbs("nc_tt1", [128, 512], F32)
    oqb = [C.sbs(f"nc_o{i}", [128, 8, 512], BF16) for i in range(2)]
    ps16 = C.pss("nc_ps16", [16, 512])
    psrb = C.pss("nc_psrb", [128, 512])
    psrot = C.pss("nc_psrot", [128, 512])
    wl2 = [C.sem(f"nc_w{i}") for i in range(2)]
    st2 = [C.sem(f"nc_s{i}") for i in range(2)]
    tq = C.sem("nc_tq")
    tks2 = {}
    it2 = 0
    for qk in range(2):
        rows_off = 0 if qk == 0 else 8
        src_cs = scsq if qk == 0 else scsk
        src_ss = sssq if qk == 0 else sssk
        scl = qs if qk == 0 else ks
        dst = sqp if qk == 0 else skp
        for ti, (t0, tn) in enumerate(mk_tiles(NQK, 512)):
            par = it2 % 2
            lw = [ld] + list(bb) if it2 == 0 else []
            if it2 >= 2:
                lw.append(tks2[it2 - 2])
            C.rec("sync", lambda e, b_=qb[par], t0=t0, tn=tn, ro=rows_off:
                  e.dma_start(out=b_[:, :, 0:tn], in_=sqkv[:, ro:ro + 8, t0:t0 + tn]),
                  wait=lw, inc=((wl2[par], 16),))
            wsem2 = wl2[par]
            def tload(e, b_=tbl[par], t0=t0, tn=tn, cs=src_cs, ss=src_ss,
                      wsem2=wsem2):
                e.dma_start(out=b_[:, 0, 0:tn],
                            in_=cs[:, t0:t0 + tn]).then_inc(wsem2.h, 16)
                return e.dma_start(out=b_[:, 1, 0:tn],
                                   in_=ss[:, t0:t0 + tn]).then_inc(wsem2.h, 16)
            C.rec("sync", tload, wait=[])
            C.mark(wsem2, 32, "sync")
            C.rec("scalar", lambda e, b_=qb[par], tn=tn:
                  e.activation(sqb[:, :, 0:tn], b_[:, :, 0:tn], AF.Square),
                  wait=[(wl2[par], wl2[par].n)], nops=1)

            def mm_ss(t, tn=tn):
                ins = None
                for c in range(8):
                    ins = t.matmul(ps16[:, 0:tn], hsel[:, c, :], sqb[:, c, 0:tn],
                                   start=(c == 0), stop=(c == 7))
                return ins
            C.rec("tensor", mm_ss, wait=[(SC, SC.n)], inc=((tq, 1),))
            C.rec("scalar", lambda e, tn=tn:
                  e.activation(rqrow[:, 0:tn], ps16[:, 0:tn], AF.Sqrt),
                  wait=[(tq, tq.n)], nops=1)

            def rowo(e, tn=tn):
                e.tensor_scalar(rqrow[:, 0:tn], rqrow[:, 0:tn],
                                cn["cst"][0:16, 11:12], None, OP.max)
                return e.reciprocal(rqrow[:, 0:tn], rqrow[:, 0:tn])
            C.rec("vector", rowo, wait=[(SC, SC.n)], nops=2)
            rq_tok = (VC, VC.n)
            for c in range(8):
                C.rec("tensor", lambda t, c=c, tn=tn:
                      t.matmul(psrb[:, 0:tn], hexp[:, c, :], rqrow[0:16, 0:tn],
                               start=True, stop=True),
                      wait=[rq_tok, (VC, VC.n)], inc=((tq, 1),))

                def app1(e, b_=qb[par], c=c, tn=tn, scl=scl):
                    e.scalar_tensor_tensor(qn[:, 0:tn], b_[:, c, 0:tn],
                                           scl[:, c:c + 1], psrb[:, 0:tn],
                                           OP.mult, OP.mult)
                    return e.tensor_copy(qnb[:, 0:tn], qn[:, 0:tn])
                C.rec("vector", app1, wait=[(tq, tq.n)], nops=2)
                C.rec("tensor", lambda t, tn=tn:
                      t.matmul(psrot[:, 0:tn], rotT[:, :], qnb[:, 0:tn],
                               start=True, stop=True),
                      wait=[(VC, VC.n)], inc=((tq, 1),))

                def app2(e, b_=tbl[par], o=oqb[par], c=c, tn=tn):
                    e.tensor_tensor(t1b[:, 0:tn], psrot[:, 0:tn], b_[:, 1, 0:tn],
                                    OP.mult)
                    e.tensor_tensor(qn[:, 0:tn], qn[:, 0:tn], b_[:, 0, 0:tn],
                                    OP.mult)
                    e.tensor_tensor(qn[:, 0:tn], qn[:, 0:tn], t1b[:, 0:tn], OP.add)
                    return e.tensor_copy(o[:, c, 0:tn], qn[:, 0:tn])
                a2w = [(tq, tq.n)]
                if c == 0:
                    a2w.append((st2[par], 16 * (it2 // 2)))
                C.rec("vector", app2, wait=a2w, nops=4)
            tks2[it2] = (VC, VC.n)
            C.rec("sync", lambda e, o=oqb[par], dst=dst, t0=t0, tn=tn:
                  e.dma_start(out=dst[:, :, t0:t0 + tn], in_=o[:, :, 0:tn]),
                  wait=[(VC, VC.n)], inc=((st2[par], 16),))
            it2 += 1
    bc = [(s, s.n) for s in st2]
    C.sb_pop()

    C.sb_push()
    vres = C.sbs("vres9", [128, 20, 1040], BF16)
    C.rec("vector", lambda e: [e.memset(vres[:, a, :], 1.0)
                               for a in range(20)][-1], nops=20)
    C.sb_push()
    vb = [C.sbs(f"vt_v{i}", [128, 8, 512], BF16) for i in range(2)]
    psvt = [C.pss(f"vt_ps{i}", [128, 128], BF16) for i in range(2)]
    wl3 = [C.sem(f"vt_w{i}") for i in range(2)]
    tv = C.sem("vt_tv")
    tks3 = {}
    it3 = 0
    for ti, (t0, tn) in enumerate(mk_tiles(NQK, 512)):
        par = ti % 2
        lw = [ld] + bb if ti == 0 else []
        if ti >= 2:
            lw.append(tks3[ti - 2])
        C.rec("sync", lambda e, b_=vb[par], t0=t0, tn=tn:
              e.dma_start(out=b_[:, :, 0:tn], in_=sqkv[:, 16:24, t0:t0 + tn]),
              wait=lw, inc=((wl3[par], 16),))
        for c in range(8):
            for sub in range(cdiv(tn, 128)):
                s0 = sub * 128
                sn = min(128, tn - s0)
                sp = it3 % 2

                def mm_tr(t, b_=vb[par], c=c, s0=s0, sn=sn, sp=sp):
                    return t.transpose(psvt[sp][0:sn, :], b_[:, c, s0:s0 + sn],
                                       ibf[:, :])
                C.rec("tensor", mm_tr,
                      wait=[(wl3[par], wl3[par].n), (VC, VC.n)], inc=((tv, 1),))

                def vcp(e, sp=sp, sn=sn, c=c, a=(t0 + s0) // 128):
                    e.tensor_copy(vres[0:sn, a, 130 * c:130 * c + 64],
                                  psvt[sp][0:sn, 0:64])
                    return e.tensor_copy(vres[0:sn, a, 130 * c + 65:130 * c + 129],
                                         psvt[sp][0:sn, 64:128])
                C.rec("vector", vcp, wait=[(tv, tv.n)], nops=2)
                it3 += 1
        tks3[ti] = (VC, VC.n)
    bd = [(VC, VC.n)]
    C.sb_pop()

    onesr_bf_d = C.const("onesr_bf", np.ones((1, 128), ml_dtypes.bfloat16))
    onesr_bf = C.sb("onesr_bf_s", [1, 128], BF16)
    C.dma("sync", onesr_bf[:, :], onesr_bf_d, ld)
    C.sb_push()
    saw = fm3(aw_d)
    NSA = 4
    qw = [C.sbs(f"at_q{i}", [64, 256], BF16) for i in range(NSA)]
    kw = [C.sbs(f"at_k{i}", [64, 512], BF16) for i in range(NSA)]
    attb = [C.sbs(f"at_a{i}", [128, 512], BF16) for i in range(NSA)]
    negm = [C.sbs(f"at_m{i}", [1, 256], BF16) for i in range(NSA)]
    qn9a = [C.sbs(f"at_qn{i}", [128, 2], F32) for i in range(NSA)]
    rzb = [C.sbs(f"at_rz{i}", [1, 256], F32) for i in range(NSA)]
    yeb = [C.sbs(f"at_ye{i}", [64, 256], F32) for i in range(NSA)]
    yhb = [C.sbs(f"at_yh{i}", [64, 256], BF16) for i in range(NSA)]
    bS = [C.pss(f"at_bS{i}", [128, 512]) for i in range(NSA)]
    bV = [C.pss(f"at_bV{i}", [128, 512]) for i in range(NSA)]
    wl4 = [C.sem(f"at_w{i}") for i in range(NSA)]
    st4 = [C.sem(f"at_s{i}") for i in range(NSA)]
    ta = C.sem("at_t")
    iters = [(h, wi) for h in range(16) for wi in range(1, NW + 1)]
    NIT = len(iters)
    ldtok, t1, t2, t3, t4, t5, t6, t7, t8 = ({} for _ in range(9))
    v3, v4, v5, v6, v7, sx1, sx2, dn = ({} for _ in range(8))

    def st_L(i):
        h, wi = iters[i]
        sl = i % NSA
        chq, ro = h // 2, 64 * (h % 2)
        j0 = wi * 256
        k0 = (wi - 1) * 256
        lw = bc + bd + [ld] if i == 0 else []
        if i >= NSA:
            lw.append(dn[i - NSA])
        C.rec("sync", lambda e, sl=sl, chq=chq, ro=ro, j0=j0:
              e.dma_start(out=qw[sl][:, :], in_=sqp[ro:ro + 64, chq, j0:j0 + 256]),
              wait=lw, inc=((wl4[sl], 16),))
        C.rec("sync", lambda e, sl=sl, chq=chq, ro=ro, k0=k0:
              e.dma_start(out=kw[sl][:, :], in_=skp[ro:ro + 64, chq, k0:k0 + 512]),
              wait=[], inc=((wl4[sl], 16),))
        ldtok[i] = (wl4[sl], wl4[sl].n)

    def st_T1(i):
        sl = i % NSA
        w = [ldtok[i]] + ([dn[i - NSA]] if i >= NSA else [])
        C.rec("tensor", lambda t, sl=sl:
              t.matmul(bS[sl][:, 0:512], qw[sl][:, 0:128], kw[sl][:, :],
                       start=True, stop=True), wait=w, inc=((ta, 1),))
        t1[i] = (ta, ta.n)

    def st_V1(i):
        sl = i % NSA
        C.rec("vector", lambda e, sl=sl:
              e.reduce_max(qn9a[sl][:, 0:1], bS[sl][:, 0:512], AX.X, None),
              wait=[t1[i]], nops=1)

    def st_T2(i):
        sl = i % NSA
        w = [ldtok[i]] + ([dn[i - NSA]] if i >= NSA else [])
        C.rec("tensor", lambda t, sl=sl:
              t.matmul(bV[sl][:, 0:512], qw[sl][:, 128:256], kw[sl][:, :],
                       start=True, stop=True), wait=w, inc=((ta, 1),))
        t2[i] = (ta, ta.n)

    def st_V2(i):
        sl = i % NSA
        C.rec("vector", lambda e, sl=sl:
              e.reduce_max(qn9a[sl][:, 1:2], bV[sl][:, 0:512], AX.X, None),
              wait=[t2[i]], nops=1)
        v3[i] = None
        t3.setdefault("v2_%d" % i, (VC, VC.n))

    def st_T3(i):
        sl = i % NSA

        def mm_mrow(t, sl=sl):
            t.matmul(bV[sl][0:1, 256:384], qn9a[sl][:, 0:1], i128[:, :],
                     start=True, stop=True)
            return t.matmul(bV[sl][0:1, 384:512], qn9a[sl][:, 1:2],
                            i128[:, :], start=True, stop=True)
        C.rec("tensor", mm_mrow, wait=[t3["v2_%d" % i]], inc=((ta, 1),))
        t3[i] = (ta, ta.n)

    def st_V3(i):
        sl = i % NSA
        C.rec("vector", lambda e, sl=sl:
              e.tensor_scalar(negm[sl][:, :], bV[sl][0:1, 256:512],
                              cn["cst"][0:1, 7:8], None, OP.mult),
              wait=[t3[i]], nops=1)
        v3[i] = (VC, VC.n)

    def mk_p2(i, half):
        sl = i % NSA

        def mm_p2(t, sl=sl, half=half):
            ins = None
            for jj in range(2):
                j = 2 * half + jj
                t.matmul(bS[sl][:, jj * 256:(jj + 1) * 256],
                         kw[sl][:, j * 128:(j + 1) * 128],
                         qw[sl][:, :], start=True, stop=False,
                         skip_group_check=True)
                ins = t.matmul(bS[sl][:, jj * 256:(jj + 1) * 256],
                               onesr_bf[:, :], negm[sl][:, :],
                               start=False, stop=True,
                               skip_group_check=True)
            return ins
        return mm_p2

    def st_T4(i):
        C.rec("tensor", mk_p2(i, 0), wait=[v3[i]], inc=((ta, 1),))
        t4[i] = (ta, ta.n)

    def st_S1(i):
        sl = i % NSA
        w = [t4[i]] + ([t7[i - NSA]] if i >= NSA else [])
        C.rec("scalar", lambda e, sl=sl:
              e.activation(attb[sl][:, :], bS[sl][:, 0:512], AF.Exp,
                           scale=cn["cst"][:, 6:7]), wait=w, nops=1)
        sx1[i] = (SC, SC.n)

    def st_V4(i):
        h, wi = iters[i]
        sl = i % NSA
        mA = bmask[:, :] if wi == 2 else amask[:, 0:512]
        C.rec("vector", lambda e, sl=sl, mj=mA:
              e.tensor_tensor(attb[sl][:, :], attb[sl][:, :], mj, OP.mult),
              wait=[sx1[i]], nops=1)
        v4[i] = (VC, VC.n)

    def mk_av(i, half):
        h, wi = iters[i]
        sl = i % NSA
        k0 = (wi - 1) * 256

        def mm_av(t, sl=sl, half=half, h=h, k0=k0):
            ins = None
            for jj in range(2):
                j = 2 * half + jj
                ins = t.matmul(bV[sl][0:65, 0:256],
                               vres[:, k0 // 128 + j, 65 * h:65 * h + 65],
                               attb[sl][:, jj * 256:(jj + 1) * 256],
                               start=(j == 0), stop=(j == 3),
                               skip_group_check=True)
            return ins
        return mm_av

    def st_T5(i):
        C.rec("tensor", mk_av(i, 0), wait=[v4[i]], inc=((ta, 1),))
        t5[i] = (ta, ta.n)

    def st_T6(i):
        C.rec("tensor", mk_p2(i, 1), wait=[v3[i]], inc=((ta, 1),))
        t6[i] = (ta, ta.n)

    def st_S2(i):
        sl = i % NSA
        C.rec("scalar", lambda e, sl=sl:
              e.activation(attb[sl][:, :], bS[sl][:, 0:512], AF.Exp,
                           scale=cn["cst"][:, 6:7]), wait=[t6[i]], nops=1)
        sx2[i] = (SC, SC.n)

    def st_V5(i):
        sl = i % NSA
        C.rec("vector", lambda e, sl=sl, mj=amask[:, 512:1024]:
              e.tensor_tensor(attb[sl][:, :], attb[sl][:, :], mj, OP.mult),
              wait=[sx2[i]], nops=1)
        v5[i] = (VC, VC.n)

    def st_T7(i):
        C.rec("tensor", mk_av(i, 1), wait=[v5[i]], inc=((ta, 1),))
        t7[i] = (ta, ta.n)

    def st_V6(i):
        sl = i % NSA

        def fin1(e, sl=sl):
            e.reciprocal(rzb[sl][:, :], bV[sl][64:65, 0:256])
            return e.tensor_copy(yeb[sl][:, :], bV[sl][0:64, 0:256])
        C.rec("vector", fin1, wait=[t7[i], (st4[sl], 16 * (i // NSA))], nops=2)
        v6[i] = (VC, VC.n)

    def st_T8(i):
        sl = i % NSA
        C.rec("tensor", lambda t, sl=sl:
              t.matmul(bV[sl][0:64, 256:512], cn["ones_r"][0:1, 0:64],
                       rzb[sl][:, :], start=True, stop=True),
              wait=[v6[i]], inc=((ta, 1),))
        t8[i] = (ta, ta.n)

    def st_V7(i):
        sl = i % NSA

        def fin(e, sl=sl):
            e.tensor_tensor(yeb[sl][:, :], yeb[sl][:, :],
                            bV[sl][0:64, 256:512], OP.mult)
            return e.tensor_copy(yhb[sl][:, :], yeb[sl][:, :])
        C.rec("vector", fin, wait=[t8[i]], nops=2)
        v7[i] = dn[i] = (VC, VC.n)

    def st_St(i):
        h, wi = iters[i]
        sl = i % NSA
        chq, ro = h // 2, 64 * (h % 2)
        C.rec("sync", lambda e, sl=sl, chq=chq, ro=ro, wi=wi:
              e.dma_start(out=saw[ro:ro + 64, chq, (wi - 1) * 256:wi * 256],
                          in_=yhb[sl][:, :]),
              wait=[v7[i]], inc=((st4[sl], 16),))

    def ph1_list(p):
        a, b = 2 * p, 2 * p + 1
        out = []
        for st in (st_L, st_T1, st_V1, st_T2, st_V2, st_T3, st_V3, st_T4,
                   st_S1, st_V4):
            out.append((st, a))
            out.append((st, b))
        return out

    def ph2_list(p):
        a, b = 2 * p, 2 * p + 1
        out = []
        for st in (st_T5, st_T6, st_S2, st_V5, st_T7, st_V6, st_T8, st_V7,
                   st_St):
            out.append((st, a))
            out.append((st, b))
        return out

    NP = NIT // 2
    for p in range(NP + 1):
        l2 = ph2_list(p - 1) if p >= 1 else []
        l1 = ph1_list(p) if p < NP else []
        n = max(len(l1), len(l2))
        for k in range(n):
            if k < len(l2):
                l2[k][0](l2[k][1])
            if k < len(l1):
                l1[k][0](l1[k][1])
    be = [(s, s.n) for s in st4]
    C.sb_pop()
    C.sb_pop()

    C.sb_push()
    epi_wo = mk_epi(C, "wo", dict(dst_f=xatt_d, aux="add"))
    bf_ = stage_proj(C, "wo", aw_d, wo_d, D, D, NA, epi_wo,
                     aux_d=x_d[:, 258:258 + NA], aux_dim=D, wait=be)
    C.sb_pop()
    C.sb_push()
    sxa = fm3(xatt_d)
    fixb = C.sbs("fx_b", [128, 8, 8], F32)
    wl5 = C.sem("fx_w")
    st5 = C.sem("fx_s")
    C.rec("sync", lambda e: e.dma_start(out=fixb[:, :, :], in_=sxa[:, :, 248:256]),
          wait=bf_, inc=((wl5, 16),))

    def fixops(e):
        ins = None
        for c in range(8):
            ins = e.tensor_scalar(fixb[:, c, 7:8], fixb[:, c, 7:8],
                                  h0f[:, c:c + 1], None, OP.mult)
        return ins
    C.rec("vector", fixops, wait=[(wl5, wl5.n), ld], nops=8)
    C.rec("sync", lambda e: e.dma_start(out=sxa[:, :, 248:256], in_=fixb[:, :, :]),
          wait=[(VC, VC.n)], inc=((st5, 16),))
    bfix = [(st5, st5.n)]
    C.sb_pop()

    C.sb_push()
    w1s = w1_d.rearrange("(c p) m -> p c m", p=128)
    w2s = w2_d.rearrange("(c p) m -> p c m", p=128)
    wA = C.sbs("ff_wa", [128, 8, 2730], BF16)
    wG = C.sbs("ff_wg", [128, 8, 2730], BF16)
    w2b = C.sbs("ff_w2", [128, 22, 1024], BF16)
    wld = C.sem("ff_wld")

    def wload_all(e):
        ins = None
        for c in range(8):
            ins = e.dma_start(out=wA[:, c, :], in_=w1s[:, c, 0:2730])
            ins.then_inc(wld.h, 16)
        for c in range(8):
            ins = e.dma_start(out=wG[:, c, :], in_=w1s[:, c, 2730:5460])
            ins.then_inc(wld.h, 16)
        for c in range(22):
            ins = e.dma_start(out=w2b[:, c, :], in_=w2s[:, c, :])
            ins.then_inc(wld.h, 16)
        return ins
    C.rec("gpsimd", wload_all, wait=list(bf_) + bfix + [ld])
    C.mark(wld, 16 * 38, "gpsimd")
    wA_tok = (wld, 16 * 8)
    w1_tok = (wld, 16 * 16)
    w2_tok = (wld, 16 * 38)

    C.sb_push()
    sxs2 = fm3(xs2_d)
    ab2 = [C.sbs(f"s2_a{i}", [128, 8, 257], F32) for i in range(2)]
    ob2 = [C.sbs(f"s2_o{i}", [128, 8, 256], F32) for i in range(2)]
    wl6 = [C.sem(f"s2_w{i}") for i in range(2)]
    st6 = [C.sem(f"s2_s{i}") for i in range(2)]
    tks6 = {}
    for ti, (t0, tn) in enumerate(mk_tiles(nt, 256)):
        par = ti % 2
        lw = list(bfix) if ti == 0 else []
        if ti >= 2:
            lw.append(tks6[ti - 2])
        C.rec("sync", lambda e, a=ab2[par], t0=t0, tn=tn:
              e.dma_start(out=a[:, :, 0:tn + 1], in_=sxa[:, :, 255 + t0:256 + t0 + tn]),
              wait=lw, inc=((wl6[par], 16),))

        def sh2(e, a=ab2[par], o=ob2[par], tn=tn):
            ins = None
            for c in range(8):
                src = a[:, c, 1:1 + tn] if c < 4 else a[:, c, 0:tn]
                ins = e.tensor_copy(o[:, c, 0:tn], src)
            return ins
        C.rec("vector", sh2,
              wait=[(wl6[par], wl6[par].n), (st6[par], 16 * (ti // 2))], nops=8)
        tks6[ti] = (VC, VC.n)
        C.rec("sync", lambda e, o=ob2[par], t0=t0, tn=tn:
              e.dma_start(out=sxs2[:, :, t0:t0 + tn], in_=o[:, :, 0:tn]),
              wait=[(VC, VC.n)], inc=((st6[par], 16),))
    bg = [(s, s.n) for s in st6]
    C.sb_pop()
    C.sb_push()
    bh = stage_rms(C, "l9n", xs2_d, nt, D, dst_bf_d=x2n_d, with_mean=True,
                   scale_sb=lng, bias_sb=lnb, wait=bg + [ld], tt=256)
    C.sb_pop()
    b1a = C.sb("b1a_s", [128, 22], F32)
    b1g = C.sb("b1g_s", [128, 22], F32)
    b2s = C.sb("b2s_s", [128, 8], F32)
    C.dma("sync", b1a[:, :], b1a_d, ld)
    C.dma("sync", b1g[:, :], b1g_d, ld)
    C.dma("sync", b2s[:, :], b2_d, ld)
    mt_a = [(i * 128, 128) for i in range(21)] + [(2688, 42)]
    C.sb_push()
    sx2n = fm3(x2n_d)
    sxo = fm3(xout_o)
    TGF = 512
    NGF = nt // TGF
    xgb = [C.sbs(f"ff_x{i}", [128, 8, TGF], BF16) for i in range(2)]
    hhs = C.sbs("ff_hh", [128, 22, TGF], BF16)
    ebg2 = [C.sbs(f"ff_g{i}", [128, TGF], F32) for i in range(2)]
    auxt = [C.sbs(f"ff_t{i}", [128, TGF], F32) for i in range(2)]
    psA = [C.pss(f"ff_pa{i}", [128, TGF]) for i in range(2)]
    psG = [C.pss(f"ff_pg{i}", [128, TGF]) for i in range(2)]
    psO = [C.pss(f"ff_po{i}", [128, TGF]) for i in range(2)]
    wxf = C.sem("ff_wx")
    waf = C.sem("ff_waf")
    taf = C.sem("ff_ta")
    stf = C.sem("ff_st")
    C.rec("vector", lambda e: e.memset(hhs[:, 21, :], 0.0),
          wait=list(bh) + list(bg), nops=1)
    vstt, stoks, vepis, xtoks, t_w1last, w2done = {}, {}, {}, {}, {}, {}
    gi = 0
    oi = 0
    for gei, g in enumerate([2, 3, 0, 1]):
        g0 = g * TGF
        lw = list(bh) + [ld] if gei == 0 else \
            ([t_w1last[gei - 2]] if gei >= 2 else [])
        C.rec("sync", lambda e, b_=xgb[gei % 2], g0=g0:
              e.dma_start(out=b_[:, :, :], in_=sx2n[:, :, g0:g0 + TGF]),
              wait=lw, inc=((wxf, 16),))
        xtoks[gei] = (wxf, wxf.n)
        for mi, (m0, msz) in enumerate(mt_a):
            bank = gi % 2
            wa_ = [xtoks[gei], wA_tok]
            if gi >= 2:
                wa_.append(vstt[gi - 2])

            def mm_a(t, bank=bank, m0=m0, msz=msz, xg_=xgb[gei % 2]):
                ins = None
                for k in range(8):
                    ins = t.matmul(psA[bank][0:msz, :], wA[:, k, m0:m0 + msz],
                                   xg_[:, k, :], start=(k == 0), stop=(k == 7))
                return ins
            C.rec("tensor", mm_a, wait=wa_, inc=((taf, 1),))

            def mm_g(t, bank=bank, m0=m0, msz=msz, xg_=xgb[gei % 2]):
                ins = None
                for k in range(8):
                    ins = t.matmul(psG[bank][0:msz, :], wG[:, k, m0:m0 + msz],
                                   xg_[:, k, :], start=(k == 0), stop=(k == 7))
                return ins
            wg_ = [w1_tok]
            if gi >= 2:
                wg_.append(stoks[gi - 2])
            C.rec("tensor", mm_g, wait=wg_, inc=((taf, 1),))
            tg_tok = (taf, taf.n)
            ch = m0 // 128
            sw = [tg_tok]
            if gi >= 2:
                sw.append(vstt[gi - 2])
            C.rec("scalar", lambda e, bank=bank, msz=msz, ch=ch:
                  e.activation(ebg2[bank][0:msz, :], psG[bank][0:msz, :],
                               AF.Gelu, bias=b1g[0:msz, ch:ch + 1]),
                  wait=sw, nops=1)
            stoks[gi] = (SC, SC.n)
            vw_ = [stoks[gi]]
            if mi == 0 and gei >= 1:
                vw_.append(w2done[gei - 1])

            def hhw(e, bank=bank, msz=msz, ch=ch, mi=mi):
                e.scalar_tensor_tensor(ebg2[bank][0:msz, :], psA[bank][0:msz, :],
                                       b1a[0:msz, ch:ch + 1],
                                       ebg2[bank][0:msz, :], OP.add, OP.mult)
                return e.tensor_copy(hhs[0:msz, mi, :], ebg2[bank][0:msz, :])
            C.rec("vector", hhw, wait=vw_, nops=2)
            vstt[gi] = (VC, VC.n)
            gi += 1
        t_w1last[gei] = (taf, taf.n)
        for o in range(8):
            bank2 = oi % 2
            alw = []
            if oi >= 2:
                alw.append(vepis[oi - 2])
            if oi == 0:
                alw += bfix + list(bf_)
            C.rec("sync", lambda e, b_=auxt[bank2], o=o, g0=g0:
                  e.dma_start(out=b_[:, :], in_=sxa[:, o, 256 + g0:256 + g0 + TGF]),
                  wait=alw, inc=((waf, 16),))
            atok = (waf, waf.n)
            ww_ = [w2_tok, vstt[gi - 1]]
            if oi >= 2:
                ww_.append(vepis[oi - 2])

            def mm_o(t, bank2=bank2, o=o):
                ins = None
                for m in range(22):
                    ins = t.matmul(psO[bank2][:, :],
                                   w2b[:, m, o * 128:(o + 1) * 128],
                                   hhs[:, m, :], start=(m == 0), stop=(m == 21))
                return ins
            C.rec("tensor", mm_o, wait=ww_, inc=((taf, 1),))
            to_tok = (taf, taf.n)
            C.rec("vector", lambda e, bank2=bank2, o=o:
                  e.scalar_tensor_tensor(auxt[bank2][:, :], psO[bank2][:, :],
                                         b2s[:, o:o + 1], auxt[bank2][:, :],
                                         OP.add, OP.add),
                  wait=[to_tok, atok], nops=1)
            vepis[oi] = (VC, VC.n)
            C.rec("sync", lambda e, b_=auxt[bank2], o=o, g0=g0:
                  e.dma_start(out=sxo[:, o, g0:g0 + TGF], in_=b_[:, :]),
                  wait=[vepis[oi]], inc=((stf, 16),))
            oi += 1
        w2done[gei] = (taf, taf.n)
    C.sb_pop()
    C.sb_pop()
    C.l9_half_tok = (stf, 16 * 16)
    bk = [(stf, stf.n), (wxf, wxf.n), (waf, waf.n)]
    C.rec("gpsimd", lambda e: e.nop(), wait=bk)
    return bk



def emit_launch10(C, nt):
    s_d = C.inp("s1", (D, nt), F32)
    g_d = C.inp("gff", (D, nt), F32)
    x_d = C.inp("xl1", (D, nt), F32)
    o_d = C.out("yout", (D, nt), F32)
    cn = make_consts(C)
    ss = fm3(s_d)
    sg = fm3(g_d)
    sx = fm3(x_d)
    sd = fm3(o_d)
    VC, SC = C.ch("vector"), C.ch("scalar")
    ab = [C.sb(f"f_a{i}", [128, 8, 512], F32) for i in range(2)]
    bb2 = [C.sb(f"f_b{i}", [128, 8, 512], F32) for i in range(2)]
    cb2 = [C.sb(f"f_c{i}", [128, 8, 512], F32) for i in range(2)]
    ob = [C.sb(f"f_o{i}", [128, 8, 512], F32) for i in range(2)]
    wl = [C.sem(f"f_w{i}") for i in range(2)]
    st = [C.sem(f"f_s{i}") for i in range(2)]
    tk = {}
    for ti, (t0, tn) in enumerate(mk_tiles(nt, 512)):
        par = ti % 2
        lw = [cn["ld"]] if ti == 0 else []
        if ti >= 2:
            lw.append(tk[ti - 2])
        for src, buf in ((ss, ab[par]), (sg, bb2[par]), (sx, cb2[par])):
            C.rec("sync", lambda e, src=src, buf=buf, t0=t0, tn=tn:
                  e.dma_start(out=buf[:, :, 0:tn], in_=src[:, :, t0:t0 + tn]),
                  wait=lw, inc=((wl[par], 16),))
            lw = []
        C.rec("scalar", lambda e, b_=bb2[par], tn=tn:
              e.activation(b_[:, :, 0:tn], b_[:, :, 0:tn], AF.Silu),
              wait=[(wl[par], wl[par].n), (st[par], 16 * (ti // 2))], nops=1)

        def ops(e, a=ab[par], b_=bb2[par], c=cb2[par], o=ob[par], tn=tn):
            e.tensor_tensor(o[:, :, 0:tn], a[:, :, 0:tn], b_[:, :, 0:tn], OP.mult)
            return e.tensor_tensor(o[:, :, 0:tn], o[:, :, 0:tn], c[:, :, 0:tn],
                                   OP.add)
        C.rec("vector", ops, wait=[(SC, SC.n)], nops=2)
        tk[ti] = (VC, VC.n)
        C.rec("sync", lambda e, o=ob[par], t0=t0, tn=tn:
              e.dma_start(out=sd[:, :, t0:t0 + tn], in_=o[:, :, 0:tn]),
              wait=[(VC, VC.n)], inc=((st[par], 16),))
    C.rec("gpsimd", lambda e: e.nop(), wait=[(s_, s_.n) for s_ in st])
    return [(s_, s_.n) for s_ in st]




def build_launch1(nt_out, halo=3):
    C = Ctx("l1")
    emit_launch1(C, nt_out, halo)
    C.emit()
    return C


def build_launch2(nt):
    C = Ctx("l2")
    emit_launch2(C, nt)
    C.emit()
    return C


def build_launch3(Tfull, NU=8):
    C = Ctx("l3")
    emit_launch3(C, Tfull, NU)
    C.emit()
    return C


def build_launch4(Tfull, NU2=4):
    C = Ctx("l4")
    emit_launch4(C, Tfull, NU2)
    C.emit()
    return C


def build_launch5(nt):
    C = Ctx("l5")
    emit_launch5(C, nt)
    C.emit()
    return C


def build_launch6(nt):
    C = Ctx("l6")
    emit_launch6(C, nt)
    C.emit()
    return C


def build_launch7(nt):
    C = Ctx("l7")
    emit_launch7(C, nt)
    C.emit()
    return C


def build_launch8(Tseg, nseg=2):
    C = Ctx("l8")
    emit_launch8(C, Tseg, nseg)
    C.emit()
    return C


def build_launch9(nt):
    C = Ctx("l9")
    emit_launch9(C, nt)
    C.emit()
    return C


def build_launch10(nt):
    C = Ctx("l10")
    emit_launch10(C, nt)
    C.emit()
    return C



_CACHE = {}
_DBG = {}
_STASH = False


def _get(builder, *args):
    key = (builder.__name__,) + args
    if key not in _CACHE:
        _CACHE[key] = builder(*args)
    return _CACHE[key]


def _halo_slice(full, b, s0, lo, hi):
    blk = full[:, b * T:(b + 1) * T]
    lo_i, hi_i = s0 + lo, s0 + hi
    out = np.zeros((full.shape[0], hi_i - lo_i), full.dtype)
    a = max(0, lo_i)
    bnd = min(T, hi_i)
    if bnd > a:
        out[:, a - lo_i:bnd - lo_i] = blk[:, a:bnd]
    return np.ascontiguousarray(out)


def _asm(results, key, rows):
    return np.concatenate([np.asarray(r[key], np.float32).reshape(rows, -1)
                           for r in results], axis=1)





G4 = [[0, 1, 2, 3], [4, 5, 6, 7]]
ALL8 = [[0, 1, 2, 3, 4, 5, 6, 7]]


def _fence(C, toks):
    toks = list(toks) + [(C.ch(e), C.ch(e).n) for e in Ctx.CHAINED]
    for eng in ENGINES:
        C.rec(eng, lambda e: None, wait=list(toks))


def emit_fused():
    nt, Tb = NT, T
    NTH = nt + 3
    NQK = nt + 512
    H9 = 514
    C = Ctx("mega")
    C.scoped_mode = True
    C.setup_cfg(32)
    cn = make_consts(C)
    dt_ = lambda nm, sh, ty=F32: C.nc.dram_tensor(nm, list(sh), ty,
                                                  kind="Internal")
    ein = lambda nm, sh, ty=F32: C.nc.dram_tensor(nm, list(sh), ty,
                                                  kind="ExternalInput")

    t_csq = ein("g_csq", (128, NQK)); t_ssq = ein("g_ssq", (128, NQK))
    t_csk = ein("g_csk", (128, NQK)); t_ssk = ein("g_ssk", (128, NQK))
    t_h0f = ein("g_h0flag", (128, 8))
    t_bm0 = ein("g_bmask0", (128, 256), BF16)
    t_bm1 = ein("g_bmask1", (128, 256), BF16)

    f_xn = dt_("f_xn", (D, NTH)); f_xnbf = dt_("f_xnbf", (D, NTH), BF16)
    f_zs = dt_("f_zs", (2048, NTH)); f_dt0 = dt_("f_dt0", (32, NTH))
    f_xbc = dt_("f_xbc", (2176, nt)); f_xbcbf = dt_("f_xbcbf", (2176, nt), BF16)
    f_dt0c = dt_("f_dt0c", (32, nt)); f_bcfm = dt_("f_bcfm", (128, nt), BF16)
    g_xbcg = dt_("g_xbcg", (4 * nt, 2176), BF16)
    g_dtg = dt_("g_dtg", (128, nt))
    g_bcfmg = dt_("g_bcfmg", (512, nt), BF16)
    f_y3 = dt_("f_y3", (512, Tb)); g_y3g = dt_("g_y3g", (2048, Tb))
    f_qkv = dt_("f_qkv", (1152, nt), BF16)
    g_qkvg = dt_("g_qkvg", (4 * 1152, nt), BF16)
    f_x2 = dt_("f_x2", (D, nt))
    f_y4 = dt_("f_y4", (256, Tb)); g_y4g = dt_("g_y4g", (1024, Tb))
    y4l = dt_("y4l", (D, nt))
    f_x3 = dt_("f_x3", (D, nt)); f_x3t = dt_("f_x3t", (2, 1024))
    g_x3tg = dt_("g_x3tg", (8, 1024)); f_x3h = dt_("f_x3h", (D, nt + 1))
    f_xl1 = dt_("f_xl1", (D, nt))
    f_xn1 = dt_("f_xn1", (D, nt)); f_dmy = dt_("f_dmy", (D, nt), BF16)
    f_dt1 = dt_("f_dt1", (D, nt)); f_bt = dt_("f_bt", (16, nt))
    f_ct = dt_("f_ct", (16, nt))
    g_xn1g = dt_("g_xn1g", (8 * D, nt)); g_dt1g = dt_("g_dt1g", (8 * D, nt))
    g_btg = dt_("g_btg", (8 * 16, nt)); g_ctg = dt_("g_ctg", (8 * 16, nt))
    x8u = dt_("x8u", (128, 8 * nt)); dt8 = dt_("dt8", (128, 8 * nt))
    bt16 = dt_("bt16", (16, 8 * nt)); ct16 = dt_("ct16", (16, 8 * nt))
    f_y1 = dt_("f_y1", (128, 8 * nt)); g_s1g = dt_("g_s1g", (D, 8 * nt))
    s1l = dt_("s1l", (D, nt))
    x9e = [dt_(f"x9e{i}", (D, nt + H9)) for i in range(6)]
    agsrc9 = dt_("agsrc9", (D, 2 * H9))
    g_x9tg = dt_("g_x9tg", (4 * D, 2 * H9))
    f_gff = dt_("f_gff", (D, nt))
    out_t = C.nc.dram_tensor("out", [D, nt], F32, kind="ExternalOutput")

    zs3 = bass.AP(f_zs, 3, [[NTH, 128], [128 * NTH, 16], [1, nt]])
    xn3 = bass.AP(f_xn, 3, [[NTH, 128], [128 * NTH, 8], [1, nt]])
    xnbf3 = bass.AP(f_xnbf, 3, [[NTH, 128], [128 * NTH, 8], [1, nt]])
    qkv_q = bass.AP(f_qkv, 0, [[nt, 128], [128 * nt, 8], [1, nt]])
    qkv_kv = bass.AP(f_qkv, 1024 * nt, [[nt, 128], [128 * nt, 1], [1, nt]])

    C.begin_section("l1_")
    C.io_override = dict(xn_f=f_xn.ap(), xn_bf=f_xnbf.ap(), z_silu=f_zs.ap(),
                         dt0=f_dt0.ap(), xbc=f_xbc.ap(), xbc_bf=f_xbcbf.ap())
    C.sb_push()
    r1 = emit_launch1(C, nt, 3, fused=True)
    l1bar = r1["bar"]
    cps = C.sem("cp1")
    C.rec("sync", lambda e: e.dma_start(out=f_dt0c.ap(),
                                        in_=f_dt0.ap()[:, 3:3 + nt]),
          wait=l1bar, inc=((cps, 16),))
    C.rec("sync", lambda e: e.dma_start(out=f_bcfm.ap(),
                                        in_=f_xbcbf.ap()[2048:2176, :]),
          wait=l1bar, inc=((cps, 16),))
    cc_x = C.coll("AllGather", r1["xbc_tok"], g_xbcg.ap(), G4, wait=l1bar)
    cc_d = C.coll("AllGather", f_dt0c.ap(), g_dtg.ap(), G4, wait=[(cps, 16)])
    cc_b = C.coll("AllGather", f_bcfm.ap(), g_bcfmg.ap(), G4, wait=[(cps, 32)])
    sec_toks = l1bar + [(cps, 32), cc_x, cc_d, cc_b]

    C.sb_pop()
    C.begin_section("l3_")
    C.io_override = {}
    _fence(C, sec_toks)
    C.sb_push()
    l3toks = emit_launch3(C, Tb, 8, fused=True,
                          fz=dict(xbcg=g_xbcg, bcfmg=g_bcfmg, dtfmg=g_dtg,
                                  y3=f_y3.ap(), CFG_U0=0, CFG_DT0=1,
                                  l3_entry=[], BLK=nt))
    cc_y3 = C.coll("AllGather", f_y3.ap(), g_y3g.ap(), G4, wait=l3toks)
    sec_toks = l3toks + [cc_y3]

    C.sb_pop()
    C.begin_section("l2_")
    C.io_override = dict(z_silu=zs3, xn_f=xn3, q_bf=qkv_q, kv_bf=qkv_kv,
                         x2=f_x2.ap())
    _fence(C, sec_toks)
    C.sb_push()
    l2toks = emit_launch2(C, nt, fused=True,
                          fz=dict(y3g=g_y3g, xbc=f_xbc.ap(), y3_bar=[],
                                  CFG_S0=2, TB=Tb))
    cc_qkv = C.coll("AllGather", f_qkv.ap(), g_qkvg.ap(), G4, wait=l2toks)
    sec_toks = l2toks + [cc_qkv]

    C.sb_pop()
    C.begin_section("l4_")
    C.io_override = {}
    _fence(C, sec_toks)
    C.sb_push()
    l4toks = emit_launch4(C, Tb, 4, fused=True,
                          fz=dict(qkvg=g_qkvg, y4=f_y4.ap(), CFG_Q=3,
                                  l4_entry=[], BLK=nt))
    cc_y4 = C.coll("AllGather", f_y4.ap(), g_y4g.ap(), G4, wait=l4toks)
    s_y4l = C.sem("s_y4l")
    C.dyn_dma(y4l.ap(), g_y4g, 4, [[Tb, 1024], [1, nt]], s_y4l,
              wait=[cc_y4], eng="sync")
    sec_toks = l4toks + [cc_y4, (s_y4l, 16)]

    C.sb_pop()
    C.begin_section("l5_")
    C.io_override = dict(y_mqa_f=y4l.ap(), x2=f_x2.ap(), x3=f_x3.ap())
    _fence(C, sec_toks)
    C.sb_push()
    l5toks = emit_launch5(C, nt)
    zsb = C.sb("z1col", [128, 8], F32)
    C.rec("vector", lambda e: e.memset(zsb[:, :], 0.0), nops=1)
    zt = (C.ch("vector"), C.ch("vector").n)
    cpt = C.sem("cpx3t")
    C.rec("sync", lambda e: e.dma_start(out=f_x3t.ap()[0:1, :], in_=zsb[:, :]),
          wait=[zt], inc=((cpt, 16),))
    C.rec("sync", lambda e: e.dma_start(
        out=f_x3t.ap()[1:2, :], in_=f_x3.ap()[:, nt - 1:nt],
        allow_slow_non_contiguous=True),
        wait=l5toks, inc=((cpt, 16),))
    cc_x3t = C.coll("AllGather", f_x3t.ap(), g_x3tg.ap(), G4, wait=[(cpt, 32)])
    s_x3h = C.sem("s_x3h")
    C.dyn_dma(f_x3h.ap()[:, 0:1], g_x3tg, 5, [[1, 1024]], s_x3h,
              wait=[cc_x3t], eng="sync", slow=True)
    C.rec("sync", lambda e: e.dma_start(out=f_x3h.ap()[:, 1:1 + nt],
                                        in_=f_x3.ap()),
          wait=l5toks, inc=((cpt, 16),))
    sec_toks = l5toks + [cc_x3t, (s_x3h, 16), (cpt, 48)]

    C.sb_pop()
    C.begin_section("l6_")
    C.io_override = dict(x3h=f_x3h.ap(), xn_bf_in=xnbf3, x_l1=f_xl1.ap())
    _fence(C, sec_toks)
    C.sb_push()
    l6toks = emit_launch6(C, nt)
    sec_toks = list(l6toks)

    C.sb_pop()
    C.begin_section("l7_")
    C.io_override = dict(x_l1_in=f_xl1.ap(), xn1_f=f_xn1.ap(),
                         xn1_bf=f_dmy.ap(), dt1=f_dt1.ap(), bt1=f_bt.ap(),
                         ct1=f_ct.ap())
    _fence(C, sec_toks)
    C.sb_push()
    l7toks = emit_launch7(C, nt)
    cc_xn1 = C.coll("AllGather", f_xn1.ap(), g_xn1g.ap(), ALL8, wait=l7toks)
    cc_dt1 = C.coll("AllGather", f_dt1.ap(), g_dt1g.ap(), ALL8, wait=l7toks)
    cc_bt = C.coll("AllGather", f_bt.ap(), g_btg.ap(), ALL8, wait=l7toks)
    cc_ct = C.coll("AllGather", f_ct.ap(), g_ctg.ap(), ALL8, wait=l7toks)
    s_rp8 = C.sem("s_rp8")
    s_rp8h = C.sem("s_rp8h")
    C.dyn_dma(x8u.ap(), g_xn1g, 6, [[nt, 128], [D * nt, 8], [1, nt]],
              s_rp8, wait=[cc_xn1])
    C.dyn_dma(dt8.ap(), g_dt1g, 6, [[nt, 128], [D * nt, 8], [1, nt]],
              s_rp8, wait=[cc_dt1])
    C.rec("sync", lambda e: e.dma_start(
        out=bt16.ap(), in_=bass.AP(g_btg, 0, [[nt, 16], [16 * nt, 8], [1, nt]])),
        wait=[cc_bt], inc=((s_rp8h, 16),))
    C.rec("sync", lambda e: e.dma_start(
        out=ct16.ap(), in_=bass.AP(g_ctg, 0, [[nt, 16], [16 * nt, 8], [1, nt]])),
        wait=[cc_ct], inc=((s_rp8h, 16),))
    C.rec("sync", lambda e: e.dma_start(out=x9e[0].ap()[:, H9:H9 + nt],
                                        in_=f_xn1.ap()),
          wait=l7toks, inc=((s_rp8h, 16),))
    zsb9 = C.sb("z9", [128, H9], F32)
    C.rec("vector", lambda e: e.memset(zsb9[:, :], 0.0), nops=1)
    zt9 = (C.ch("vector"), C.ch("vector").n)
    for ch in range(8):
        C.rec("sync", lambda e, ch=ch: e.dma_start(
            out=agsrc9.ap()[128 * ch:128 * (ch + 1), 0:H9], in_=zsb9[:, :]),
            wait=[zt9], inc=((s_rp8h, 16),))
    sec_toks = l7toks + [cc_xn1, cc_dt1, cc_bt, cc_ct, (s_rp8, s_rp8.n),
                         (s_rp8h, s_rp8h.n)]

    C.sb_pop()
    C.begin_section("l8_")
    C.io_override = dict(u_sl=x8u.ap(), dt_sl=dt8.ap(), bt_in=bt16.ap(),
                         ct_in=ct16.ap(), y1_sl=f_y1.ap())
    _fence(C, sec_toks)
    C.sb_push()
    l8toks = emit_launch8(C, Tb, 2)
    cc_s1 = C.coll("AllGather", f_y1.ap(), g_s1g.ap(), ALL8, wait=l8toks)
    s_s1l = C.sem("s_s1l")
    C.dyn_dma(s1l.ap(), g_s1g, 7, [[8 * nt, 1024], [1, nt]], s_s1l,
              wait=[cc_s1], eng="sync")
    sec_toks = l8toks + [cc_s1, (s_s1l, 16)]

    halo_tok = {}
    for li in range(6):
        C.sb_pop()
        C.begin_section(f"l9{li}_")
        ovr = dict(csq=t_csq.ap(), ssq=t_ssq.ap(), csk=t_csk.ap(),
                   ssk=t_ssk.ap(), h0flag=t_h0f.ap(), bmask0=t_bm0.ap(),
                   bmask1=t_bm1.ap(), x9=x9e[li].ap())
        if li < 5:
            ovr["x9o"] = bass.AP(x9e[li + 1], H9,
                                 [[nt + H9, 128], [128 * (nt + H9), 8], [1, nt]])
        else:
            ovr["x9o"] = f_gff.ap()
        C.io_override = ovr
        _fence(C, sec_toks)
        C.sb_push()
        if li == 0:
            s_t9 = C.sem("s_t9")
            C.rec("sync", lambda e: e.dma_start(
                out=agsrc9.ap()[:, H9:2 * H9], in_=x9e[0].ap()[:, nt:nt + H9]),
                wait=[], inc=((s_t9, 16),))
            cc9 = C.coll("AllGather", agsrc9.ap(), g_x9tg.ap(), G4,
                         wait=[(s_t9, 16)])
            s_p9 = C.sem("s_p9")
            C.dyn_dma(x9e[0].ap()[:, 0:H9], g_x9tg, 8,
                      [[2 * H9, 1024], [1, H9]], s_p9, wait=[cc9], eng="sync")
            _fence(C, [(s_p9, 16)])
        else:
            _fence(C, [halo_tok[li]])
        l9toks = emit_launch9(C, nt)
        if li < 5:
            s_t9b = C.sem("s_t9b")
            C.rec("gpsimd", lambda e, li=li: e.dma_start(
                out=agsrc9.ap()[:, H9:2 * H9],
                in_=x9e[li + 1].ap()[:, nt:nt + H9]),
                wait=[C.l9_half_tok], inc=((s_t9b, 16),))
            cc9b = C.coll("AllGather", agsrc9.ap(), g_x9tg.ap(), G4,
                          wait=[(s_t9b, 16)])
            s_p9b = C.sem("s_p9b")
            C.dyn_dma(x9e[li + 1].ap()[:, 0:H9], g_x9tg, 8,
                      [[2 * H9, 1024], [1, H9]], s_p9b, wait=[cc9b])
            halo_tok[li + 1] = (s_p9b, 16)
        sec_toks = list(l9toks)

    C.sb_pop()
    C.begin_section("l10_")
    C.io_override = dict(s1=s1l.ap(), gff=f_gff.ap(), xl1=f_xl1.ap(),
                         yout=out_t.ap())
    _fence(C, sec_toks + [cc_s1, (s_s1l, 16)])
    C.sb_push()
    emit_launch10(C, nt)
    C.sb_pop()
    C.io_override = {}
    C.emit()
    return C


def _kernel_fused(inp):
    f32 = np.float32
    g = lambda k: np.asarray(inp[k])
    x = g("L0_attn_x")
    x_fm = np.ascontiguousarray(x.reshape(B * T, D).T)
    rs = lambda v, k: np.ascontiguousarray(np.asarray(v, f32).reshape(k, 128).T)
    nt = NT
    H9 = 514
    NQK = nt + 512
    C = _get(emit_fused)

    half = np.arange(0, 64, 2, dtype=f32)
    inv_freq = 1.0 / 10000.0 ** (half / 64)
    base_scale = (half + 0.4 * 64) / (1.4 * 64)
    mt_a9 = [(i * 128, 128) for i in range(21)] + [(2688, 42)]
    b1as, b1gs = [], []
    for li in range(6):
        b1v = np.asarray(g("L1_ffn_b1")[li], f32)
        b1a = np.zeros((128, 22), f32)
        b1g = np.zeros((128, 22), f32)
        for i, (m0, ms) in enumerate(mt_a9):
            b1a[0:ms, i] = b1v[m0:m0 + ms]
            b1g[0:ms, i] = b1v[2730 + m0:2730 + m0 + ms]
        b1as.append(b1a)
        b1gs.append(b1g)

    provs = []
    for c in range(NCORES):
        b, j = c // 4, c % 4
        s0 = j * nt
        sl = slice(c * nt, (c + 1) * nt)
        h0_l3 = (8 * c) % 32
        pos = np.arange(s0 - 512, s0 + nt).astype(f32)
        fr = pos[:, None] * inv_freq[None, :]
        fr = np.concatenate([fr, fr], -1)
        power = (pos - T // 2) / 128.0
        sc = base_scale[None, :] ** power[:, None]
        sc = np.concatenate([sc, sc], -1)
        cfg = np.zeros((1, 32), np.int32)
        cfg[0, 0] = h0_l3 * 64
        cfg[0, 1] = h0_l3 * nt
        cfg[0, 2] = j * nt
        cfg[0, 3] = (4 * j) * 64 * nt
        cfg[0, 4] = j * nt
        cfg[0, 5] = 0 if j == 0 else ((j - 1) * 2 + 1) * 1024
        cfg[0, 6] = 128 * c * nt
        cfg[0, 7] = c * nt
        cfg[0, 8] = 0 if j == 0 else (j - 1) * 1024 * 2 * H9 + H9
        drep = np.repeat(np.asarray(g("L0_attn_D"), f32), 64).reshape(16, 128).T
        m = {
            "cfg": cfg,
            "l1_x": _halo_slice(x_fm, b, s0, -3, nt),
            "l1_w_inproj": g("L0_attn_in_proj_w"),
            "l1_dt_bias_r": g("L0_attn_dt_bias").reshape(32, 1),
            "l1_conv_w_r": g("L0_attn_conv_w").reshape(17, 128, 4),
            "l1_conv_b_r": np.ascontiguousarray(
                g("L0_attn_conv_b").reshape(17, 128).T),
            "l3_alog_r": np.ascontiguousarray(np.tile(
                g("L0_attn_A_log")[[(h0_l3 + i) % 32 for i in range(8)]],
                (128, 1)).astype(f32)),
            "l3_d_r": np.ascontiguousarray(np.tile(
                g("L0_attn_D")[[(h0_l3 + i) % 32 for i in range(8)]],
                (128, 1)).astype(f32)),
            "l2_mnorm_r": rs(g("L0_attn_mnorm_w"), 16),
            "l2_w_outproj": g("L0_attn_out_proj_w"),
            "l2_w_attn": g("L0_attn_attn_w"),
            "l2_w_attn_kv": np.ascontiguousarray(g("L0_attn_attn_w")[:, 1024:]),
            "l2_d_rep": np.ascontiguousarray(drep),
            "l5_w_proj": g("L0_attn_proj_w"),
            "l6_x_in": np.ascontiguousarray(x_fm[:, sl]),
            "l6_maa_k_r": rs(g("L0_attn_time_maa_k"), 8),
            "l6_maa_r_r": rs(g("L0_attn_time_maa_r"), 8),
            "l6_w_key": g("L0_attn_key_w"),
            "l6_w_recept": g("L0_attn_recept_w"),
            "l6_w_value": g("L0_attn_value_w"),
            "l6_w1T": np.ascontiguousarray(g("L0_ffn_w1_w").T),
            "l6_w2T": np.ascontiguousarray(g("L0_ffn_w2_w").T),
            "l6_h1T": np.ascontiguousarray(g("L0_ffn_h1_w").T),
            "l6_h2T": np.ascontiguousarray(g("L0_ffn_h2_w").T),
            "l6_w3T": np.ascontiguousarray(g("L0_ffn_w3_w").T),
            "l6_w1s_r": rs(g("L0_ffn_w1_s"), 32),
            "l6_w2s_r": rs(g("L0_ffn_w2_s"), 32),
            "l6_h1s_r": rs(g("L0_ffn_h1_s"), 32),
            "l6_h2s_r": rs(g("L0_ffn_h2_s"), 32),
            "l6_w3s_r": rs(g("L0_ffn_w3_s"), 8),
            "l7_w_xdtT": np.ascontiguousarray(g("L1_attn_W_xdt").T),
            "l7_b_xdt_r": g("L1_attn_b_xdt").reshape(64, 1),
            "l7_w_dtT": np.ascontiguousarray(g("L1_attn_W_dt").T),
            "l7_b_dt_r": rs(g("L1_attn_b_dt"), 8),
            "l7_w_bT": np.ascontiguousarray(g("L1_attn_W_B").T),
            "l7_b_b_r": g("L1_attn_b_B").reshape(16, 1),
            "l7_w_cT": np.ascontiguousarray(g("L1_attn_W_C").T),
            "l7_b_c_r": g("L1_attn_b_C").reshape(16, 1),
            "l8_a_sl": np.ascontiguousarray(
                g("L1_attn_A_log")[128 * c:128 * (c + 1), :]),
            "l8_d_sl": np.ascontiguousarray(
                g("L1_attn_D")[128 * c:128 * (c + 1)].reshape(128, 1)),
            "g_csq": np.ascontiguousarray(np.tile((np.cos(fr) * sc).T, (2, 1)).astype(f32)),
            "g_ssq": np.ascontiguousarray(np.tile((np.sin(fr) * sc).T, (2, 1)).astype(f32)),
            "g_csk": np.ascontiguousarray(np.tile((np.cos(fr) / sc).T, (2, 1)).astype(f32)),
            "g_ssk": np.ascontiguousarray(np.tile((np.sin(fr) / sc).T, (2, 1)).astype(f32)),
            "g_h0flag": np.full((128, 8), 0.0 if j == 0 else 1.0, f32),
            "g_bmask0": np.ascontiguousarray(
                (np.zeros((128, 256)) if j == 0 else
                 (np.arange(256)[None, :] <= np.arange(128)[:, None])
                 ).astype(ml_dtypes.bfloat16)),
            "g_bmask1": np.ascontiguousarray(
                (np.zeros((128, 256)) if j == 0 else
                 ((np.arange(256)[None, :] >= np.arange(128)[:, None] - 128)
                  & (np.arange(256)[None, :] <= np.arange(128)[:, None] + 128))
                 ).astype(ml_dtypes.bfloat16)),
        }
        for li in range(6):
            p = f"l9{li}_"
            m[p + "wqkv"] = np.ascontiguousarray(g("L1_ffn_Wqkv")[li])
            m[p + "wo"] = np.ascontiguousarray(g("L1_ffn_Wo")[li])
            m[p + "qs_r"] = np.ascontiguousarray(
                np.tile(np.tile(np.asarray(g("L1_ffn_q_scale")[li], f32), 2)
                        .reshape(128, 1), (1, 8)))
            m[p + "ks_r"] = np.ascontiguousarray(
                np.tile(np.tile(np.asarray(g("L1_ffn_k_scale")[li], f32), 2)
                        .reshape(128, 1), (1, 8)))
            m[p + "ln_g_r"] = rs(g("L1_ffn_ln_g")[li], 8)
            m[p + "ln_b_r"] = rs(g("L1_ffn_ln_b")[li], 8)
            m[p + "w1l"] = np.ascontiguousarray(g("L1_ffn_W1")[li])
            m[p + "b1a_r"] = b1as[li]
            m[p + "b1g_r"] = b1gs[li]
            m[p + "w2lT"] = np.ascontiguousarray(
                np.pad(np.asarray(g("L1_ffn_W2")[li], f32), ((0, 86), (0, 0))))
            m[p + "b2_r"] = rs(g("L1_ffn_b2")[li], 8)
        provs.append(m)
    res = C.run(extra_by_core=provs)
    out_fm = np.concatenate([np.asarray(r["out"], f32).reshape(D, nt)
                             for r in res], axis=1)
    return np.ascontiguousarray(out_fm.T.reshape(B, T, D)).astype(np.float32)


_FUSED = True


def kernel(**inp):
    if _FUSED:
        return _kernel_fused(inp)
    nt = NT
    f32 = np.float32
    g = lambda k: np.asarray(inp[k])
    x = g("L0_attn_x")
    x_fm = np.ascontiguousarray(x.reshape(B * T, D).T)

    rs = lambda v, k: np.ascontiguousarray(np.asarray(v, f32).reshape(k, 128).T)

    C1 = _get(build_launch1, nt, 3)
    provs = []
    for c in range(NCORES):
        b, s0 = c // 4, (c % 4) * nt
        provs.append({
            "x": _halo_slice(x_fm, b, s0, -3, nt),
            "w_inproj": g("L0_attn_in_proj_w"),
            "dt_bias_r": g("L0_attn_dt_bias").reshape(32, 1),
            "conv_w_r": g("L0_attn_conv_w").reshape(17, 128, 4),
            "conv_b_r": np.ascontiguousarray(g("L0_attn_conv_b").reshape(17, 128).T),
        })
    r1 = C1.run(extra_by_core=provs)
    xn_full = _asm(r1, "xn_f", D)[:, _drop3()]
    xnbf_full = np.concatenate(
        [np.asarray(r["xn_bf"]).reshape(D, -1)[:, 3:] for r in r1], axis=1)
    zs_full = _asm(r1, "z_silu", 2048)[:, _drop3()]
    dt_full = _asm(r1, "dt0", 32)[:, _drop3()]
    xbc_full = _asm(r1, "xbc", 2176)
    xbcbf_full = np.concatenate(
        [np.asarray(r["xbc_bf"]).reshape(2176, -1) for r in r1], axis=1)

    NCH = T // 128
    C3 = _get(build_launch3, T, 8)
    provs = []
    for c in range(NCORES):
        b = c // 4
        hs = [(8 * c + i) % 32 for i in range(8)]
        bsl = slice(b * T, (b + 1) * T)
        xs_bf = np.stack([xbcbf_full[64 * h:64 * h + 64, bsl].T
                          .reshape(NCH, 128, 64).transpose(1, 0, 2)
                          for h in hs], axis=1)
        xs_f = np.stack([xbc_full[64 * h:64 * h + 64, bsl].T
                         .reshape(NCH, 128, 64).transpose(1, 0, 2)
                         for h in hs], axis=1).astype(f32)
        btok = (xbcbf_full[2048:2112, bsl].T.reshape(NCH, 128, 64)
                .transpose(1, 0, 2))
        provs.append({
            "xs_bf": np.ascontiguousarray(xs_bf),
            "xs_f": np.ascontiguousarray(xs_f),
            "b_tok": np.ascontiguousarray(btok),
            "b_fm": np.ascontiguousarray(xbcbf_full[2048:2112, bsl]),
            "c_fm": np.ascontiguousarray(xbcbf_full[2112:2176, bsl]),
            "dt_tok": np.ascontiguousarray(
                np.stack([dt_full[h, bsl].reshape(NCH, 128).T for h in hs],
                         axis=1)),
            "alog_r": np.ascontiguousarray(
                np.tile(g("L0_attn_A_log")[hs], (128, 1)).astype(f32)),
            "d_r": np.ascontiguousarray(
                np.tile(g("L0_attn_D")[hs], (128, 1)).astype(f32)),
        })
    r3 = C3.run(extra_by_core=provs)
    y2_full = np.zeros((2048, B * T), f32)
    for c in range(NCORES):
        b = c // 4
        yt = np.asarray(r3[c]["y_tok"], f32).reshape(128, 8, NCH, 64)
        for i in range(8):
            h = (8 * c + i) % 32
            y2_full[64 * h:64 * h + 64, b * T:(b + 1) * T] = \
                yt[:, i, :, :].transpose(1, 0, 2).reshape(T, 64).T
    if _STASH: _DBG['y2'] = y2_full.copy()
    del r3

    C2 = _get(build_launch2, nt)
    provs = []
    for c in range(NCORES):
        sl = slice(c * nt, (c + 1) * nt)
        provs.append({
            "y": np.ascontiguousarray(y2_full[:, sl]),
            "z_silu": np.ascontiguousarray(zs_full[:, sl]),
            "xn_f": np.ascontiguousarray(xn_full[:, sl]),
            "mnorm_r": rs(g("L0_attn_mnorm_w"), 16),
            "w_outproj": g("L0_attn_out_proj_w"),
            "w_attn": g("L0_attn_attn_w"),
            "w_attn_kv": np.ascontiguousarray(g("L0_attn_attn_w")[:, 1024:]),
        })
    r2 = C2.run(extra_by_core=provs)
    x2_full = _asm(r2, "x2", D)
    qbf_full = np.concatenate([np.asarray(r["q_bf"]).reshape(D, -1) for r in r2],
                              axis=1)
    kvbf_full = np.concatenate([np.asarray(r["kv_bf"]).reshape(128, -1)
                                for r in r2], axis=1)
    if _STASH: _DBG.update(x2=x2_full, q=qbf_full, kv=kvbf_full)
    del r2, y2_full

    NST = T // 128
    C4 = _get(build_launch4, T, 4)
    provs = []
    for c in range(NCORES):
        b = c // 4
        bsl = slice(b * T, (b + 1) * T)
        hs = [4 * (c % 4) + i for i in range(4)]
        vtok = kvbf_full[64:128, bsl].T
        vo = np.concatenate([vtok, np.ones((T, 1), vtok.dtype)], 1)
        provs.append({
            "q_in": np.ascontiguousarray(
                np.stack([qbf_full[64 * h:64 * h + 64, bsl] for h in hs], axis=1)),
            "k_in": np.ascontiguousarray(kvbf_full[0:64, bsl]),
            "vo_tok": np.ascontiguousarray(
                vo.reshape(NST, 128, 65).transpose(1, 0, 2)),
        })
    r4 = C4.run(extra_by_core=provs)
    ymqa_full = np.zeros((D, B * T), f32)
    for c in range(NCORES):
        b = c // 4
        ym = np.asarray(r4[c]["y_mqa"], f32).reshape(64, 4, T)
        for i in range(4):
            h = 4 * (c % 4) + i
            ymqa_full[64 * h:64 * h + 64, b * T:(b + 1) * T] = ym[:, i, :]
    if _STASH: _DBG['ymqa'] = ymqa_full.copy()
    del r4

    C5 = _get(build_launch5, nt)
    provs = []
    for c in range(NCORES):
        sl = slice(c * nt, (c + 1) * nt)
        provs.append({"y_mqa_f": np.ascontiguousarray(ymqa_full[:, sl]),
                      "x2": np.ascontiguousarray(x2_full[:, sl]),
                      "w_proj": g("L0_attn_proj_w")})
    r5 = C5.run(extra_by_core=provs)
    x3_full = _asm(r5, "x3", D)
    if _STASH: _DBG['x3'] = x3_full.copy()
    del r5, ymqa_full

    C6 = _get(build_launch6, nt)
    provs = []
    for c in range(NCORES):
        b, s0 = c // 4, (c % 4) * nt
        sl = slice(c * nt, (c + 1) * nt)
        provs.append({
            "x3h": _halo_slice(x3_full, b, s0, -1, nt),
            "x_in": np.ascontiguousarray(x_fm[:, sl]),
            "xn_bf_in": np.ascontiguousarray(xnbf_full[:, sl]),
            "maa_k_r": rs(g("L0_attn_time_maa_k"), 8),
            "maa_r_r": rs(g("L0_attn_time_maa_r"), 8),
            "w_key": g("L0_attn_key_w"),
            "w_recept": g("L0_attn_recept_w"),
            "w_value": g("L0_attn_value_w"),
            "w1T": np.ascontiguousarray(g("L0_ffn_w1_w").T),
            "w2T": np.ascontiguousarray(g("L0_ffn_w2_w").T),
            "h1T": np.ascontiguousarray(g("L0_ffn_h1_w").T),
            "h2T": np.ascontiguousarray(g("L0_ffn_h2_w").T),
            "w3T": np.ascontiguousarray(g("L0_ffn_w3_w").T),
            "w1s_r": rs(g("L0_ffn_w1_s"), 32),
            "w2s_r": rs(g("L0_ffn_w2_s"), 32),
            "h1s_r": rs(g("L0_ffn_h1_s"), 32),
            "h2s_r": rs(g("L0_ffn_h2_s"), 32),
            "w3s_r": rs(g("L0_ffn_w3_s"), 8),
        })
    r6 = C6.run(extra_by_core=provs)
    xl1_full = _asm(r6, "x_l1", D)
    if _STASH: _DBG['xl1'] = xl1_full.copy()
    del r6, x3_full

    C7 = _get(build_launch7, nt)
    provs = []
    for c in range(NCORES):
        sl = slice(c * nt, (c + 1) * nt)
        provs.append({
            "x_l1_in": np.ascontiguousarray(xl1_full[:, sl]),
            "w_xdtT": np.ascontiguousarray(g("L1_attn_W_xdt").T),
            "b_xdt_r": g("L1_attn_b_xdt").reshape(64, 1),
            "w_dtT": np.ascontiguousarray(g("L1_attn_W_dt").T),
            "b_dt_r": rs(g("L1_attn_b_dt"), 8),
            "w_bT": np.ascontiguousarray(g("L1_attn_W_B").T),
            "b_b_r": g("L1_attn_b_B").reshape(16, 1),
            "w_cT": np.ascontiguousarray(g("L1_attn_W_C").T),
            "b_c_r": g("L1_attn_b_C").reshape(16, 1),
        })
    r7 = C7.run(extra_by_core=provs)
    xn1_full = _asm(r7, "xn1_f", D)
    dt1_full = _asm(r7, "dt1", D)
    bt_full = _asm(r7, "bt1", 16)
    ct_full = _asm(r7, "ct1", 16)
    if _STASH: _DBG.update(xn1=xn1_full, dt1=dt1_full, bt=bt_full, ct=ct_full)
    del r7

    C8 = _get(build_launch8, T, 2)
    provs = []
    for c in range(NCORES):
        rsl = slice(128 * c, 128 * (c + 1))
        provs.append({
            "u_sl": np.ascontiguousarray(xn1_full[rsl, :]),
            "dt_sl": np.ascontiguousarray(dt1_full[rsl, :]),
            "a_sl": np.ascontiguousarray(g("L1_attn_A_log")[rsl, :]),
            "d_sl": np.ascontiguousarray(g("L1_attn_D")[rsl].reshape(128, 1)),
            "bt_in": bt_full, "ct_in": ct_full,
        })
    r8 = C8.run(extra_by_core=provs)
    s1_full = np.concatenate([np.asarray(r8[c]["y1_sl"], f32).reshape(128, -1)
                              for c in range(NCORES)], axis=0)
    if _STASH: _DBG['s1'] = s1_full.copy()
    del r8, dt1_full

    H9 = 514
    C9 = _get(build_launch9, nt)
    half = np.arange(0, 64, 2, dtype=f32)
    inv_freq = 1.0 / 10000.0 ** (half / 64)
    base_scale = (half + 0.4 * 64) / (1.4 * 64)
    tabs = []
    for c in range(NCORES):
        s0 = (c % 4) * nt
        pos = np.arange(s0 - 512, s0 + nt).astype(f32)
        fr = pos[:, None] * inv_freq[None, :]
        fr = np.concatenate([fr, fr], -1)
        power = (pos - T // 2) / 128.0
        sc = base_scale[None, :] ** power[:, None]
        sc = np.concatenate([sc, sc], -1)
        tabs.append((np.tile((np.cos(fr) * sc).T, (2, 1)).astype(f32),
                     np.tile((np.sin(fr) * sc).T, (2, 1)).astype(f32),
                     np.tile((np.cos(fr) / sc).T, (2, 1)).astype(f32),
                     np.tile((np.sin(fr) / sc).T, (2, 1)).astype(f32)))
    mt_a9 = [(i * 128, 128) for i in range(21)] + [(2688, 42)]
    x_tr = xn1_full
    for li in range(6):
        b1v = np.asarray(g("L1_ffn_b1")[li], f32)
        b1a = np.zeros((128, 22), f32)
        b1g = np.zeros((128, 22), f32)
        for i, (m0, ms) in enumerate(mt_a9):
            b1a[0:ms, i] = b1v[m0:m0 + ms]
            b1g[0:ms, i] = b1v[2730 + m0:2730 + m0 + ms]
        provs = []
        for c in range(NCORES):
            b, s0 = c // 4, (c % 4) * nt
            csq, ssq, csk, ssk = tabs[c]
            provs.append({
                "x9": _halo_slice(x_tr, b, s0, -H9, nt),
                "wqkv": np.ascontiguousarray(g("L1_ffn_Wqkv")[li]),
                "wo": np.ascontiguousarray(g("L1_ffn_Wo")[li]),
                "qs_r": np.ascontiguousarray(
                    np.tile(np.tile(np.asarray(g("L1_ffn_q_scale")[li], f32), 2)
                            .reshape(128, 1), (1, 8))),
                "ks_r": np.ascontiguousarray(
                    np.tile(np.tile(np.asarray(g("L1_ffn_k_scale")[li], f32), 2)
                            .reshape(128, 1), (1, 8))),
                "csq": csq, "ssq": ssq, "csk": csk, "ssk": ssk,
                "h0flag": np.full((128, 8), 0.0 if c % 4 == 0 else 1.0, f32),
                "bmask0": (np.zeros((128, 256)) if c % 4 == 0 else
                           (np.arange(256)[None, :] <= np.arange(128)[:, None])
                           ).astype(ml_dtypes.bfloat16),
                "bmask1": (np.zeros((128, 256)) if c % 4 == 0 else
                           ((np.arange(256)[None, :] >= np.arange(128)[:, None] - 128)
                            & (np.arange(256)[None, :] <= np.arange(128)[:, None] + 128))
                           ).astype(ml_dtypes.bfloat16),
                "ln_g_r": rs(g("L1_ffn_ln_g")[li], 8),
                "ln_b_r": rs(g("L1_ffn_ln_b")[li], 8),
                "w1l": np.ascontiguousarray(g("L1_ffn_W1")[li]),
                "b1a_r": b1a, "b1g_r": b1g,
                "w2lT": np.ascontiguousarray(
                    np.pad(np.asarray(g("L1_ffn_W2")[li], f32), ((0, 86), (0, 0)))),
                "b2_r": rs(g("L1_ffn_b2")[li], 8),
            })
        r9 = C9.run(extra_by_core=provs)
        x_tr = _asm(r9, "x9o", D)
        if _STASH: _DBG[f'tr{li}'] = x_tr.copy()
        del r9

    C10 = _get(build_launch10, nt)
    provs = []
    for c in range(NCORES):
        sl = slice(c * nt, (c + 1) * nt)
        provs.append({"s1": np.ascontiguousarray(s1_full[:, sl]),
                      "gff": np.ascontiguousarray(x_tr[:, sl]),
                      "xl1": np.ascontiguousarray(xl1_full[:, sl])})
    r10 = C10.run(extra_by_core=provs)
    out_fm = _asm(r10, "yout", D)
    if _STASH: _DBG.update(xn=xn_full, xbc=xbc_full, xbcbf=xbcbf_full, zs=zs_full, dt0=dt_full, out=out_fm)
    return np.ascontiguousarray(out_fm.T.reshape(B, T, D)).astype(np.float32)


def _drop3():
    w = NT + 3
    keep = []
    for c in range(NCORES):
        keep.extend(range(c * w + 3, (c + 1) * w))
    return np.array(keep)
